# version 20
# baseline (speedup 1.0000x reference)
import numpy as np
import ml_dtypes
import concourse.bass as bass
import concourse.mybir as mybir
from concourse.bass_utils import run_bass_kernel_spmd

F32 = mybir.dt.float32
BF16 = mybir.dt.bfloat16
AL = mybir.AluOpType
AF = mybir.ActivationFunctionType
AX = mybir.AxisListType
AP = bass.AP

B, S, D, H, HD, I = 4, 4096, 1024, 16, 64, 3072
T = 2048
NT = T // 128
DC = D // 128
ITI = I // 128
QBN = T // 512
KCN = S // 128
NPAIR = H // 2
ISCALE = 1.0 / 32.0
MAGIC = 12582912.0
EPS = 1e-5
QA_ = 32767.0
KC_GROUPS = [(g * 3, min(3, KCN - g * 3)) for g in range((KCN + 2) // 3)]

DEBUG = []
NOEV = ("V", 0)
ENGS = ("tensor", "vector", "scalar", "sync", "gpsimd")


class P:

    def __init__(self, nc, sems, sem_factory):
        self.nc = nc
        self.sems = sems
        self.sem_factory = sem_factory
        self.ops = {e: [] for e in ENGS}
        self.meta = {e: [] for e in ENGS}
        self.cnt = {k: 0 for k in sems}
        self.hwm = {e: {k: 0 for k in sems} for e in ENGS}

    def stream(self, name):
        if name not in self.sems:
            self.sems[name] = self.sem_factory(name)
            self.cnt[name] = 0
            for e in ENGS:
                self.hwm[e][name] = 0
        return name

    def emit(self, e, fn, waits=(), inc=None, k=1):
        waits = [w for w in waits if w is not None and w[1] > 0]
        for (s, v) in waits:
            if self.hwm[e][s] < v:
                self.hwm[e][s] = v
                h = self.sems[s]
                self.ops[e].append(lambda eng, h=h, v=v: eng.wait_ge(h, v))
                self.meta[e].append((((s, v),), None, 0))
        ev = None
        if inc is not None:
            self.cnt[inc] += k
            ev = (inc, self.cnt[inc])
            h = self.sems[inc]
            self.ops[e].append(lambda eng, fn=fn, h=h, k=k: fn(eng).then_inc(h, k))
            self.meta[e].append(((), inc, k))
        else:
            self.ops[e].append(lambda eng, fn=fn: fn(eng))
            self.meta[e].append(((), None, 0))
        return ev

    def dma(self, q, out, in_, waits=(), transpose=False, slow=False,
            stream=None):
        assert stream is not None, "dma needs a stream name"
        sem = self.stream(stream)
        eng = "gpsimd" if q == "barrier" else q
        if transpose:
            fn = lambda eng_, out=out, in_=in_: eng_.dma_start_transpose(out, in_)
        elif slow:
            fn = lambda eng_, out=out, in_=in_: eng_.dma_start(
                out=out, in_=in_, allow_slow_non_contiguous=True)
        else:
            fn = lambda eng_, out=out, in_=in_: eng_.dma_start(out=out, in_=in_)
        return self.emit(eng, fn, waits, inc=sem, k=16)

    def coll(self, kind, op, groups, in_ap, out_ap, waits=(), stream="C"):
        fn = lambda eng: eng.collective_compute(
            kind, op, replica_groups=groups, ins=[in_ap], outs=[out_ap])
        return self.emit("gpsimd", fn, waits, inc=self.stream(stream))

    def now(self, sem):
        return (sem, self.cnt[sem])

    def check(self):
        pos = {e: 0 for e in ENGS}
        cnt = {k: 0 for k in self.cnt}
        progress = True
        while progress:
            progress = False
            for e in ENGS:
                while pos[e] < len(self.meta[e]):
                    waits, inc, k = self.meta[e][pos[e]]
                    if any(cnt[s] < v for (s, v) in waits):
                        break
                    if inc is not None:
                        cnt[inc] += k
                    pos[e] += 1
                    progress = True
        stuck = {e: (len(self.meta[e]) - pos[e]) for e in ENGS
                 if pos[e] < len(self.meta[e])}
        if stuck:
            det = {e: self.meta[e][pos[e]][0] for e in stuck}
            raise RuntimeError(
                f"deadlock in plan: remaining={stuck} blocked_on={det} cnt={cnt}")


def build_program(debug=()):
    nc = bass.Bass("TRN2", target_bir_lowering=False)

    COLL_OUTS = {"kag_o", "vag_o", "wu_bf", "wg_bf", "wd_bf"}

    def dram(name, shape, dt, kind="Internal"):
        k = ("ExternalOutput" if (name in debug and kind == "Internal"
                                  and name not in COLL_OUTS) else kind)
        return nc.dram_tensor(name, shape, dt, kind=k)

    x_d = nc.dram_tensor("x", [T, D], F32, kind="ExternalInput")
    wqT_d = nc.dram_tensor("wqT", [D, D], F32, kind="ExternalInput")
    wkT_d = nc.dram_tensor("wkT", [D, D], F32, kind="ExternalInput")
    wvT_d = nc.dram_tensor("wvT", [D, D], F32, kind="ExternalInput")
    woT_d = nc.dram_tensor("woT", [D, D], F32, kind="ExternalInput")
    bq_d = nc.dram_tensor("bq", [1, D], F32, kind="ExternalInput")
    bk_d = nc.dram_tensor("bk", [1, D], F32, kind="ExternalInput")
    bv_d = nc.dram_tensor("bv", [1, D], F32, kind="ExternalInput")
    wu_sh_d = nc.dram_tensor("wu_sh", [I // 8, D], F32, kind="ExternalInput")
    wg_sh_d = nc.dram_tensor("wg_sh", [I // 8, D], F32, kind="ExternalInput")
    wd_sh_d = nc.dram_tensor("wd_sh", [D // 8, I], F32, kind="ExternalInput")
    ident_d = nc.dram_tensor("ident", [128, 128], F32, kind="ExternalInput")
    cos_d = nc.dram_tensor("cos_t", [128, NT * 32], F32, kind="ExternalInput")
    sin_d = nc.dram_tensor("sin_t", [128, NT * 32], F32, kind="ExternalInput")
    out_d = nc.dram_tensor("out", [T, D], F32, kind="ExternalOutput")

    xn32_d = dram("xn32", [T, D], F32)
    xnbf_d = dram("xnbf", [T, D], BF16)
    xqbf_d = dram("xqbf", [T, D], BF16)
    q_tm_d = dram("q_tm", [T, D], BF16)
    kag_i = dram("kag_i", [T, D], BF16)
    kag_o = dram("kag_o", [S, D], BF16)
    vag_i = dram("vag_i", [T, H * 65], BF16)
    vag_o = dram("vag_o", [S, H * 65], BF16)
    wu_ag_i = dram("wu_ag_i", [D, I // 8], BF16)
    wu_bf = dram("wu_bf", [8 * D, I // 8], BF16)
    wg_ag_i = dram("wg_ag_i", [D, I // 8], BF16)
    wg_bf = dram("wg_bf", [8 * D, I // 8], BF16)
    wd_ag_i = dram("wd_ag_i", [I, D // 8], BF16)
    wd_bf = dram("wd_bf", [8 * I, D // 8], BF16)
    up_sp = dram("up_sp", [I, T], F32)
    gate_sp = dram("gate_sp", [I, T], F32)
    oq_d = dram("oq", [I, T], BF16)
    attn_d = dram("attn", [D, T], BF16)
    crs = [dram(f"crd{i}", [128, nv], F32)
           for i, nv in enumerate([1, 2, 1, 1])]
    aris = [dram(f"ar{i}i", [1, 2], F32) for i in range(4)]
    aros = [dram(f"ar{i}o", [1, 2], F32) for i in range(4)]

    from contextlib import ExitStack
    stack = ExitStack()

    def sbt(name, shape, dt):
        return stack.enter_context(nc.sbuf_tensor(name, shape, dt))

    XNFM_O = 0
    WQ_O = XNFM_O + 8 * 2048
    WK_O = WQ_O + 8 * 1024
    WV_O = WK_O + 8 * 1024
    WU_O = 0
    WG_O = WU_O + 8 * 3072
    XQ_O = WG_O + 8 * 3072
    P3_END = XQ_O + 8 * 2048
    KP_O = 0
    VP_O = KP_O + 2 * 4096
    QP_O = VP_O + 2 * 4160
    P4_END_E = QP_O + 2 * 2048
    WO_O = P4_END_E
    WD_O = WO_O + 8 * 1024
    P6_END = WD_O + 24 * 1024

    xst = sbt("xst", [128, 2 * 1024], F32)
    sc1 = sbt("sc1", [128, 1024], F32)
    xn32s = sbt("xn32s", [128, 2 * 1024], F32)
    xnbs = sbt("xnbs", [128, 2 * 1024], BF16)
    stt = sbt("stt", [128, 8], F32)
    lnst = sbt("lnst", [128, 8], F32)
    lnsum = sbt("lnsum", [128, 16], F32)
    lnsq = sbt("lnsq", [128, 16], F32)
    lnmean = sbt("lnmean", [128, 16], F32)
    lnvar = sbt("lnvar", [128, 16], F32)
    lnsqr = sbt("lnsqr", [128, 16], F32)
    lnrst = sbt("lnrst", [128, 16], F32)
    lntmp = sbt("lntmp", [128, 16], F32)
    cosb = sbt("cosb", [128, NT * 32], F32)
    sinb = sbt("sinb", [128, NT * 32], F32)
    bqs = sbt("bqs", [1, D], BF16)
    bks = sbt("bks", [1, D], BF16)
    bvs = sbt("bvs", [1, D], BF16)
    ones1 = sbt("ones1", [128, 128], BF16)
    ones1f = sbt("ones1f", [128, 64], F32)
    identb = sbt("identb", [128, 128], F32)
    rt1 = sbt("rt1", [128, 512], F32)
    rt2 = sbt("rt2", [128, 512], F32)
    dqs = sbt("dqs", [128, 3 * 96], F32)
    spst = sbt("spst", [128, 4 * 512], F32)
    mux = sbt("mux", [128, 8], F32)
    brx = sbt("brx", [128, 8], F32)
    rowb = sbt("rowb", [1, 256], F32)
    m0b = sbt("m0b", [1, 2], F32)
    ptb = sbt("ptb", [128, 4 * 1536], BF16)
    aub = sbt("aub", [128, 4 * 512], F32)
    lnr = sbt("lnr", [128, 512], F32)
    anb = sbt("anb", [128, 2 * 512], BF16)
    chld = sbt("chld", [128, 4 * 512], BF16)

    ARW = 64 * 1024
    arena = sbt("arena", [128, ARW], BF16)
    abase = nc.lookup_mloc(arena).addr
    ffw = nc.alloc_sbuf_tensor_at("ffw", [128, 10 * 2048], F32,
                                  offset=abase + 2 * P4_END_E)
    oqbf = nc.alloc_sbuf_tensor_at("oqbf", [128, 2 * 2048], BF16,
                                   offset=abase + 2 * P4_END_E + 8 * 2048 * 4)
    dqw = nc.alloc_sbuf_tensor_at("dqw", [128, 2 * 3072], F32,
                                  offset=abase + 81920)
    dqo = nc.alloc_sbuf_tensor_at("dqo", [128, 3072], BF16,
                                  offset=abase + 81920 + 24576)
    dqt = nc.alloc_sbuf_tensor_at("dqt", [128, 24 * 128], BF16, offset=abase)
    dqo = nc.alloc_sbuf_tensor_at("dqo", [128, 3072], F32,
                                  offset=abase + 6144)
    fin_x = nc.alloc_sbuf_tensor_at("fin_x", [128, 4 * 1024], F32,
                                    offset=abase)
    fin_o = nc.alloc_sbuf_tensor_at("fin_o", [128, 4 * 1024], F32,
                                    offset=abase + 16384)
    qko = nc.alloc_sbuf_tensor_at("qko", [128, 2 * 1024], BF16,
                                  offset=abase + 112640)
    qko2 = nc.alloc_sbuf_tensor_at("qko2", [128, 2 * 1024], BF16,
                                   offset=abase + 116736)
    v65 = nc.alloc_sbuf_tensor_at("v65", [128, 2 * 1040], BF16,
                                  offset=abase + 120832)

    def ar_(off, nelem, p0=0, np_=128):
        return AP(arena, p0 * ARW + off, [[ARW, np_], [1, nelem]])

    assert max(P3_END, P6_END) <= ARW

    def fw(slot):
        return ffw[:, slot * 2048:(slot + 1) * 2048]

    ps = stack.enter_context(nc.psum_tensor("ps", [128, 4096], F32))

    semnames = ["T", "V", "SC", "C"]
    sems = {n: stack.enter_context(nc.semaphore(n)) for n in semnames}
    p = P(nc, sems, lambda nm: stack.enter_context(nc.semaphore(nm)))

    bar_idx = [0]

    def barrier_max(src_ev, m_ap, nvals, extra=()):
        i = bar_idx[0]
        bar_idx[0] += 1
        cr, ari, aro = crs[i], aris[i], aros[i]
        e1 = p.dma("barrier", cr.ap()[:, 0:nvals], m_ap, waits=[src_ev],
                   stream="BAR")
        e2 = p.dma("barrier", rowb[0:1, 0:nvals * 128],
                   AP(cr, 0, [[1, 1], [1, nvals], [nvals, 128]]), waits=[e1],
                   slow=True, stream="BAR")
        e3 = p.emit("vector", lambda eng, nvals=nvals: eng.tensor_reduce(
            m0b[0:1, 0:nvals, None],
            AP(rowb, 0, [[256, 1], [128, nvals], [1, 128]]),
            AX.X, AL.max), waits=[e2], inc="V")
        e4 = p.dma("barrier", ari.ap()[:, 0:nvals], m0b[0:1, 0:nvals],
                   waits=[e3], stream="BAR")
        e5 = p.coll("AllReduce", AL.max, [list(range(8))],
                    ari.ap()[:, 0:nvals], aro.ap()[:, 0:nvals], waits=[e4],
                    stream=f"CB{i}")
        e6 = p.dma("barrier", brx[:, 6:6 + nvals],
                   aro.ap()[0, 0:nvals].partition_broadcast(128),
                   waits=[e5] + list(extra), stream="BAR")
        return e6

    e_id = p.dma("sync", identb[:, :], ident_d.ap(), stream="IDL")
    e_cos = p.dma("sync", cosb[:, :], cos_d.ap(), stream="COS")
    e_sin = p.dma("sync", sinb[:, :], sin_d.ap(), stream="SIN")
    e_ones = p.emit("vector", lambda eng: eng.memset(ones1[:, :], 1.0), inc="V")
    e_onesf = p.emit("vector", lambda eng: eng.memset(ones1f[:, :], 1.0), inc="V")
    wq_off = {0: WQ_O, 1: WK_O, 2: WV_O}
    cast_items = [(bqs[:, :], bq_d.ap()), (bks[:, :], bk_d.ap()),
                  (bvs[:, :], bv_d.ap())]
    for c in range(DC):
        for wi, wd_ in enumerate([wqT_d, wkT_d, wvT_d]):
            cast_items.append((ar_(wq_off[wi] + c * 1024, 1024),
                               wd_.ap()[c * 128:(c + 1) * 128, :]))
    CAST_STREAMS = ["OUT0", "OUT1", "OUT2", "OUT3", "FX0", "FX1", "FX2",
                    "FX3", "CHL0", "CHL1", "CHL2", "CHL3"]
    wc_last = {}
    for i, (dst, src) in enumerate(cast_items):
        st_ = CAST_STREAMS[i % 12]
        wc_last[st_] = p.dma("gpsimd", dst, src,
                             waits=[wc_last.get(st_, None)], stream=st_)
    e_wcast = tuple(wc_last.values())

    st_dq = {}

    def dequant(src_d, dst_ag_i, ntile, width, stream, start_wait=None):
        nblk = width // 32
        nch = width // 128
        last = start_wait
        ev6 = None
        ekv = []
        for t in range(ntile):
            b = t % 2
            wt = dqw[:, b * 3072:b * 3072 + width]
            el = p.dma("gpsimd", wt, src_d.ap()[t * 128:(t + 1) * 128, :],
                       waits=[last] if last else (), stream=stream)
            wv3 = AP(dqw, b * 3072, [[2 * 3072, 128], [32, nblk], [1, 32]])
            bm = dqs[:, 0:nblk]
            sv = dqs[:, 96:96 + nblk]
            rv = dqs[:, 192:192 + nblk]
            e1 = p.emit("vector", lambda eng, wv3=wv3, bm=bm: eng.tensor_reduce(
                bm[:, :, None], wv3, AX.X, AL.max, apply_absolute_value=True),
                waits=[el, ev6], inc="V")
            e2 = p.emit("vector", lambda eng, bm=bm, sv=sv: eng.tensor_scalar(
                sv, bm, 1.0 / 7.0, 1e-12, AL.mult, AL.max), waits=[e1], inc="V")
            e3 = p.emit("vector", lambda eng, sv=sv, rv=rv: eng.reciprocal(
                rv, sv), waits=[e2], inc="V")
            rbc = rv[:, :, None].broadcast_to([128, nblk, 32])
            sbc = sv[:, :, None].broadcast_to([128, nblk, 32])
            if width <= 2048:
                t1o = b * 3072 + width
            else:
                t1o = (1 - b) * 3072
            t1v = AP(dqw, t1o, [[2 * 3072, 128], [32, nblk], [1, 32]])
            e4 = p.emit("vector", lambda eng, t1v=t1v, wv3=wv3, rbc=rbc:
                        eng.tensor_tensor(t1v, wv3, rbc, AL.mult),
                        waits=[e3], inc="V")
            e5 = p.emit("vector", lambda eng, t1v=t1v: eng.tensor_scalar(
                t1v, t1v, MAGIC, MAGIC, AL.add, AL.subtract),
                waits=[e4], inc="V")
            ov3 = AP(dqo, 0, [[3072, 128], [32, nblk], [1, 32]])
            e6 = p.emit("vector", lambda eng, ov3=ov3, t1v=t1v, sbc=sbc:
                        eng.tensor_tensor(ov3, t1v, sbc, AL.mult),
                        waits=[e5], inc="V")
            ev6 = e6
            for c in range(nch):
                pslot = ps[:, 3072 + (c % 2) * 512:3072 + (c % 2) * 512 + 128]
                etr = p.emit("tensor", lambda eng, c=c, pslot=pslot:
                             eng.transpose(pslot, dqo[:, c * 128:(c + 1) * 128],
                                           identb[:, :]),
                             waits=[e6, e_id,
                                    st_dq.get(c % 2, None)], inc="T")
                eev = p.emit("vector", lambda eng, c=c, t=t, pslot=pslot,
                             ntile=ntile: eng.tensor_copy(
                                 dqt[:, (c * ntile + t) * 128:
                                     (c * ntile + t) * 128 + 128], pslot),
                             waits=[etr], inc="V")
                st_dq[c % 2] = eev
                ekv.append(eev)
            last = el
        ew = p.dma("gpsimd", AP(dst_ag_i, 0,
                              [[ntile * 128, 128], [128 * ntile * 128, nch],
                               [1, ntile * 128]]),
                   dqt[:, 0:nch * ntile * 128].rearrange(
                       "p (c x) -> p c x", c=nch),
                   waits=ekv, stream=stream)
        return ew, ev6

    e_dq_u, ev_u = dequant(wu_sh_d, wu_ag_i, 3, 1024, "DQ")
    e_dq_g, ev_g = dequant(wg_sh_d, wg_ag_i, 3, 1024, "DQ", start_wait=ev_u)
    e_dq_d, ev_d = dequant(wd_sh_d, wd_ag_i, 1, 3072, "DQ", start_wait=ev_g)
    grp8 = [list(range(8))]
    e_ag_wu = p.coll("AllGather", AL.bypass, grp8, wu_ag_i.ap(), wu_bf.ap(),
                     waits=[e_dq_u], stream="CWU")
    e_ag_wg = p.coll("AllGather", AL.bypass, grp8, wg_ag_i.ap(), wg_bf.ap(),
                     waits=[e_dq_g], stream="CWG")
    e_ag_wd = p.coll("AllGather", AL.bypass, grp8, wd_ag_i.ap(), wd_bf.ap(),
                     waits=[e_dq_d], stream="CWD")
    e_dqT_V = p.now("V")

    e_xn_w = [None] * NT
    lnA = {}
    for tt in range(NT):
        b = tt % 2
        xt = xst[:, b * 1024:(b + 1) * 1024]
        w = [lnA.get(tt - 2, None)]
        el = p.dma("sync", xt, x_d.ap()[tt * 128:(tt + 1) * 128, :], waits=w,
                   stream=f"X1{b}")
        e_s = p.emit("vector", lambda eng, xt=xt, tt=tt: eng.reduce_sum(
            lnsum[:, tt:tt + 1], xt, AX.X), waits=[el], inc="V")
        e_q = p.emit("scalar", lambda eng, xt=xt, tt=tt: eng.activation(
            sc1[:, :], xt, AF.Square, accum_out=lnsq[:, tt:tt + 1]),
            waits=[el], inc="SC")
        lnA[tt] = ("V", max(e_s[1], 0))
        lnA[(tt, "sc")] = e_q
    e_rng = [None] * 4
    for g4 in range(4):
        c0, c1 = g4 * 4, g4 * 4 + 4
        wA = [lnA[g4 * 4 + 3], lnA[(g4 * 4 + 3, "sc")],
              lnA[(g4 * 4 + 2, "sc")], lnA[(g4 * 4 + 1, "sc")],
              lnA[(g4 * 4, "sc")]]
        e1 = p.emit("vector", lambda eng, c0=c0, c1=c1: eng.tensor_scalar(
            lnmean[:, c0:c1], lnsum[:, c0:c1], 1.0 / D, None, AL.mult),
            waits=wA, inc="V")
        e2 = p.emit("vector", lambda eng, c0=c0, c1=c1: eng.tensor_scalar(
            lnvar[:, c0:c1], lnsq[:, c0:c1], 1.0 / D, None, AL.mult),
            inc="V")
        e3 = p.emit("vector", lambda eng, c0=c0, c1=c1: eng.tensor_tensor(
            lntmp[:, c0:c1], lnmean[:, c0:c1], lnmean[:, c0:c1], AL.mult),
            waits=[e1], inc="V")
        e4 = p.emit("vector", lambda eng, c0=c0, c1=c1: eng.tensor_tensor(
            lnvar[:, c0:c1], lnvar[:, c0:c1], lntmp[:, c0:c1], AL.subtract),
            waits=[e2, e3], inc="V")
        e5 = p.emit("vector", lambda eng, c0=c0, c1=c1: eng.tensor_scalar(
            lnvar[:, c0:c1], lnvar[:, c0:c1], EPS, None, AL.add),
            waits=[e4], inc="V")
        e_sq = p.emit("scalar", lambda eng, c0=c0, c1=c1: eng.activation(
            lnsqr[:, c0:c1], lnvar[:, c0:c1], AF.Sqrt), waits=[e5], inc="SC")
        e_rn = p.emit("vector", lambda eng, c0=c0, c1=c1: eng.reciprocal(
            lnrst[:, c0:c1], lnsqr[:, c0:c1]), waits=[e_sq], inc="V")
        for _ in range(2):
            en1 = p.emit("vector", lambda eng, c0=c0, c1=c1: eng.tensor_tensor(
                lntmp[:, c0:c1], lnrst[:, c0:c1], lnrst[:, c0:c1], AL.mult),
                waits=[e_rn], inc="V")
            en2 = p.emit("vector", lambda eng, c0=c0, c1=c1: eng.tensor_tensor(
                lntmp[:, c0:c1], lntmp[:, c0:c1], lnvar[:, c0:c1], AL.mult),
                waits=[en1], inc="V")
            en3 = p.emit("vector", lambda eng, c0=c0, c1=c1: eng.tensor_scalar(
                lntmp[:, c0:c1], lntmp[:, c0:c1], -0.5, 1.5, AL.mult, AL.add),
                waits=[en2], inc="V")
            e_rn = p.emit("vector", lambda eng, c0=c0, c1=c1: eng.tensor_tensor(
                lnrst[:, c0:c1], lnrst[:, c0:c1], lntmp[:, c0:c1], AL.mult),
                waits=[en3], inc="V")
        e_rng[g4] = e_rn

    e_mx = None
    for tt in range(NT):
        b = tt % 2
        xt = xst[:, b * 1024:(b + 1) * 1024]
        w = [e_xn_w[tt - 2][0], e_xn_w[tt - 2][1]] if tt >= 2 else \
            [lnA[NT - 1], lnA[NT - 2], lnA[(NT - 1, "sc")],
             lnA[(NT - 2, "sc")]]
        el = p.dma("sync", xt, x_d.ap()[tt * 128:(tt + 1) * 128, :], waits=w,
                   stream=f"X1{b}")
        xnt = xn32s[:, b * 1024:(b + 1) * 1024]
        xbt = xnbs[:, b * 1024:(b + 1) * 1024]
        wprev = [e_xn_w[tt - 2][1], e_xn_w[tt - 2][0]] if tt >= 2 else []
        e_xn = p.emit("vector", lambda eng, xt=xt, xnt=xnt, tt=tt:
                      eng.tensor_scalar(xnt, xt, lnmean[:, tt:tt + 1],
                                        lnrst[:, tt:tt + 1], AL.subtract,
                                        AL.mult),
                      waits=[e_rng[tt // 4], el] + wprev, inc="V")
        e_xb = p.emit("vector", lambda eng, xnt=xnt, xbt=xbt: eng.tensor_copy(
            xbt, xnt), waits=[e_xn], inc="V")
        mc = 0 if tt == 0 else 1
        e_m2 = p.emit("vector", lambda eng, xnt=xnt, mc=mc: eng.tensor_reduce(
            mux[:, mc:mc + 1], xnt, AX.X, AL.max, apply_absolute_value=True),
            waits=[e_xn, e_mx], inc="V")
        if tt > 0:
            e_m2 = p.emit("vector", lambda eng: eng.tensor_tensor(
                mux[:, 0:1], mux[:, 0:1], mux[:, 1:2], AL.max),
                waits=[e_m2], inc="V")
        e_mx = e_m2
        ew1 = p.dma("sync", xn32_d.ap()[tt * 128:(tt + 1) * 128, :], xnt,
                    waits=[e_xn], stream=f"XNW{b}")
        ew2 = p.dma("sync", xnbf_d.ap()[tt * 128:(tt + 1) * 128, :], xbt,
                    waits=[e_xb], stream=f"XBW{b}")
        e_xn_w[tt] = (ew1, ew2)
    e_ln_SC = p.now("SC")

    e_b1 = barrier_max(e_mx, mux[:, 0:1], 1)
    e_s1 = p.emit("vector", lambda eng: eng.tensor_scalar(
        brx[:, 0:1], brx[:, 6:7], 1.0 / QA_, 1e-12, AL.mult, AL.max),
        waits=[e_b1], inc="V")
    e_rs1 = p.emit("vector", lambda eng: eng.reciprocal(
        brx[:, 1:2], brx[:, 0:1]), waits=[e_s1], inc="V")



    e_xq_w = []

    def qdq_tile(tt):
        b = tt % 2
        xt = xst[:, b * 1024:(b + 1) * 1024]
        w = [e_xn_w[tt][0]]
        w += ([e_xq_w[tt - 2]] if tt >= 2 else
              [e_xn_w[NT - 1][0], e_xn_w[NT - 2][0],
               e_xn_w[NT - 1][1], e_xn_w[NT - 2][1]])
        el = p.dma("sync", xt, xn32_d.ap()[tt * 128:(tt + 1) * 128, :],
                   waits=w, stream=f"XQL{tt % 2}")
        e1 = p.emit("vector", lambda eng, xt=xt: eng.tensor_scalar(
            sc1[:, :], xt, brx[:, 1:2], MAGIC, AL.mult, AL.add),
            waits=[el, e_rs1, e_ln_SC], inc="V")
        xbt = xnbs[:, b * 1024:(b + 1) * 1024]
        e2 = p.emit("vector", lambda eng, xbt=xbt: eng.tensor_scalar(
            xbt, sc1[:, :], MAGIC, brx[:, 0:1], AL.subtract, AL.mult),
            waits=[e1], inc="V")
        e_xq_w.append(p.dma("sync", xqbf_d.ap()[tt * 128:(tt + 1) * 128, :],
                            xbt, waits=[e2], stream=f"XQW{tt % 2}"))



    e_xnfm_g = [None] * 4
    for g in range(4):
        for c in range(DC):
            e_xnfm_g[g] = p.dma(
                "sync", ar_(XNFM_O + c * 2048 + g * 512, 512),
                xnbf_d.ap()[g * 512:(g + 1) * 512, c * 128:(c + 1) * 128],
                waits=[e_xn_w[4 * g + 3][1], e_xn_w[4 * g + 2][1],
                       e_xnfm_g[g], e_dq_u, e_dq_g, e_dq_d, ev_d],
                transpose=True, stream=f"XNF{g}")
    e_xnfm = e_xnfm_g[3]
    e_v65i = p.emit("vector", lambda eng: eng.memset(
        AP(v65, 64, [[2 * 1040, 128], [1040, 2], [65, 16]]), 1.0), inc="V")

    PSLOT4 = [ps[:, s * 1024:(s + 1) * 1024] for s in range(4)]
    slot_ev = [NOEV, NOEV, NOEV, e_dqT_V]
    e_q_w, e_k_w, e_v_w = [], [], []
    slot_i = [0]

    def proj(woff, bsb, tt, extra):
        s = slot_i[0] % 4
        slot_i[0] += 1
        pslot = PSLOT4[s]
        for c in range(DC):
            for ob in range(2):
                w = ([slot_ev[s]] + extra) if (c == 0 and ob == 0) else ()
                p.emit("tensor", lambda eng, pslot=pslot, c=c, ob=ob, tt=tt,
                       woff=woff: eng.matmul(
                           pslot[:, ob * 512:(ob + 1) * 512],
                           AP(arena, XNFM_O + c * 2048 + tt * 128,
                              [[ARW, 128], [1, 128]]),
                           AP(arena, woff + c * 1024 + ob * 512,
                              [[ARW, 128], [1, 512]]),
                           start=(c == 0), stop=False, skip_group_check=True),
                       waits=w, inc="T")
        ev = None
        for ob in range(2):
            ev = p.emit("tensor", lambda eng, pslot=pslot, ob=ob, bsb=bsb:
                        eng.matmul(pslot[:, ob * 512:(ob + 1) * 512],
                                   ones1[0:1, 0:128],
                                   bsb[0:1, ob * 512:(ob + 1) * 512],
                                   start=False, stop=True,
                                   skip_group_check=True), inc="T")
        return s, ev

    def rope_evac(s, tt, osb, obase, waits):
        Ev = AP(ps, s * 1024, [[4096, 128], [64, 16], [1, 32]])
        Ov = AP(ps, s * 1024 + 32, [[4096, 128], [64, 16], [1, 32]])
        OE = AP(osb, obase, [[2 * 1024, 128], [64, 16], [1, 32]])
        OO = AP(osb, obase + 32, [[2 * 1024, 128], [64, 16], [1, 32]])
        cv = AP(cosb, tt * 32, [[NT * 32, 128], [0, 16], [1, 32]])
        sv = AP(sinb, tt * 32, [[NT * 32, 128], [0, 16], [1, 32]])
        r1 = AP(rt1, 0, [[512, 128], [32, 16], [1, 32]])
        r2 = AP(rt2, 0, [[512, 128], [32, 16], [1, 32]])
        p.emit("vector", lambda eng: eng.tensor_tensor(r1, Ev, cv, AL.mult),
               waits=list(waits) + [e_cos, e_sin], inc="V")
        p.emit("vector", lambda eng: eng.tensor_tensor(r2, Ov, sv, AL.mult),
               inc="V")
        p.emit("vector", lambda eng: eng.tensor_tensor(OE, r1, r2, AL.subtract),
               inc="V")
        p.emit("vector", lambda eng: eng.tensor_tensor(r1, Ev, sv, AL.mult),
               inc="V")
        p.emit("vector", lambda eng: eng.tensor_tensor(r2, Ov, cv, AL.mult),
               inc="V")
        return p.emit("vector", lambda eng: eng.tensor_tensor(
            OO, r1, r2, AL.add), inc="V")

    for tt in range(NT):
        b = tt % 2
        s, e_mm = proj(WQ_O, bqs, tt,
                       [e_xnfm_g[tt // 4], e_ones] + list(e_wcast))
        e_ro = rope_evac(s, tt, qko, b * 1024,
                         [e_mm] + ([e_q_w[tt - 2]] if tt >= 2 else []))
        slot_ev[s] = e_ro
        e_q_w.append(p.dma("sync", q_tm_d.ap()[tt * 128:(tt + 1) * 128, :],
                           qko[:, b * 1024:(b + 1) * 1024], waits=[e_ro],
                           stream=f"QW{tt % 2}"))
        s, e_mm = proj(WK_O, bks, tt, [])
        e_ro = rope_evac(s, tt, qko2, b * 1024,
                         [e_mm] + ([e_k_w[tt - 2]] if tt >= 2 else []))
        slot_ev[s] = e_ro
        e_k_w.append(p.dma("sync", kag_i.ap()[tt * 128:(tt + 1) * 128, :],
                           qko2[:, b * 1024:(b + 1) * 1024], waits=[e_ro],
                           stream=f"KW{tt % 2}"))
        s, e_mm = proj(WV_O, bvs, tt, [])
        pv_in = AP(ps, s * 1024, [[4096, 128], [64, 16], [1, 64]])
        vv_out = AP(v65, b * 1040, [[2 * 1040, 128], [65, 16], [1, 64]])
        e_cv = p.emit("vector", lambda eng, pv_in=pv_in, vv_out=vv_out:
                      eng.tensor_copy(vv_out, pv_in),
                      waits=[e_mm, e_v65i] +
                      ([e_v_w[tt - 2]] if tt >= 2 else []), inc="V")
        slot_ev[s] = e_cv
        e_v_w.append(p.dma("sync", vag_i.ap()[tt * 128:(tt + 1) * 128, :],
                           v65[:, b * 1040:(b + 1) * 1040], waits=[e_cv],
                           stream=f"VW{tt % 2}"))
        qdq_tile(tt)

    grpp = [[0, 1], [2, 3], [4, 5], [6, 7]]
    e_ag_k = p.coll("AllGather", AL.bypass, grpp, kag_i.ap(), kag_o.ap(),
                    waits=[e_k_w[NT - 1], e_k_w[NT - 2]], stream="CK")
    e_ag_v = p.coll("AllGather", AL.bypass, grpp, vag_i.ap(), vag_o.ap(),
                    waits=[e_v_w[NT - 1], e_v_w[NT - 2]], stream="CV")

    e_p2T = p.now("T")
    e_p2V = [e for e in slot_ev]
    wul_gate = [e_ag_wu, e_ag_wg, e_p2T] + e_xnfm_g + e_p2V
    xqf_gate = [e_dq_d, ev_d,
                e_v_w[NT - 1], e_v_w[NT - 2], e_q_w[NT - 1], e_q_w[NT - 2],
                e_k_w[NT - 1], e_k_w[NT - 2]]
    wul_last = [None] * 6
    e_wT_l, e_xqfm_l = [], []
    for c in range(DC):
        st0, st1 = f"WUL{c % 2}", f"WUL{2 + c % 2}"
        e1_ = p.dma("sync", AP(arena, WU_O + c * 3072,
                               [[ARW, 128], [384, 8], [1, 384]]),
                    AP(wu_bf, c * 128 * 384,
                       [[384, 128], [D * 384, 8], [1, 384]]),
                    waits=wul_gate + [wul_last[c % 2]], stream=st0)
        wul_last[c % 2] = e1_
        e_wT_l.append(e1_)
        e2_ = p.dma("sync", AP(arena, WG_O + c * 3072,
                               [[ARW, 128], [384, 8], [1, 384]]),
                    AP(wg_bf, c * 128 * 384,
                       [[384, 128], [D * 384, 8], [1, 384]]),
                    waits=wul_gate + [wul_last[2 + c % 2]], stream=st1)
        wul_last[2 + c % 2] = e2_
        e_wT_l.append(e2_)
    e_wu_only = [wul_last[0], wul_last[1]]
    e_wg_only = [wul_last[2], wul_last[3]]
    e_xqf_g = [None] * 4
    for g in range(4):
        for c in range(DC):
            e_xqf_g[g] = p.dma(
                "sync", ar_(XQ_O + c * 2048 + g * 512, 512),
                xqbf_d.ap()[g * 512:(g + 1) * 512, c * 128:(c + 1) * 128],
                waits=xqf_gate + [e_xq_w[4 * g + 3], e_xq_w[4 * g + 2],
                                 e_xqf_g[g]],
                transpose=True, stream=f"XQF{g}")
    e_wul = [e for e in wul_last[0:4]]
    e_wT = None
    e_xqfm = None

    if "xqfm" in debug:
        dbg_xqfm = nc.dram_tensor("dbg_xqfm", [128, 8 * 2048], BF16,
                                  kind="ExternalOutput")
        dbg_wuT = nc.dram_tensor("dbg_wuT", [128, 8 * 3072], BF16,
                                 kind="ExternalOutput")
        p.dma("sync", dbg_xqfm.ap(), ar_(XQ_O, 8 * 2048),
              waits=e_wul + e_xqf_g, stream="DBG")
        p.dma("sync", dbg_wuT.ap(), ar_(WU_O, 8 * 3072), waits=e_wul,
              stream="DBG")

    PS8 = [ps[:, s * 512:(s + 1) * 512] for s in range(8)]
    ps8_ev = [NOEV] * 8
    sp_ev = [NOEV] * 4
    si = [0, 0]
    mux_last = {}
    spill_last = {}
    for wi, (woff, spd, muxc) in enumerate([(WU_O, up_sp, 1),
                                            (WG_O, gate_sp, 2)]):
        e_m = None
        e_sp = None
        for it in range(ITI):
            for tb in range(QBN):
                s8 = si[0] % 8
                si[0] += 1
                e_mm = None
                for c in range(DC):
                    wgt = e_wu_only if woff == WU_O else e_wg_only
                    w = ([ps8_ev[s8], e_xqf_g[tb]] + wgt) if c == 0 else ()
                    e_mm = p.emit("tensor", lambda eng, s8=s8, c=c, it=it,
                                  tb=tb, woff=woff: eng.matmul(
                                      PS8[s8],
                                      AP(arena, woff + c * 3072 + it * 128,
                                         [[ARW, 128], [1, 128]]),
                                      AP(arena, XQ_O + c * 2048 + tb * 512,
                                         [[ARW, 128], [1, 512]]),
                                      start=(c == 0), stop=(c == DC - 1),
                                      skip_group_check=True),
                                  waits=w, inc="T")
                ss = si[1] % 4
                si[1] += 1
                stg = spst[:, ss * 512:(ss + 1) * 512]
                e_ev = p.emit("vector", lambda eng, stg=stg, s8=s8:
                              eng.tensor_copy(stg, PS8[s8]),
                              waits=[e_mm, sp_ev[ss]], inc="V")
                if it == 0 and tb == 0:
                    e_m = p.emit("vector", lambda eng, s8=s8, muxc=muxc:
                                 eng.tensor_reduce(
                                     mux[:, muxc:muxc + 1], PS8[s8], AX.X,
                                     AL.max, apply_absolute_value=True),
                                 waits=[e_mm], inc="V")
                else:
                    e_m = p.emit("vector", lambda eng, s8=s8: eng.tensor_reduce(
                        mux[:, 3:4], PS8[s8], AX.X, AL.max,
                        apply_absolute_value=True), waits=[e_mm, e_m], inc="V")
                    e_m = p.emit("vector", lambda eng, muxc=muxc:
                                 eng.tensor_tensor(mux[:, muxc:muxc + 1],
                                                   mux[:, muxc:muxc + 1],
                                                   mux[:, 3:4], AL.max),
                                 waits=[e_m], inc="V")
                ps8_ev[s8] = ("V", max(e_ev[1], e_m[1]))
                e_sp = p.dma("scalar",
                             spd.ap()[it * 128:(it + 1) * 128,
                                      tb * 512:(tb + 1) * 512],
                             stg, waits=[e_ev], stream=f"SP{ss}")
                sp_ev[ss] = e_sp
        mux_last[wi] = e_m
        spill_last[wi] = [e for e in sp_ev]

    e_cp = p.emit("vector", lambda eng: eng.tensor_copy(
        mux[:, 4:5], mux[:, 1:2]), waits=[mux_last[0], mux_last[1]], inc="V")
    e_cp = p.emit("vector", lambda eng: eng.tensor_copy(
        mux[:, 5:6], mux[:, 2:3]), waits=[e_cp], inc="V")
    e_b2 = barrier_max(e_cp, mux[:, 4:6], 2, extra=[e_s1])
    e_su = p.emit("vector", lambda eng: eng.tensor_scalar(
        brx[:, 2:4], brx[:, 6:8], 1.0 / QA_, 1e-12, AL.mult, AL.max),
        waits=[e_b2], inc="V")
    e_rsu = p.emit("vector", lambda eng: eng.reciprocal(
        brx[:, 4:6], brx[:, 2:4]), waits=[e_su], inc="V")
    e_mbg = p.emit("vector", lambda eng: eng.tensor_scalar(
        lnst[:, 6:7], brx[:, 3:4], -MAGIC, None, AL.mult),
        waits=[e_su], inc="V")

    e_dbg_up = e_dbg_gate = None
    if "up_raw" in debug:
        t_ = nc.dram_tensor("dbg_up_raw", [I, T], F32, kind="ExternalOutput")
        e_dbg_up = p.dma("sync", t_.ap(), up_sp.ap(),
                         waits=spill_last[0], stream="DBG")
    if "gate_raw" in debug:
        t_ = nc.dram_tensor("dbg_gate_raw", [I, T], F32, kind="ExternalOutput")
        e_dbg_gate = p.dma("sync", t_.ap(), gate_sp.ap(),
                           waits=spill_last[1], stream="DBG")
    e_g2_w = [None] * ITI
    e_sig = [None] * ITI
    e_gq = [None] * ITI
    gq_read = [None] * ITI
    sig_read = [None] * ITI
    st_ = {}

    s1_state = {}

    def stage1_load(it):
        inb = fw(it % 2)
        w = [s1_state.get(("t1e", it - 2), None)]
        el = p.dma("gpsimd", inb, gate_sp.ap()[it * 128:(it + 1) * 128, :],
                   waits=spill_last[1] + [e_dbg_gate] + w,
                   stream=f"S1L{it % 2}")
        t1 = fw(2 + it % 2)
        e1 = p.emit("vector", lambda eng, inb=inb, t1=t1: eng.tensor_scalar(
            t1, inb, brx[:, 5:6], MAGIC, AL.mult, AL.add),
            waits=[el, e_rsu, e_mbg, s1_state.get(("gq", it - 2), None),
                   s1_state.get(("sg", it - 2), None)], inc="V")
        s1_state[("t1e", it)] = e1

    def stage1_sig(it):
        t1 = fw(2 + it % 2)
        sg = fw(6 + it % 2)
        e = p.emit("scalar", lambda eng, t1=t1, sg=sg: eng.activation(
            sg, t1, AF.Sigmoid, bias=lnst[:, 6:7], scale=brx[:, 3:4]),
            waits=[s1_state[("t1e", it)],
                   s1_state.get(("sgq", it - 2), None)], inc="SC")
        e_sig[it] = e

    def stage1_gq(it):
        t1 = fw(2 + it % 2)
        gq = fw(4 + it % 2)
        e = p.emit("vector", lambda eng, t1=t1, gq=gq: eng.tensor_scalar(
            gq, t1, MAGIC, brx[:, 3:4], AL.subtract, AL.mult),
            waits=[s1_state[("t1e", it)],
                   s1_state.get(("g2", it - 2), None)], inc="V")
        s1_state[("gq", it)] = e
        e_gq[it] = e

    def stage1_tail(it):
        gq = fw(4 + it % 2)
        sg = fw(6 + it % 2)
        ob = fw(8 + it % 2)
        e3 = p.emit("vector", lambda eng, sg=sg: eng.tensor_scalar(
            sg, sg, 65536.0, MAGIC, AL.mult, AL.add),
            waits=[e_sig[it]], inc="V")
        e4 = p.emit("vector", lambda eng, sg=sg: eng.tensor_scalar(
            sg, sg, MAGIC, 65535.0, AL.subtract, AL.min), waits=[e3], inc="V")
        wo = [e_g2_w[it - 2]] if it >= 2 else []
        e5 = p.emit("vector", lambda eng, ob=ob, gq=gq, sg=sg:
                    eng.scalar_tensor_tensor(ob, gq, 1.0 / 65536.0, sg,
                                             AL.mult, AL.mult),
                    waits=[e4] + wo, inc="V")
        s1_state[("g2", it)] = e5
        s1_state[("sg", it)] = e5
        s1_state[("sgq", it)] = e5
        if it == 0:
            e_m = p.emit("vector", lambda eng, ob=ob: eng.tensor_reduce(
                mux[:, 6:7], ob, AX.X, AL.max, apply_absolute_value=True),
                waits=[e5], inc="V")
        else:
            e_m = p.emit("vector", lambda eng, ob=ob: eng.tensor_reduce(
                mux[:, 7:8], ob, AX.X, AL.max, apply_absolute_value=True),
                waits=[e5, st_["m1"]], inc="V")
            e_m = p.emit("vector", lambda eng: eng.tensor_tensor(
                mux[:, 6:7], mux[:, 6:7], mux[:, 7:8], AL.max),
                waits=[e_m], inc="V")
        st_["m1"] = e_m
        e_g2_w[it] = p.dma("gpsimd", gate_sp.ap()[it * 128:(it + 1) * 128, :],
                           ob, waits=[e5, e_m], stream=f"G2W{it % 2}")

    def barrier3():
        e_b3 = barrier_max(st_["m1"], mux[:, 6:7], 1, extra=[e_su])
        e_s3 = p.emit("vector", lambda eng: eng.tensor_scalar(
            stt[:, 3:4], brx[:, 6:7], 1.0 / QA_, 1e-12, AL.mult, AL.max),
            waits=[e_b3], inc="V")
        st_["s2ev"] = e_s3
        st_["rs2ev"] = p.emit("vector", lambda eng: eng.reciprocal(
            stt[:, 5:6], stt[:, 3:4]), waits=[e_s3], inc="V")

    e_o_w = [None] * ITI
    e_dbg_g2 = None

    def stage2_load(it):
        g2b = fw(it % 2)
        upb = fw(2 + it % 2)
        w1 = [e_o_w[it - 2]] if it >= 2 else []
        el1 = p.dma("gpsimd", g2b, gate_sp.ap()[it * 128:(it + 1) * 128, :],
                    waits=[e_g2_w[ITI - 1], e_g2_w[ITI - 2], st_["rs2ev"],
                           st_.get("dbg_g2", None)] + w1,
                    stream=f"S2A{it % 2}")
        el2 = p.dma("gpsimd", upb, up_sp.ap()[it * 128:(it + 1) * 128, :],
                    waits=spill_last[0] + [e_g2_w[ITI - 1], e_g2_w[ITI - 2],
                                           e_dbg_up] + w1,
                    stream=f"S2B{it % 2}")
        s1_state[("s2", it)] = (el1, el2, None)

    def stage2_qdq(it, k):
        g2b = fw(it % 2)
        upb = fw(2 + it % 2)
        g2q = fw(4 + it % 2)
        upq = fw(6 + it % 2)
        el1, el2, prev = s1_state[("s2", it)]
        src, dst, rss, ss_ = ((g2b, g2q, stt[:, 5:6], stt[:, 3:4]) if k < 2
                              else (upb, upq, brx[:, 4:5], brx[:, 2:3]))
        half = k % 2
        e1 = p.emit("vector", lambda eng, src=src, rss=rss, half=half:
                    eng.tensor_scalar(
                        sc1[:, :], src[:, half * 1024:(half + 1) * 1024],
                        rss, MAGIC, AL.mult, AL.add),
                    waits=[el1, el2, e_rsu, prev], inc="V")
        e2 = p.emit("vector", lambda eng, dst=dst, ss_=ss_, half=half:
                    eng.tensor_scalar(
                        dst[:, half * 1024:(half + 1) * 1024], sc1[:, :],
                        MAGIC, ss_, AL.subtract, AL.mult),
                    waits=[e1], inc="V")
        s1_state[("s2", it)] = (el1, el2, e2)

    def stage2_mul(it):
        g2q = fw(4 + it % 2)
        upq = fw(6 + it % 2)
        ob = fw(8 + it % 2)
        _, _, prev = s1_state[("s2", it)]
        wo = [e_o_w[it - 2]] if it >= 2 else []
        e5 = p.emit("vector", lambda eng, ob=ob, g2q=g2q, upq=upq:
                    eng.tensor_tensor(ob, g2q, upq, AL.mult),
                    waits=[prev] + wo, inc="V")
        s1_state[("s2m", it)] = e5

    def stage2_max(it):
        ob = fw(8 + it % 2)
        e5 = s1_state[("s2m", it)]
        if it == 0:
            e_m = p.emit("vector", lambda eng, ob=ob: eng.tensor_reduce(
                mux[:, 6:7], ob, AX.X, AL.max, apply_absolute_value=True),
                waits=[e5, st_["b3r"]], inc="V")
        else:
            e_m = p.emit("vector", lambda eng, ob=ob: eng.tensor_reduce(
                mux[:, 7:8], ob, AX.X, AL.max, apply_absolute_value=True),
                waits=[e5, st_["m2"]], inc="V")
            e_m = p.emit("vector", lambda eng: eng.tensor_tensor(
                mux[:, 6:7], mux[:, 6:7], mux[:, 7:8], AL.max),
                waits=[e_m], inc="V")
        st_["m2"] = e_m
        e_o_w[it] = p.dma("gpsimd", up_sp.ap()[it * 128:(it + 1) * 128, :],
                          ob, waits=[e5, e_m], stream=f"OW2{it % 2}")

    def barrier4():
        e_b4 = barrier_max(st_["m2"], mux[:, 6:7], 1, extra=[st_["s2ev"]])
        e_s4 = p.emit("vector", lambda eng: eng.tensor_scalar(
            stt[:, 6:7], brx[:, 6:7], 1.0 / QA_, 1e-12, AL.mult, AL.max),
            waits=[e_b4], inc="V")
        st_["s4ev"] = e_s4
        st_["rs4ev"] = p.emit("vector", lambda eng: eng.reciprocal(
            stt[:, 7:8], stt[:, 6:7]), waits=[e_s4], inc="V")

    e_oq_w = [None] * ITI

    def stage3_load(it):
        ob = fw(it % 2)
        w = [e_oq_w[it - 2]] if it >= 2 else []
        el = p.dma("gpsimd", ob, up_sp.ap()[it * 128:(it + 1) * 128, :],
                   waits=[e_o_w[ITI - 1], e_o_w[ITI - 2], st_["rs4ev"]] + w,
                   stream=f"S3L{it % 2}")
        s1_state[("s3", it)] = (el, None, [])

    def stage3_half(it, half):
        ob = fw(it % 2)
        el, prev, evs = s1_state[("s3", it)]
        e1 = p.emit("vector", lambda eng, ob=ob, half=half:
                    eng.tensor_scalar(
                        sc1[:, :], ob[:, half * 1024:(half + 1) * 1024],
                        stt[:, 7:8], MAGIC, AL.mult, AL.add),
                    waits=[el, prev], inc="V")
        prev = p.emit("vector", lambda eng, it=it, half=half:
                      eng.tensor_scalar(
                          oqbf[:, (it % 2) * 2048 + half * 1024:
                               (it % 2) * 2048 + (half + 1) * 1024],
                          sc1[:, :], MAGIC, stt[:, 6:7], AL.subtract,
                          AL.mult), waits=[e1], inc="V")
        evs = evs + [prev]
        s1_state[("s3", it)] = (el, prev, evs)
        if half == 1:
            e_oq_w[it] = p.dma("gpsimd",
                               oq_d.ap()[it * 128:(it + 1) * 128, :],
                               oqbf[:, (it % 2) * 2048:(it % 2 + 1) * 2048],
                               waits=evs, stream=f"OQW{it % 2}")

    STB = [ps[:, i * 512:(i + 1) * 512] for i in range(6)]
    PVB = [ps[0:65, 3072:3584], ps[0:65, 3584:4096]]
    kp_ev = [NOEV, NOEV]
    e_kp, e_vp, e_qp = [NOEV, NOEV], [NOEV, NOEV], [NOEV, NOEV]
    e_p3T = p.now("T")

    def prefetch_pair(hp):
        b = hp % 2
        e_kp[b] = p.dma("sync", ar_(KP_O + b * 4096, 4096),
                        kag_o.ap()[:, hp * 128:(hp + 1) * 128],
                        waits=[e_ag_k, kp_ev[b], e_p3T], transpose=True,
                        stream=f"KP{b}")
        vdst = AP(arena, VP_O + b * 4160, [[ARW, 128], [130, KCN], [1, 130]])
        vsrc = AP(vag_o, hp * 130,
                  [[H * 65, 128], [128 * H * 65, KCN], [1, 130]])
        e_vp[b] = p.dma("sync", vdst, vsrc,
                        waits=[e_ag_v, kp_ev[b], e_p3T], stream=f"VP{b}")
        e_qp[b] = p.dma("sync", ar_(QP_O + b * 2048, 2048),
                        q_tm_d.ap()[:, hp * 128:(hp + 1) * 128],
                        waits=[e_q_w[NT - 1], e_q_w[NT - 2], kp_ev[b], e_p3T],
                        transpose=True, stream=f"QP{b}")

    st_["b3r"] = NOEV
    ffn_sched = {}

    def at_(r, g, fn):
        ffn_sched.setdefault((r, g), []).append(fn)

    NG = len(KC_GROUPS)
    for it_ in range(ITI):
        r, o = it_ // 2, it_ % 2
        at_(r, 0 + o, lambda it=it_: stage1_load(it))
        at_(r, 3 + o, lambda it=it_: (stage1_sig(it), stage1_gq(it)))
        at_(r, 7 + o, lambda it=it_: stage1_tail(it))

    def _b3():
        if "g2" in debug:
            t_ = nc.dram_tensor("dbg_g2", [I, T], F32, kind="ExternalOutput")
            st_["dbg_g2"] = p.dma("sync", t_.ap(), gate_sp.ap(),
                                  waits=[e_g2_w[ITI - 1], e_g2_w[ITI - 2]],
                                  stream="DBG")
        barrier3()
        st_["b3r"] = st_["rs2ev"]
    at_(12, 2, _b3)
    for it_ in range(ITI):
        r, o = 13 + it_ // 2, it_ % 2
        at_(r, 0 + o, lambda it=it_: (stage2_load(it), stage2_qdq(it, 0),
                                      stage2_qdq(it, 1)))
        at_(r, 5 + o, lambda it=it_: (stage2_qdq(it, 2), stage2_qdq(it, 3),
                                      stage2_mul(it), stage2_max(it)))
    at_(25, 2, barrier4)
    for it_ in range(ITI):
        r, o = 26 + it_ // 5, it_ % 5
        at_(r, 2 * o, lambda it=it_: (stage3_load(it), stage3_half(it, 0),
                                      stage3_half(it, 1)))
    sched_pos = [0]

    def pump_slot(r, g):
        tgt = r * NG + g
        while sched_pos[0] <= tgt:
            rr, gg = sched_pos[0] // NG, sched_pos[0] % NG
            for fn in ffn_sched.get((rr, gg), []):
                fn()
            sched_pos[0] += 1

    def pump_all():
        mx = max(r * NG + g for (r, g) in ffn_sched) if ffn_sched else 0
        while sched_pos[0] <= mx:
            rr, gg = sched_pos[0] // NG, sched_pos[0] % NG
            for fn in ffn_sched.get((rr, gg), []):
                fn()
            sched_pos[0] += 1

    prefetch_pair(0)
    e_p3V = p.now("V")
    pv_free = [e_p3V, e_p3V]
    e_attn_w = []
    st_free = [e_p3V] * 6
    pt_last = [NOEV] * 4
    e_pv_g = [NOEV, NOEV]

    for hp in range(NPAIR):
        b = hp % 2
        if hp + 1 < NPAIR:
            prefetch_pair(hp + 1)
        kpe, vpe, qpe = e_kp[b], e_vp[b], e_qp[b]
        for qb in range(QBN):
            pend_pv = []

            def emit_pv(hh, gi, kc0, glen, e_e):
                e_pv = None
                for j in range(glen):
                    kc = kc0 + j
                    vlhs = AP(arena, VP_O + b * 4160 + kc * 130 + hh * 65,
                              [[ARW, 128], [1, 65]])
                    pt_off = (hh * 2 + gi % 2) * 1536
                    prhs = ptb[:, pt_off + j * 512:pt_off + (j + 1) * 512]
                    w = [e_e, vpe]
                    if gi == 0 and j == 0:
                        w.append(pv_free[hh])
                    e_pv = p.emit("tensor", lambda eng, hh=hh, vlhs=vlhs,
                                  prhs=prhs, st=(gi == 0 and j == 0),
                                  sp_=(kc == KCN - 1): eng.matmul(
                                      PVB[hh], vlhs, prhs, start=st,
                                      stop=sp_, skip_group_check=True),
                                  waits=w, inc="T")
                e_pv_g[hh] = e_pv
                pt_last[hh * 2 + gi % 2] = ("T", e_pv[1])

            for gi, (kc0, glen) in enumerate(KC_GROUPS):
                e_st = [None, None]
                for hh in range(2):
                    for j in range(glen):
                        kc = kc0 + j
                        bank = hh * 3 + j
                        klhs = AP(arena, hh * 64 * ARW + KP_O + b * 4096
                                  + kc * 128, [[ARW, 64], [1, 128]])
                        qrhs = AP(arena, hh * 64 * ARW + QP_O + b * 2048
                                  + qb * 512, [[ARW, 64], [1, 512]])
                        e_st[hh] = p.emit(
                            "tensor", lambda eng, bank=bank, klhs=klhs,
                            qrhs=qrhs, hh=hh: eng.matmul(
                                STB[bank], klhs, qrhs, start=True, stop=True,
                                skip_group_check=True,
                                tile_position=(0, 0) if hh == 0 else (64, 0)),
                            waits=[kpe, qpe, st_free[bank]], inc="T")
                exps_now = []
                for hh in range(2):
                    pslot = hh * 2 + gi % 2
                    pt_off = pslot * 1536
                    pdst = ptb[:, pt_off:pt_off + glen * 512]
                    psrc = ps[:, hh * 3 * 512: hh * 3 * 512 + glen * 512]
                    e_e = p.emit("scalar", lambda eng, pdst=pdst, psrc=psrc:
                                 eng.activation(pdst, psrc, AF.Exp,
                                                scale=ISCALE),
                                 waits=[e_st[hh], pt_last[pslot]], inc="SC")
                    for j in range(glen):
                        st_free[hh * 3 + j] = e_e
                    exps_now.append((hh, gi, kc0, glen, e_e))
                for args in pend_pv:
                    emit_pv(*args)
                pend_pv = exps_now
                pump_slot(hp * QBN + qb, gi)
            for args in pend_pv:
                emit_pv(*args)
            for hh in range(2):
                auo = (2 * (qb % 2) + hh) * 512
                au = aub[:, auo:auo + 512]
                e_ev = p.emit("vector", lambda eng, au=au, hh=hh:
                              eng.tensor_copy(au[0:65, :], PVB[hh]),
                              waits=[e_pv_g[hh]] +
                              ([e_attn_w[-4]] if len(e_attn_w) >= 4 else []),
                              inc="V")
                pv_free[hh] = e_ev
                e_l1 = p.emit("scalar", lambda eng, auo=auo: eng.activation(
                    lnr[64:65, 0:512], aub[64:65, auo:auo + 512], AF.Ln),
                    waits=[e_ev, st_.get("lnfree", NOEV)], inc="SC")
                e_l2 = p.emit("scalar", lambda eng: eng.activation(
                    lnr[64:65, 0:512], lnr[64:65, 0:512], AF.Exp, scale=-1.0),
                    waits=[e_l1], inc="SC")
                e_bc = p.emit("tensor", lambda eng: eng.matmul(
                    STB[5][0:64, :], ones1f[64:65, 0:64],
                    lnr[64:65, 0:512], start=True, stop=True,
                    skip_group_check=True),
                    waits=[e_l2, st_free[5], e_onesf], inc="T")
                an = anb[:, hh * 512:(hh + 1) * 512]
                e_an = p.emit("vector", lambda eng, an=an, au=au:
                              eng.tensor_tensor(an[0:64, :], au[0:64, :],
                                                STB[5][0:64, :], AL.mult),
                              waits=[e_bc] + ([e_attn_w[-2]]
                                              if len(e_attn_w) >= 2 else []),
                              inc="V")
                st_free[5] = e_an
                st_["lnfree"] = ("T", e_bc[1])
                e_attn_w.append(
                    p.dma("sync",
                          attn_d.ap()[(hp * 2 + hh) * 64:
                                      (hp * 2 + hh + 1) * 64,
                                      qb * 512:(qb + 1) * 512],
                          an[0:64, :], waits=[e_an], stream=f"ATW{hh}"))
        eT = p.now("T")
        kp_ev[b] = eT


    pump_all()

    e_p4T = p.now("T")
    e_wo = None
    for c in range(DC):
        e_wo = p.dma("gpsimd", ar_(WO_O + c * 1024, 1024),
                     woT_d.ap()[c * 128:(c + 1) * 128, :],
                     waits=[e_p4T, e_oq_w[ITI - 1], e_oq_w[ITI - 2], e_wo],
                     stream="WOC")
    wd_last = [None, None]
    for c in range(ITI):
        wd_last[c % 2] = p.dma(
            "sync", AP(arena, WD_O + c * 1024,
                       [[ARW, 128], [128, 8], [1, 128]]),
            AP(wd_bf, c * 128 * 128,
               [[128, 128], [I * 128, 8], [1, 128]]),
            waits=[e_ag_wd, e_p4T, e_oq_w[ITI - 1], e_oq_w[ITI - 2],
                   wd_last[c % 2]],
            stream=f"WDL{c % 2}")
    e_wd = wd_last

    PF = [ps[:, i * 512:(i + 1) * 512] for i in range(8)]
    ch_ev = [NOEV, NOEV, NOEV, NOEV]
    e_out_w = []
    prev_adds = NOEV
    prev_outs = NOEV
    for tg in range(4):
        e_mm_last = [None] * 8
        for ci in range(DC + ITI):
            cb = ci % 4
            if ci < DC:
                src = attn_d.ap()[ci * 128:(ci + 1) * 128,
                                  tg * 512:(tg + 1) * 512]
                wsrc = [e_attn_w[-1], e_attn_w[-2], ch_ev[cb]]
                roff = WO_O + ci * 1024
            else:
                src = oq_d.ap()[(ci - DC) * 128:(ci - DC + 1) * 128,
                                tg * 512:(tg + 1) * 512]
                wsrc = [e_oq_w[ITI - 1], e_oq_w[ITI - 2], ch_ev[cb]]
                roff = WD_O + (ci - DC) * 1024
            el = p.dma("sync", chld[:, cb * 512:(cb + 1) * 512], src,
                       waits=wsrc, stream=f"CHL{cb}")
            for tti in range(4):
                for jb in range(2):
                    s8 = tti * 2 + jb
                    w = ([el, e_wo, prev_adds] + e_wd) if ci == 0 else [el]
                    e_mm_last[s8] = p.emit(
                        "tensor", lambda eng, s8=s8, cb=cb, tti=tti, jb=jb,
                        roff=roff, ci=ci: eng.matmul(
                            PF[s8],
                            chld[:, cb * 512 + tti * 128:
                                 cb * 512 + (tti + 1) * 128],
                            ar_(roff + jb * 512, 512),
                            start=(ci == 0), stop=(ci == DC + ITI - 1),
                            skip_group_check=True),
                        waits=w, inc="T")
            ch_ev[cb] = e_mm_last[7]
        adds = []
        for tti in range(4):
            tt = tg * 4 + tti
            xb2 = fin_x[:, tti * 1024:(tti + 1) * 1024]
            el = p.dma("sync", xb2, x_d.ap()[tt * 128:(tt + 1) * 128, :],
                       waits=[prev_adds, e_p4T, e_oq_w[ITI - 1],
                              e_oq_w[ITI - 2]], stream=f"FX{tti}")
            ot = fin_o[:, tti * 1024:(tti + 1) * 1024]
            tev = []
            for jb in range(2):
                s8 = tti * 2 + jb
                prev_out_slot = (e_out_w[-4] if len(e_out_w) >= 4 else None)
                e_ad = p.emit("vector", lambda eng, ot=ot, xb2=xb2, s8=s8,
                              jb=jb: eng.tensor_tensor(
                                  ot[:, jb * 512:(jb + 1) * 512],
                                  xb2[:, jb * 512:(jb + 1) * 512],
                                  PF[s8], AL.add),
                              waits=[el, e_mm_last[s8], prev_out_slot],
                              inc="V")
                tev.append(e_ad)
                adds.append(e_ad)
            e_out_w.append(p.dma("sync",
                                 out_d.ap()[tt * 128:(tt + 1) * 128, :], ot,
                                 waits=tev, stream=f"OUT{tti}"))
        prev_adds = adds[-1]
        prev_outs = None

    coll_dbg = {"kag_o": (kag_o, e_ag_k), "vag_o": (vag_o, e_ag_v),
                "wu_bf": (wu_bf, e_ag_wu), "wg_bf": (wg_bf, e_ag_wg),
                "wd_bf": (wd_bf, e_ag_wd),
                "scales": None}
    for nm in debug:
        if nm in coll_dbg and nm == "scales":
            dbg_t = nc.dram_tensor("dbg_scales", [128, 16], F32,
                                   kind="ExternalOutput")
            p.dma("sync", dbg_t.ap()[:, 0:8], brx[:, :],
                  waits=[e_out_w[-1]], stream="DBG")
            p.dma("sync", dbg_t.ap()[:, 8:16], stt[:, :],
                  waits=[e_out_w[-1]], stream="DBG")
        elif nm in coll_dbg:
            tsr, ev = coll_dbg[nm]
            dbg_t = nc.dram_tensor("dbg_" + nm, list(tsr.shape), tsr.dtype,
                                   kind="ExternalOutput")
            p.dma("sync", dbg_t.ap(), tsr.ap(), waits=[ev], stream="DBG")

    p.emit("gpsimd", lambda eng: eng.engine_nop(),
           waits=[p.now(k) for k in sorted(p.cnt)], inc="V")

    p.check()

    with nc.Block() as block:
        @block.tensor
        def _(eng):
            for f in p.ops["tensor"]:
                f(eng)

        @block.vector
        def _(eng):
            for f in p.ops["vector"]:
                f(eng)

        @block.scalar
        def _(eng):
            for f in p.ops["scalar"]:
                f(eng)

        @block.sync
        def _(eng):
            for f in p.ops["sync"]:
                f(eng)

        @block.gpsimd
        def _(eng):
            for f in p.ops["gpsimd"]:
                f(eng)

    stack.close()
    return nc


def _perm():
    pm = []
    for h in range(H):
        pm += [h * HD + 2 * j for j in range(HD // 2)]
        pm += [h * HD + 2 * j + 1 for j in range(HD // 2)]
    return np.array(pm)


def _tables(half):
    i = np.arange(32, dtype=np.float64)
    freqs = 10000.0 ** (-i / 32.0)
    p_ = np.arange(128, dtype=np.float64)
    tt = np.arange(NT, dtype=np.float64)
    pos = half * T + tt[None, :, None] * 128 + p_[:, None, None]
    ang = pos * freqs[None, None, :]
    return (np.cos(ang).reshape(128, NT * 32).astype(np.float32),
            np.sin(ang).reshape(128, NT * 32).astype(np.float32))


def kernel(L0_attn_x, L0_attn_Wq, L0_attn_bq, L0_attn_Wk, L0_attn_bk,
           L0_attn_Wv, L0_attn_bv, L0_attn_Wo, L0_ffn_w_gate, L0_ffn_w_up,
           L0_ffn_w_down):
    x = np.ascontiguousarray(np.asarray(L0_attn_x, dtype=np.float32))
    pm = _perm()
    wqT = np.ascontiguousarray(np.asarray(L0_attn_Wq, dtype=np.float32).T[:, pm])
    wkT = np.ascontiguousarray(np.asarray(L0_attn_Wk, dtype=np.float32).T[:, pm])
    wvT = np.ascontiguousarray(np.asarray(L0_attn_Wv, dtype=np.float32).T)
    woT = np.ascontiguousarray(np.asarray(L0_attn_Wo, dtype=np.float32).T)
    bq = np.asarray(L0_attn_bq, dtype=np.float32)[pm][None, :]
    bk = np.asarray(L0_attn_bk, dtype=np.float32)[pm][None, :]
    bv = np.asarray(L0_attn_bv, dtype=np.float32)[None, :]
    wu = np.ascontiguousarray(np.asarray(L0_ffn_w_up, dtype=np.float32))
    wg = np.ascontiguousarray(np.asarray(L0_ffn_w_gate, dtype=np.float32))
    wd = np.ascontiguousarray(np.asarray(L0_ffn_w_down, dtype=np.float32))

    nc = build_program(tuple(DEBUG))

    tables = [_tables(0), _tables(1)]
    in_maps = []
    for c in range(8):
        b, half = c // 2, c % 2
        cos_t, sin_t = tables[half]
        in_maps.append({
            "x": np.ascontiguousarray(x[b, half * T:(half + 1) * T, :]),
            "wqT": wqT, "wkT": wkT, "wvT": wvT, "woT": woT,
            "bq": bq, "bk": bk, "bv": bv,
            "wu_sh": np.ascontiguousarray(wu[c * (I // 8):(c + 1) * (I // 8)]),
            "wg_sh": np.ascontiguousarray(wg[c * (I // 8):(c + 1) * (I // 8)]),
            "wd_sh": np.ascontiguousarray(wd[c * (D // 8):(c + 1) * (D // 8)]),
            "cos_t": cos_t, "sin_t": sin_t,
            "ident": np.eye(128, dtype=np.float32),
        })
    res = run_bass_kernel_spmd(nc, in_maps, core_ids=list(range(8)))
    out = np.empty((B, S, D), dtype=np.float32)
    for c in range(8):
        b, half = c // 2, c % 2
        out[b, half * T:(half + 1) * T, :] = res.results[c]["out"]
    if DEBUG:
        kernel.debug_results = res.results
    return out


# version 21
# speedup vs baseline: 1.1008x; 1.1008x over previous
import numpy as np
import ml_dtypes
import concourse.bass as bass
import concourse.mybir as mybir
from concourse.bass_utils import run_bass_kernel_spmd

F32 = mybir.dt.float32
BF16 = mybir.dt.bfloat16
AL = mybir.AluOpType
AF = mybir.ActivationFunctionType
AX = mybir.AxisListType
AP = bass.AP

B, S, D, H, HD, I = 4, 4096, 1024, 16, 64, 3072
T = 2048
NT = T // 128
DC = D // 128
ITI = I // 128
QBN = T // 512
KCN = S // 128
NPAIR = H // 2
ISCALE = 1.0 / 32.0
MAGIC = 12582912.0
EPS = 1e-5
QA_ = 32767.0
KC_GROUPS = [(g * 3, min(3, KCN - g * 3)) for g in range((KCN + 2) // 3)]

DEBUG = []
NOEV = ("V", 0)
ENGS = ("tensor", "vector", "scalar", "sync", "gpsimd")


class P:

    def __init__(self, nc, sems, sem_factory):
        self.nc = nc
        self.sems = sems
        self.sem_factory = sem_factory
        self.ops = {e: [] for e in ENGS}
        self.meta = {e: [] for e in ENGS}
        self.cnt = {k: 0 for k in sems}
        self.hwm = {e: {k: 0 for k in sems} for e in ENGS}

    def stream(self, name):
        if name not in self.sems:
            self.sems[name] = self.sem_factory(name)
            self.cnt[name] = 0
            for e in ENGS:
                self.hwm[e][name] = 0
        return name

    def emit(self, e, fn, waits=(), inc=None, k=1):
        waits = [w for w in waits if w is not None and w[1] > 0]
        for (s, v) in waits:
            if self.hwm[e][s] < v:
                self.hwm[e][s] = v
                h = self.sems[s]
                self.ops[e].append(lambda eng, h=h, v=v: eng.wait_ge(h, v))
                self.meta[e].append((((s, v),), None, 0))
        ev = None
        if inc is not None:
            self.cnt[inc] += k
            ev = (inc, self.cnt[inc])
            h = self.sems[inc]
            self.ops[e].append(lambda eng, fn=fn, h=h, k=k: fn(eng).then_inc(h, k))
            self.meta[e].append(((), inc, k))
        else:
            self.ops[e].append(lambda eng, fn=fn: fn(eng))
            self.meta[e].append(((), None, 0))
        return ev

    def dma(self, q, out, in_, waits=(), transpose=False, slow=False,
            stream=None):
        assert stream is not None, "dma needs a stream name"
        sem = self.stream(stream)
        eng = "gpsimd" if q == "barrier" else q
        if transpose:
            fn = lambda eng_, out=out, in_=in_: eng_.dma_start_transpose(out, in_)
        elif slow:
            fn = lambda eng_, out=out, in_=in_: eng_.dma_start(
                out=out, in_=in_, allow_slow_non_contiguous=True)
        else:
            fn = lambda eng_, out=out, in_=in_: eng_.dma_start(out=out, in_=in_)
        return self.emit(eng, fn, waits, inc=sem, k=16)

    def coll(self, kind, op, groups, in_ap, out_ap, waits=(), stream="C"):
        fn = lambda eng: eng.collective_compute(
            kind, op, replica_groups=groups, ins=[in_ap], outs=[out_ap])
        return self.emit("gpsimd", fn, waits, inc=self.stream(stream))

    def now(self, sem):
        return (sem, self.cnt[sem])

    def check(self):
        pos = {e: 0 for e in ENGS}
        cnt = {k: 0 for k in self.cnt}
        progress = True
        while progress:
            progress = False
            for e in ENGS:
                while pos[e] < len(self.meta[e]):
                    waits, inc, k = self.meta[e][pos[e]]
                    if any(cnt[s] < v for (s, v) in waits):
                        break
                    if inc is not None:
                        cnt[inc] += k
                    pos[e] += 1
                    progress = True
        stuck = {e: (len(self.meta[e]) - pos[e]) for e in ENGS
                 if pos[e] < len(self.meta[e])}
        if stuck:
            det = {e: self.meta[e][pos[e]][0] for e in stuck}
            raise RuntimeError(
                f"deadlock in plan: remaining={stuck} blocked_on={det} cnt={cnt}")


def build_program(debug=()):
    nc = bass.Bass("TRN2", target_bir_lowering=False)

    COLL_OUTS = {"kag_o", "vag_o", "wu_bf", "wg_bf", "wd_bf"}

    def dram(name, shape, dt, kind="Internal"):
        k = ("ExternalOutput" if (name in debug and kind == "Internal"
                                  and name not in COLL_OUTS) else kind)
        return nc.dram_tensor(name, shape, dt, kind=k)

    x_d = nc.dram_tensor("x", [T, D], F32, kind="ExternalInput")
    wqT_d = nc.dram_tensor("wqT", [D, D], F32, kind="ExternalInput")
    wkT_d = nc.dram_tensor("wkT", [D, D], F32, kind="ExternalInput")
    wvT_d = nc.dram_tensor("wvT", [D, D], F32, kind="ExternalInput")
    woT_d = nc.dram_tensor("woT", [D, D], F32, kind="ExternalInput")
    bq_d = nc.dram_tensor("bq", [1, D], F32, kind="ExternalInput")
    bk_d = nc.dram_tensor("bk", [1, D], F32, kind="ExternalInput")
    bv_d = nc.dram_tensor("bv", [1, D], F32, kind="ExternalInput")
    wu_sh_d = nc.dram_tensor("wu_sh", [I // 8, D], F32, kind="ExternalInput")
    wg_sh_d = nc.dram_tensor("wg_sh", [I // 8, D], F32, kind="ExternalInput")
    wd_sh_d = nc.dram_tensor("wd_sh", [D // 8, I], F32, kind="ExternalInput")
    ident_d = nc.dram_tensor("ident", [128, 128], F32, kind="ExternalInput")
    cos_d = nc.dram_tensor("cos_t", [128, NT * 32], F32, kind="ExternalInput")
    sin_d = nc.dram_tensor("sin_t", [128, NT * 32], F32, kind="ExternalInput")
    out_d = nc.dram_tensor("out", [T, D], F32, kind="ExternalOutput")

    xn32_d = dram("xn32", [T, D], F32)
    xnbf_d = dram("xnbf", [T, D], BF16)
    xqbf_d = dram("xqbf", [T, D], BF16)
    q_tm_d = dram("q_tm", [T, D], BF16)
    kag_i = dram("kag_i", [T, D], BF16)
    kag_o = dram("kag_o", [S, D], BF16)
    vag_i = dram("vag_i", [T, H * 65], BF16)
    vag_o = dram("vag_o", [S, H * 65], BF16)
    wu_ag_i = dram("wu_ag_i", [D, I // 8], BF16)
    wu_bf = dram("wu_bf", [8 * D, I // 8], BF16)
    wg_ag_i = dram("wg_ag_i", [D, I // 8], BF16)
    wg_bf = dram("wg_bf", [8 * D, I // 8], BF16)
    wd_ag_i = dram("wd_ag_i", [I, D // 8], BF16)
    wd_bf = dram("wd_bf", [8 * I, D // 8], BF16)
    up_sp = dram("up_sp", [I, T], F32)
    gate_sp = dram("gate_sp", [I, T], F32)
    oq_d = dram("oq", [I, T], BF16)
    attn_d = dram("attn", [D, T], BF16)
    crs = [dram(f"crd{i}", [128, nv], F32)
           for i, nv in enumerate([1, 2, 1, 1])]
    aris = [dram(f"ar{i}i", [1, 2], F32) for i in range(4)]
    aros = [dram(f"ar{i}o", [1, 2], F32) for i in range(4)]

    from contextlib import ExitStack
    stack = ExitStack()

    def sbt(name, shape, dt):
        return stack.enter_context(nc.sbuf_tensor(name, shape, dt))

    XNFM_O = 0
    WQ_O = XNFM_O + 8 * 2048
    WK_O = WQ_O + 8 * 1024
    WV_O = WK_O + 8 * 1024
    WU_O = 0
    WG_O = WU_O + 8 * 3072
    XQ_O = WG_O + 8 * 3072
    P3_END = XQ_O + 8 * 2048
    KP_O = 0
    VP_O = KP_O + 2 * 4096
    QP_O = VP_O + 2 * 4160
    P4_END_E = QP_O + 2 * 2048
    WO_O = P4_END_E
    WD_O = WO_O + 8 * 1024
    P6_END = WD_O + 24 * 1024

    xst = sbt("xst", [128, 2 * 1024], F32)
    sc1 = sbt("sc1", [128, 1024], F32)
    xn32s = sbt("xn32s", [128, 2 * 1024], F32)
    xnbs = sbt("xnbs", [128, 2 * 1024], BF16)
    stt = sbt("stt", [128, 8], F32)
    lnst = sbt("lnst", [128, 8], F32)
    lnsum = sbt("lnsum", [128, 16], F32)
    lnsq = sbt("lnsq", [128, 16], F32)
    lnmean = sbt("lnmean", [128, 16], F32)
    lnvar = sbt("lnvar", [128, 16], F32)
    lnsqr = sbt("lnsqr", [128, 16], F32)
    lnrst = sbt("lnrst", [128, 16], F32)
    lntmp = sbt("lntmp", [128, 16], F32)
    cosb = sbt("cosb", [128, NT * 32], F32)
    sinb = sbt("sinb", [128, NT * 32], F32)
    bqs = sbt("bqs", [1, D], BF16)
    bks = sbt("bks", [1, D], BF16)
    bvs = sbt("bvs", [1, D], BF16)
    ones1 = sbt("ones1", [128, 128], BF16)
    ones1f = sbt("ones1f", [128, 64], F32)
    identb = sbt("identb", [128, 128], F32)
    rt1 = sbt("rt1", [128, 512], F32)
    rt2 = sbt("rt2", [128, 512], F32)
    dqs = sbt("dqs", [128, 3 * 96], F32)
    spst = sbt("spst", [128, 4 * 512], F32)
    mux = sbt("mux", [128, 8], F32)
    brx = sbt("brx", [128, 8], F32)
    rowb = sbt("rowb", [1, 256], F32)
    m0b = sbt("m0b", [1, 2], F32)
    ptb = sbt("ptb", [128, 4 * 1536], BF16)
    aub = sbt("aub", [128, 4 * 512], F32)
    lnr = sbt("lnr", [128, 512], F32)
    anb = sbt("anb", [128, 2 * 512], BF16)
    chld = sbt("chld", [128, 4 * 512], BF16)

    ARW = 64 * 1024
    arena = sbt("arena", [128, ARW], BF16)
    abase = nc.lookup_mloc(arena).addr
    ffw = nc.alloc_sbuf_tensor_at("ffw", [128, 10 * 2048], F32,
                                  offset=abase + 2 * P4_END_E)
    oqbf = nc.alloc_sbuf_tensor_at("oqbf", [128, 2 * 2048], BF16,
                                   offset=abase + 2 * P4_END_E + 8 * 2048 * 4)
    dqw = nc.alloc_sbuf_tensor_at("dqw", [128, 2 * 3072], F32,
                                  offset=abase + 81920)
    dqo = nc.alloc_sbuf_tensor_at("dqo", [128, 3072], BF16,
                                  offset=abase + 81920 + 24576)
    dqt = nc.alloc_sbuf_tensor_at("dqt", [128, 24 * 128], BF16, offset=abase)
    dqo = nc.alloc_sbuf_tensor_at("dqo", [128, 3072], F32,
                                  offset=abase + 6144)
    fin_x = nc.alloc_sbuf_tensor_at("fin_x", [128, 4 * 1024], F32,
                                    offset=abase)
    fin_o = nc.alloc_sbuf_tensor_at("fin_o", [128, 4 * 1024], F32,
                                    offset=abase + 16384)
    qko = nc.alloc_sbuf_tensor_at("qko", [128, 2 * 1024], BF16,
                                  offset=abase + 112640)
    qko2 = nc.alloc_sbuf_tensor_at("qko2", [128, 2 * 1024], BF16,
                                   offset=abase + 116736)
    v65 = nc.alloc_sbuf_tensor_at("v65", [128, 2 * 1040], BF16,
                                  offset=abase + 120832)

    def ar_(off, nelem, p0=0, np_=128):
        return AP(arena, p0 * ARW + off, [[ARW, np_], [1, nelem]])

    assert max(P3_END, P6_END) <= ARW

    def fw(slot):
        return ffw[:, slot * 2048:(slot + 1) * 2048]

    ps = stack.enter_context(nc.psum_tensor("ps", [128, 4096], F32))

    semnames = ["T", "V", "SC", "C"]
    sems = {n: stack.enter_context(nc.semaphore(n)) for n in semnames}
    p = P(nc, sems, lambda nm: stack.enter_context(nc.semaphore(nm)))

    bar_idx = [0]

    def barrier_max(src_ev, m_ap, nvals, extra=()):
        i = bar_idx[0]
        bar_idx[0] += 1
        cr, ari, aro = crs[i], aris[i], aros[i]
        e1 = p.dma("barrier", cr.ap()[:, 0:nvals], m_ap, waits=[src_ev],
                   stream="BAR")
        e2 = p.dma("barrier", rowb[0:1, 0:nvals * 128],
                   AP(cr, 0, [[1, 1], [1, nvals], [nvals, 128]]), waits=[e1],
                   slow=True, stream="BAR")
        e3 = p.emit("vector", lambda eng, nvals=nvals: eng.tensor_reduce(
            m0b[0:1, 0:nvals, None],
            AP(rowb, 0, [[256, 1], [128, nvals], [1, 128]]),
            AX.X, AL.max), waits=[e2], inc="V")
        e4 = p.dma("barrier", ari.ap()[:, 0:nvals], m0b[0:1, 0:nvals],
                   waits=[e3], stream="BAR")
        e5 = p.coll("AllReduce", AL.max, [list(range(8))],
                    ari.ap()[:, 0:nvals], aro.ap()[:, 0:nvals], waits=[e4],
                    stream=f"CB{i}")
        e6 = p.dma("barrier", brx[:, 6:6 + nvals],
                   aro.ap()[0, 0:nvals].partition_broadcast(128),
                   waits=[e5] + list(extra), stream="BAR")
        return e6

    e_id = p.dma("sync", identb[:, :], ident_d.ap(), stream="IDL")
    e_cos = p.dma("sync", cosb[:, :], cos_d.ap(), stream="COS")
    e_sin = p.dma("sync", sinb[:, :], sin_d.ap(), stream="SIN")
    e_ones = p.emit("vector", lambda eng: eng.memset(ones1[:, :], 1.0), inc="V")
    e_onesf = p.emit("vector", lambda eng: eng.memset(ones1f[:, :], 1.0), inc="V")
    wq_off = {0: WQ_O, 1: WK_O, 2: WV_O}
    cast_items = [(bqs[:, :], bq_d.ap()), (bks[:, :], bk_d.ap()),
                  (bvs[:, :], bv_d.ap())]
    for c in range(DC):
        for wi, wd_ in enumerate([wqT_d, wkT_d, wvT_d]):
            cast_items.append((ar_(wq_off[wi] + c * 1024, 1024),
                               wd_.ap()[c * 128:(c + 1) * 128, :]))
    CAST_STREAMS = ["OUT0", "OUT1", "OUT2", "OUT3", "FX0", "FX1", "FX2",
                    "FX3", "CHL0", "CHL1", "CHL2", "CHL3"]
    wc_last = {}
    for i, (dst, src) in enumerate(cast_items):
        st_ = CAST_STREAMS[i % 12]
        wc_last[st_] = p.dma("gpsimd", dst, src,
                             waits=[wc_last.get(st_, None)], stream=st_)
    e_wcast = tuple(wc_last.values())

    st_dq = {}

    def dequant(src_d, dst_ag_i, ntile, width, stream, start_wait=None):
        nblk = width // 32
        nch = width // 128
        last = start_wait
        ev6 = None
        ekv = []
        for t in range(ntile):
            b = t % 2
            wt = dqw[:, b * 3072:b * 3072 + width]
            el = p.dma("gpsimd", wt, src_d.ap()[t * 128:(t + 1) * 128, :],
                       waits=[last] if last else (), stream=stream)
            wv3 = AP(dqw, b * 3072, [[2 * 3072, 128], [32, nblk], [1, 32]])
            bm = dqs[:, 0:nblk]
            sv = dqs[:, 96:96 + nblk]
            rv = dqs[:, 192:192 + nblk]
            e1 = p.emit("vector", lambda eng, wv3=wv3, bm=bm: eng.tensor_reduce(
                bm[:, :, None], wv3, AX.X, AL.max, apply_absolute_value=True),
                waits=[el, ev6], inc="V")
            e2 = p.emit("vector", lambda eng, bm=bm, sv=sv: eng.tensor_scalar(
                sv, bm, 1.0 / 7.0, 1e-12, AL.mult, AL.max), waits=[e1], inc="V")
            e3 = p.emit("vector", lambda eng, sv=sv, rv=rv: eng.reciprocal(
                rv, sv), waits=[e2], inc="V")
            rbc = rv[:, :, None].broadcast_to([128, nblk, 32])
            sbc = sv[:, :, None].broadcast_to([128, nblk, 32])
            if width <= 2048:
                t1o = b * 3072 + width
            else:
                t1o = (1 - b) * 3072
            t1v = AP(dqw, t1o, [[2 * 3072, 128], [32, nblk], [1, 32]])
            e4 = p.emit("vector", lambda eng, t1v=t1v, wv3=wv3, rbc=rbc:
                        eng.tensor_tensor(t1v, wv3, rbc, AL.mult),
                        waits=[e3], inc="V")
            e5 = p.emit("vector", lambda eng, t1v=t1v: eng.tensor_scalar(
                t1v, t1v, MAGIC, MAGIC, AL.add, AL.subtract),
                waits=[e4], inc="V")
            ov3 = AP(dqo, 0, [[3072, 128], [32, nblk], [1, 32]])
            e6 = p.emit("vector", lambda eng, ov3=ov3, t1v=t1v, sbc=sbc:
                        eng.tensor_tensor(ov3, t1v, sbc, AL.mult),
                        waits=[e5], inc="V")
            ev6 = e6
            for c in range(nch):
                pslot = ps[:, 3072 + (c % 2) * 512:3072 + (c % 2) * 512 + 128]
                etr = p.emit("tensor", lambda eng, c=c, pslot=pslot:
                             eng.transpose(pslot, dqo[:, c * 128:(c + 1) * 128],
                                           identb[:, :]),
                             waits=[e6, e_id,
                                    st_dq.get(c % 2, None)], inc="T")
                eev = p.emit("vector", lambda eng, c=c, t=t, pslot=pslot,
                             ntile=ntile: eng.tensor_copy(
                                 dqt[:, (c * ntile + t) * 128:
                                     (c * ntile + t) * 128 + 128], pslot),
                             waits=[etr], inc="V")
                st_dq[c % 2] = eev
                ekv.append(eev)
            last = el
        ew = p.dma("gpsimd", AP(dst_ag_i, 0,
                              [[ntile * 128, 128], [128 * ntile * 128, nch],
                               [1, ntile * 128]]),
                   dqt[:, 0:nch * ntile * 128].rearrange(
                       "p (c x) -> p c x", c=nch),
                   waits=ekv, stream=stream)
        return ew, ev6

    e_dq_u, ev_u = dequant(wu_sh_d, wu_ag_i, 3, 1024, "DQ")
    e_dq_g, ev_g = dequant(wg_sh_d, wg_ag_i, 3, 1024, "DQ", start_wait=ev_u)
    e_dq_d, ev_d = dequant(wd_sh_d, wd_ag_i, 1, 3072, "DQ", start_wait=ev_g)
    grp8 = [list(range(8))]
    e_ag_wu = p.coll("AllGather", AL.bypass, grp8, wu_ag_i.ap(), wu_bf.ap(),
                     waits=[e_dq_u], stream="CWU")
    e_ag_wg = p.coll("AllGather", AL.bypass, grp8, wg_ag_i.ap(), wg_bf.ap(),
                     waits=[e_dq_g], stream="CWG")
    e_ag_wd = p.coll("AllGather", AL.bypass, grp8, wd_ag_i.ap(), wd_bf.ap(),
                     waits=[e_dq_d], stream="CWD")
    e_dqT_V = p.now("V")

    e_xn_w = [None] * NT
    lnA = {}
    for tt in range(NT):
        b = tt % 2
        xt = xst[:, b * 1024:(b + 1) * 1024]
        w = [lnA.get(tt - 2, None)]
        el = p.dma("sync", xt, x_d.ap()[tt * 128:(tt + 1) * 128, :], waits=w,
                   stream=f"X1{b}")
        e_s = p.emit("vector", lambda eng, xt=xt, tt=tt: eng.reduce_sum(
            lnsum[:, tt:tt + 1], xt, AX.X), waits=[el], inc="V")
        e_q = p.emit("scalar", lambda eng, xt=xt, tt=tt: eng.activation(
            sc1[:, :], xt, AF.Square, accum_out=lnsq[:, tt:tt + 1]),
            waits=[el], inc="SC")
        lnA[tt] = ("V", max(e_s[1], 0))
        lnA[(tt, "sc")] = e_q
    e_rng = [None] * 4
    for g4 in range(4):
        c0, c1 = g4 * 4, g4 * 4 + 4
        wA = [lnA[g4 * 4 + 3], lnA[(g4 * 4 + 3, "sc")],
              lnA[(g4 * 4 + 2, "sc")], lnA[(g4 * 4 + 1, "sc")],
              lnA[(g4 * 4, "sc")]]
        e1 = p.emit("vector", lambda eng, c0=c0, c1=c1: eng.tensor_scalar(
            lnmean[:, c0:c1], lnsum[:, c0:c1], 1.0 / D, None, AL.mult),
            waits=wA, inc="V")
        e2 = p.emit("vector", lambda eng, c0=c0, c1=c1: eng.tensor_scalar(
            lnvar[:, c0:c1], lnsq[:, c0:c1], 1.0 / D, None, AL.mult),
            inc="V")
        e3 = p.emit("vector", lambda eng, c0=c0, c1=c1: eng.tensor_tensor(
            lntmp[:, c0:c1], lnmean[:, c0:c1], lnmean[:, c0:c1], AL.mult),
            waits=[e1], inc="V")
        e4 = p.emit("vector", lambda eng, c0=c0, c1=c1: eng.tensor_tensor(
            lnvar[:, c0:c1], lnvar[:, c0:c1], lntmp[:, c0:c1], AL.subtract),
            waits=[e2, e3], inc="V")
        e5 = p.emit("vector", lambda eng, c0=c0, c1=c1: eng.tensor_scalar(
            lnvar[:, c0:c1], lnvar[:, c0:c1], EPS, None, AL.add),
            waits=[e4], inc="V")
        e_sq = p.emit("scalar", lambda eng, c0=c0, c1=c1: eng.activation(
            lnsqr[:, c0:c1], lnvar[:, c0:c1], AF.Sqrt), waits=[e5], inc="SC")
        e_rn = p.emit("vector", lambda eng, c0=c0, c1=c1: eng.reciprocal(
            lnrst[:, c0:c1], lnsqr[:, c0:c1]), waits=[e_sq], inc="V")
        for _ in range(2):
            en1 = p.emit("vector", lambda eng, c0=c0, c1=c1: eng.tensor_tensor(
                lntmp[:, c0:c1], lnrst[:, c0:c1], lnrst[:, c0:c1], AL.mult),
                waits=[e_rn], inc="V")
            en2 = p.emit("vector", lambda eng, c0=c0, c1=c1: eng.tensor_tensor(
                lntmp[:, c0:c1], lntmp[:, c0:c1], lnvar[:, c0:c1], AL.mult),
                waits=[en1], inc="V")
            en3 = p.emit("vector", lambda eng, c0=c0, c1=c1: eng.tensor_scalar(
                lntmp[:, c0:c1], lntmp[:, c0:c1], -0.5, 1.5, AL.mult, AL.add),
                waits=[en2], inc="V")
            e_rn = p.emit("vector", lambda eng, c0=c0, c1=c1: eng.tensor_tensor(
                lnrst[:, c0:c1], lnrst[:, c0:c1], lntmp[:, c0:c1], AL.mult),
                waits=[en3], inc="V")
        e_rng[g4] = e_rn

    e_mx = None
    for tt in range(NT):
        b = tt % 2
        xt = xst[:, b * 1024:(b + 1) * 1024]
        w = [e_xn_w[tt - 2][0], e_xn_w[tt - 2][1]] if tt >= 2 else \
            [lnA[NT - 1], lnA[NT - 2], lnA[(NT - 1, "sc")],
             lnA[(NT - 2, "sc")]]
        el = p.dma("sync", xt, x_d.ap()[tt * 128:(tt + 1) * 128, :], waits=w,
                   stream=f"X1{b}")
        xnt = xn32s[:, b * 1024:(b + 1) * 1024]
        xbt = xnbs[:, b * 1024:(b + 1) * 1024]
        wprev = [e_xn_w[tt - 2][1], e_xn_w[tt - 2][0]] if tt >= 2 else []
        e_xn = p.emit("vector", lambda eng, xt=xt, xnt=xnt, tt=tt:
                      eng.tensor_scalar(xnt, xt, lnmean[:, tt:tt + 1],
                                        lnrst[:, tt:tt + 1], AL.subtract,
                                        AL.mult),
                      waits=[e_rng[tt // 4], el] + wprev, inc="V")
        e_xb = p.emit("vector", lambda eng, xnt=xnt, xbt=xbt: eng.tensor_copy(
            xbt, xnt), waits=[e_xn], inc="V")
        mc = 0 if tt == 0 else 1
        e_m2 = p.emit("vector", lambda eng, xnt=xnt, mc=mc: eng.tensor_reduce(
            mux[:, mc:mc + 1], xnt, AX.X, AL.max, apply_absolute_value=True),
            waits=[e_xn, e_mx], inc="V")
        if tt > 0:
            e_m2 = p.emit("vector", lambda eng: eng.tensor_tensor(
                mux[:, 0:1], mux[:, 0:1], mux[:, 1:2], AL.max),
                waits=[e_m2], inc="V")
        e_mx = e_m2
        ew1 = p.dma("sync", xn32_d.ap()[tt * 128:(tt + 1) * 128, :], xnt,
                    waits=[e_xn], stream=f"XNW{b}")
        ew2 = p.dma("sync", xnbf_d.ap()[tt * 128:(tt + 1) * 128, :], xbt,
                    waits=[e_xb], stream=f"XBW{b}")
        e_xn_w[tt] = (ew1, ew2)
    e_ln_SC = p.now("SC")

    e_b1 = barrier_max(e_mx, mux[:, 0:1], 1)
    e_s1 = p.emit("vector", lambda eng: eng.tensor_scalar(
        brx[:, 0:1], brx[:, 6:7], 1.0 / QA_, 1e-12, AL.mult, AL.max),
        waits=[e_b1], inc="V")
    e_rs1 = p.emit("vector", lambda eng: eng.reciprocal(
        brx[:, 1:2], brx[:, 0:1]), waits=[e_s1], inc="V")



    e_xq_w = []

    def qdq_tile(tt):
        b = tt % 2
        xt = xst[:, b * 1024:(b + 1) * 1024]
        w = [e_xn_w[tt][0]]
        w += ([e_xq_w[tt - 2]] if tt >= 2 else
              [e_xn_w[NT - 1][0], e_xn_w[NT - 2][0],
               e_xn_w[NT - 1][1], e_xn_w[NT - 2][1]])
        el = p.dma("sync", xt, xn32_d.ap()[tt * 128:(tt + 1) * 128, :],
                   waits=w, stream=f"XQL{tt % 2}")
        e1 = p.emit("vector", lambda eng, xt=xt: eng.tensor_scalar(
            sc1[:, :], xt, brx[:, 1:2], MAGIC, AL.mult, AL.add),
            waits=[el, e_rs1, e_ln_SC], inc="V")
        xbt = xnbs[:, b * 1024:(b + 1) * 1024]
        e2 = p.emit("vector", lambda eng, xbt=xbt: eng.tensor_scalar(
            xbt, sc1[:, :], MAGIC, brx[:, 0:1], AL.subtract, AL.mult),
            waits=[e1], inc="V")
        e_xq_w.append(p.dma("sync", xqbf_d.ap()[tt * 128:(tt + 1) * 128, :],
                            xbt, waits=[e2], stream=f"XQW{tt % 2}"))



    e_xnfm_g = [None] * 4
    for g in range(4):
        for c in range(DC):
            e_xnfm_g[g] = p.dma(
                "sync", ar_(XNFM_O + c * 2048 + g * 512, 512),
                xnbf_d.ap()[g * 512:(g + 1) * 512, c * 128:(c + 1) * 128],
                waits=[e_xn_w[4 * g + 3][1], e_xn_w[4 * g + 2][1],
                       e_xnfm_g[g], e_dq_u, e_dq_g, e_dq_d, ev_d],
                transpose=True, stream=f"XNF{g}")
    e_xnfm = e_xnfm_g[3]
    e_v65i = p.emit("vector", lambda eng: eng.memset(
        AP(v65, 64, [[2 * 1040, 128], [1040, 2], [65, 16]]), 1.0), inc="V")

    PSLOT4 = [ps[:, s * 1024:(s + 1) * 1024] for s in range(4)]
    slot_ev = [NOEV, NOEV, NOEV, e_dqT_V]
    e_q_w, e_k_w, e_v_w = [], [], []
    slot_i = [0]

    def proj(woff, bsb, tt, extra):
        s = slot_i[0] % 4
        slot_i[0] += 1
        pslot = PSLOT4[s]
        for c in range(DC):
            for ob in range(2):
                w = ([slot_ev[s]] + extra) if (c == 0 and ob == 0) else ()
                p.emit("tensor", lambda eng, pslot=pslot, c=c, ob=ob, tt=tt,
                       woff=woff: eng.matmul(
                           pslot[:, ob * 512:(ob + 1) * 512],
                           AP(arena, XNFM_O + c * 2048 + tt * 128,
                              [[ARW, 128], [1, 128]]),
                           AP(arena, woff + c * 1024 + ob * 512,
                              [[ARW, 128], [1, 512]]),
                           start=(c == 0), stop=False, skip_group_check=True),
                       waits=w, inc="T")
        ev = None
        for ob in range(2):
            ev = p.emit("tensor", lambda eng, pslot=pslot, ob=ob, bsb=bsb:
                        eng.matmul(pslot[:, ob * 512:(ob + 1) * 512],
                                   ones1[0:1, 0:128],
                                   bsb[0:1, ob * 512:(ob + 1) * 512],
                                   start=False, stop=True,
                                   skip_group_check=True), inc="T")
        return s, ev

    def rope_evac(s, tt, osb, obase, waits):
        Ev = AP(ps, s * 1024, [[4096, 128], [64, 16], [1, 32]])
        Ov = AP(ps, s * 1024 + 32, [[4096, 128], [64, 16], [1, 32]])
        OE = AP(osb, obase, [[2 * 1024, 128], [64, 16], [1, 32]])
        OO = AP(osb, obase + 32, [[2 * 1024, 128], [64, 16], [1, 32]])
        cv = AP(cosb, tt * 32, [[NT * 32, 128], [0, 16], [1, 32]])
        sv = AP(sinb, tt * 32, [[NT * 32, 128], [0, 16], [1, 32]])
        r1 = AP(rt1, 0, [[512, 128], [32, 16], [1, 32]])
        r2 = AP(rt2, 0, [[512, 128], [32, 16], [1, 32]])
        p.emit("vector", lambda eng: eng.tensor_tensor(r1, Ev, cv, AL.mult),
               waits=list(waits) + [e_cos, e_sin], inc="V")
        p.emit("vector", lambda eng: eng.tensor_tensor(r2, Ov, sv, AL.mult),
               inc="V")
        p.emit("vector", lambda eng: eng.tensor_tensor(OE, r1, r2, AL.subtract),
               inc="V")
        p.emit("vector", lambda eng: eng.tensor_tensor(r1, Ev, sv, AL.mult),
               inc="V")
        p.emit("vector", lambda eng: eng.tensor_tensor(r2, Ov, cv, AL.mult),
               inc="V")
        return p.emit("vector", lambda eng: eng.tensor_tensor(
            OO, r1, r2, AL.add), inc="V")

    for tt in range(NT):
        b = tt % 2
        s, e_mm = proj(WQ_O, bqs, tt,
                       [e_xnfm_g[tt // 4], e_ones] + list(e_wcast))
        e_ro = rope_evac(s, tt, qko, b * 1024,
                         [e_mm] + ([e_q_w[tt - 2]] if tt >= 2 else []))
        slot_ev[s] = e_ro
        e_q_w.append(p.dma("sync", q_tm_d.ap()[tt * 128:(tt + 1) * 128, :],
                           qko[:, b * 1024:(b + 1) * 1024], waits=[e_ro],
                           stream=f"QW{tt % 2}"))
        s, e_mm = proj(WK_O, bks, tt, [])
        e_ro = rope_evac(s, tt, qko2, b * 1024,
                         [e_mm] + ([e_k_w[tt - 2]] if tt >= 2 else []))
        slot_ev[s] = e_ro
        e_k_w.append(p.dma("sync", kag_i.ap()[tt * 128:(tt + 1) * 128, :],
                           qko2[:, b * 1024:(b + 1) * 1024], waits=[e_ro],
                           stream=f"KW{tt % 2}"))
        s, e_mm = proj(WV_O, bvs, tt, [])
        pv_in = AP(ps, s * 1024, [[4096, 128], [64, 16], [1, 64]])
        vv_out = AP(v65, b * 1040, [[2 * 1040, 128], [65, 16], [1, 64]])
        e_cv = p.emit("vector", lambda eng, pv_in=pv_in, vv_out=vv_out:
                      eng.tensor_copy(vv_out, pv_in),
                      waits=[e_mm, e_v65i] +
                      ([e_v_w[tt - 2]] if tt >= 2 else []), inc="V")
        slot_ev[s] = e_cv
        e_v_w.append(p.dma("sync", vag_i.ap()[tt * 128:(tt + 1) * 128, :],
                           v65[:, b * 1040:(b + 1) * 1040], waits=[e_cv],
                           stream=f"VW{tt % 2}"))
        qdq_tile(tt)

    grpp = [[0, 1], [2, 3], [4, 5], [6, 7]]
    e_ag_k = p.coll("AllGather", AL.bypass, grpp, kag_i.ap(), kag_o.ap(),
                    waits=[e_k_w[NT - 1], e_k_w[NT - 2]], stream="CK")
    e_ag_v = p.coll("AllGather", AL.bypass, grpp, vag_i.ap(), vag_o.ap(),
                    waits=[e_v_w[NT - 1], e_v_w[NT - 2]], stream="CV")

    e_p2T = p.now("T")
    e_p2V = [e for e in slot_ev]
    wul_gate = [e_ag_wu, e_ag_wg, e_p2T] + e_xnfm_g + e_p2V
    xqf_gate = [e_dq_d, ev_d,
                e_v_w[NT - 1], e_v_w[NT - 2], e_q_w[NT - 1], e_q_w[NT - 2],
                e_k_w[NT - 1], e_k_w[NT - 2]]
    WUL_STREAMS = [("WUL0", None), ("WUL1", None),
                   ("QW0", e_q_w[NT - 2]), ("QW1", e_q_w[NT - 1]),
                   ("KW0", e_k_w[NT - 2]), ("KW1", e_k_w[NT - 1]),
                   ("VW0", e_v_w[NT - 2]), ("VW1", e_v_w[NT - 1]),
                   ("WUL2", None), ("WUL3", None),
                   ("XQW0", e_xq_w[NT - 2]), ("XQW1", e_xq_w[NT - 1]),
                   ("XNW0", e_xn_w[NT - 2][0]), ("XNW1", e_xn_w[NT - 1][0]),
                   ("XNF0", e_xnfm_g[0]), ("XNF1", e_xnfm_g[1])]
    e_wu_only, e_wg_only = [], []
    e_wT_l, e_xqfm_l = [], []
    for c in range(DC):
        st0, ep0 = WUL_STREAMS[c]
        st1, ep1 = WUL_STREAMS[8 + c]
        e1_ = p.dma("sync", AP(arena, WU_O + c * 3072,
                               [[ARW, 128], [384, 8], [1, 384]]),
                    AP(wu_bf, c * 128 * 384,
                       [[384, 128], [D * 384, 8], [1, 384]]),
                    waits=wul_gate + [ep0], stream=st0)
        e_wu_only.append(e1_)
        e_wT_l.append(e1_)
        e2_ = p.dma("sync", AP(arena, WG_O + c * 3072,
                               [[ARW, 128], [384, 8], [1, 384]]),
                    AP(wg_bf, c * 128 * 384,
                       [[384, 128], [D * 384, 8], [1, 384]]),
                    waits=wul_gate + [ep1], stream=st1)
        e_wg_only.append(e2_)
        e_wT_l.append(e2_)
    wul_last = e_wu_only[0:2] + e_wg_only[0:2] + [None, None]
    e_xqf_g = [None] * 4
    for g in range(4):
        for c in range(DC):
            e_xqf_g[g] = p.dma(
                "sync", ar_(XQ_O + c * 2048 + g * 512, 512),
                xqbf_d.ap()[g * 512:(g + 1) * 512, c * 128:(c + 1) * 128],
                waits=xqf_gate + [e_xq_w[4 * g + 3], e_xq_w[4 * g + 2],
                                 e_xqf_g[g]],
                transpose=True, stream=f"XQF{g}")
    e_wul = [e for e in wul_last[0:4]]
    e_wT = None
    e_xqfm = None

    if "xqfm" in debug:
        dbg_xqfm = nc.dram_tensor("dbg_xqfm", [128, 8 * 2048], BF16,
                                  kind="ExternalOutput")
        dbg_wuT = nc.dram_tensor("dbg_wuT", [128, 8 * 3072], BF16,
                                 kind="ExternalOutput")
        p.dma("sync", dbg_xqfm.ap(), ar_(XQ_O, 8 * 2048),
              waits=e_wul + e_xqf_g, stream="DBG")
        p.dma("sync", dbg_wuT.ap(), ar_(WU_O, 8 * 3072), waits=e_wul,
              stream="DBG")

    PS8 = [ps[:, s * 512:(s + 1) * 512] for s in range(8)]
    ps8_ev = [NOEV] * 8
    sp_ev = [NOEV] * 4
    si = [0, 0]
    mux_last = {}
    spill_last = {}
    for wi, (woff, spd, muxc) in enumerate([(WU_O, up_sp, 1),
                                            (WG_O, gate_sp, 2)]):
        e_m = None
        e_sp = None
        for it in range(ITI):
            for tb in range(QBN):
                s8 = si[0] % 8
                si[0] += 1
                e_mm = None
                for c in range(DC):
                    wgt = e_wu_only if woff == WU_O else e_wg_only
                    w = ([ps8_ev[s8], e_xqf_g[tb]] + wgt) if c == 0 else ()
                    e_mm = p.emit("tensor", lambda eng, s8=s8, c=c, it=it,
                                  tb=tb, woff=woff: eng.matmul(
                                      PS8[s8],
                                      AP(arena, woff + c * 3072 + it * 128,
                                         [[ARW, 128], [1, 128]]),
                                      AP(arena, XQ_O + c * 2048 + tb * 512,
                                         [[ARW, 128], [1, 512]]),
                                      start=(c == 0), stop=(c == DC - 1),
                                      skip_group_check=True),
                                  waits=w, inc="T")
                ss = si[1] % 4
                si[1] += 1
                stg = spst[:, ss * 512:(ss + 1) * 512]
                e_ev = p.emit("vector", lambda eng, stg=stg, s8=s8:
                              eng.tensor_copy(stg, PS8[s8]),
                              waits=[e_mm, sp_ev[ss]], inc="V")
                if it == 0 and tb == 0:
                    e_m = p.emit("vector", lambda eng, s8=s8, muxc=muxc:
                                 eng.tensor_reduce(
                                     mux[:, muxc:muxc + 1], PS8[s8], AX.X,
                                     AL.max, apply_absolute_value=True),
                                 waits=[e_mm], inc="V")
                else:
                    e_m = p.emit("vector", lambda eng, s8=s8: eng.tensor_reduce(
                        mux[:, 3:4], PS8[s8], AX.X, AL.max,
                        apply_absolute_value=True), waits=[e_mm, e_m], inc="V")
                    e_m = p.emit("vector", lambda eng, muxc=muxc:
                                 eng.tensor_tensor(mux[:, muxc:muxc + 1],
                                                   mux[:, muxc:muxc + 1],
                                                   mux[:, 3:4], AL.max),
                                 waits=[e_m], inc="V")
                ps8_ev[s8] = ("V", max(e_ev[1], e_m[1]))
                e_sp = p.dma("scalar",
                             spd.ap()[it * 128:(it + 1) * 128,
                                      tb * 512:(tb + 1) * 512],
                             stg, waits=[e_ev], stream=f"SP{ss}")
                sp_ev[ss] = e_sp
        mux_last[wi] = e_m
        spill_last[wi] = [e for e in sp_ev]

    e_cp = p.emit("vector", lambda eng: eng.tensor_copy(
        mux[:, 4:5], mux[:, 1:2]), waits=[mux_last[0], mux_last[1]], inc="V")
    e_cp = p.emit("vector", lambda eng: eng.tensor_copy(
        mux[:, 5:6], mux[:, 2:3]), waits=[e_cp], inc="V")
    e_b2 = barrier_max(e_cp, mux[:, 4:6], 2, extra=[e_s1])
    e_su = p.emit("vector", lambda eng: eng.tensor_scalar(
        brx[:, 2:4], brx[:, 6:8], 1.0 / QA_, 1e-12, AL.mult, AL.max),
        waits=[e_b2], inc="V")
    e_rsu = p.emit("vector", lambda eng: eng.reciprocal(
        brx[:, 4:6], brx[:, 2:4]), waits=[e_su], inc="V")
    e_mbg = p.emit("vector", lambda eng: eng.tensor_scalar(
        lnst[:, 6:7], brx[:, 3:4], -MAGIC, None, AL.mult),
        waits=[e_su], inc="V")

    e_dbg_up = e_dbg_gate = None
    if "up_raw" in debug:
        t_ = nc.dram_tensor("dbg_up_raw", [I, T], F32, kind="ExternalOutput")
        e_dbg_up = p.dma("sync", t_.ap(), up_sp.ap(),
                         waits=spill_last[0], stream="DBG")
    if "gate_raw" in debug:
        t_ = nc.dram_tensor("dbg_gate_raw", [I, T], F32, kind="ExternalOutput")
        e_dbg_gate = p.dma("sync", t_.ap(), gate_sp.ap(),
                           waits=spill_last[1], stream="DBG")
    e_g2_w = [None] * ITI
    e_sig = [None] * ITI
    e_gq = [None] * ITI
    gq_read = [None] * ITI
    sig_read = [None] * ITI
    st_ = {}

    s1_state = {}

    def stage1_load(it):
        inb = fw(it % 2)
        w = [s1_state.get(("t1e", it - 2), None)]
        el = p.dma("gpsimd", inb, gate_sp.ap()[it * 128:(it + 1) * 128, :],
                   waits=spill_last[1] + [e_dbg_gate] + w,
                   stream=f"S1L{it % 2}")
        t1 = fw(2 + it % 2)
        e1 = p.emit("vector", lambda eng, inb=inb, t1=t1: eng.tensor_scalar(
            t1, inb, brx[:, 5:6], MAGIC, AL.mult, AL.add),
            waits=[el, e_rsu, e_mbg, s1_state.get(("gq", it - 2), None),
                   s1_state.get(("sg", it - 2), None)], inc="V")
        s1_state[("t1e", it)] = e1

    def stage1_sig(it):
        t1 = fw(2 + it % 2)
        sg = fw(6 + it % 2)
        e = p.emit("scalar", lambda eng, t1=t1, sg=sg: eng.activation(
            sg, t1, AF.Sigmoid, bias=lnst[:, 6:7], scale=brx[:, 3:4]),
            waits=[s1_state[("t1e", it)],
                   s1_state.get(("sgq", it - 2), None)], inc="SC")
        e_sig[it] = e

    def stage1_gq(it):
        t1 = fw(2 + it % 2)
        gq = fw(4 + it % 2)
        e = p.emit("vector", lambda eng, t1=t1, gq=gq: eng.tensor_scalar(
            gq, t1, MAGIC, brx[:, 3:4], AL.subtract, AL.mult),
            waits=[s1_state[("t1e", it)],
                   s1_state.get(("g2", it - 2), None)], inc="V")
        s1_state[("gq", it)] = e
        e_gq[it] = e

    def stage1_tail(it):
        gq = fw(4 + it % 2)
        sg = fw(6 + it % 2)
        ob = fw(8 + it % 2)
        e3 = p.emit("vector", lambda eng, sg=sg: eng.tensor_scalar(
            sg, sg, 65536.0, MAGIC, AL.mult, AL.add),
            waits=[e_sig[it]], inc="V")
        e4 = p.emit("vector", lambda eng, sg=sg: eng.tensor_scalar(
            sg, sg, MAGIC, 65535.0, AL.subtract, AL.min), waits=[e3], inc="V")
        wo = [e_g2_w[it - 2]] if it >= 2 else []
        e5 = p.emit("vector", lambda eng, ob=ob, gq=gq, sg=sg:
                    eng.scalar_tensor_tensor(ob, gq, 1.0 / 65536.0, sg,
                                             AL.mult, AL.mult),
                    waits=[e4] + wo, inc="V")
        s1_state[("g2", it)] = e5
        s1_state[("sg", it)] = e5
        s1_state[("sgq", it)] = e5
        if it == 0:
            e_m = p.emit("vector", lambda eng, ob=ob: eng.tensor_reduce(
                mux[:, 6:7], ob, AX.X, AL.max, apply_absolute_value=True),
                waits=[e5], inc="V")
        else:
            e_m = p.emit("vector", lambda eng, ob=ob: eng.tensor_reduce(
                mux[:, 7:8], ob, AX.X, AL.max, apply_absolute_value=True),
                waits=[e5, st_["m1"]], inc="V")
            e_m = p.emit("vector", lambda eng: eng.tensor_tensor(
                mux[:, 6:7], mux[:, 6:7], mux[:, 7:8], AL.max),
                waits=[e_m], inc="V")
        st_["m1"] = e_m
        e_g2_w[it] = p.dma("gpsimd", gate_sp.ap()[it * 128:(it + 1) * 128, :],
                           ob, waits=[e5, e_m], stream=f"G2W{it % 2}")

    def barrier3():
        e_b3 = barrier_max(st_["m1"], mux[:, 6:7], 1, extra=[e_su])
        e_s3 = p.emit("vector", lambda eng: eng.tensor_scalar(
            stt[:, 3:4], brx[:, 6:7], 1.0 / QA_, 1e-12, AL.mult, AL.max),
            waits=[e_b3], inc="V")
        st_["s2ev"] = e_s3
        st_["rs2ev"] = p.emit("vector", lambda eng: eng.reciprocal(
            stt[:, 5:6], stt[:, 3:4]), waits=[e_s3], inc="V")

    e_o_w = [None] * ITI
    e_dbg_g2 = None

    def stage2_load(it):
        g2b = fw(it % 2)
        upb = fw(2 + it % 2)
        w1 = [e_o_w[it - 2]] if it >= 2 else []
        el1 = p.dma("gpsimd", g2b, gate_sp.ap()[it * 128:(it + 1) * 128, :],
                    waits=[e_g2_w[ITI - 1], e_g2_w[ITI - 2], st_["rs2ev"],
                           st_.get("dbg_g2", None)] + w1,
                    stream=f"S2A{it % 2}")
        el2 = p.dma("gpsimd", upb, up_sp.ap()[it * 128:(it + 1) * 128, :],
                    waits=spill_last[0] + [e_g2_w[ITI - 1], e_g2_w[ITI - 2],
                                           e_dbg_up] + w1,
                    stream=f"S2B{it % 2}")
        s1_state[("s2", it)] = (el1, el2, None)

    def stage2_qdq(it, k):
        g2b = fw(it % 2)
        upb = fw(2 + it % 2)
        g2q = fw(4 + it % 2)
        upq = fw(6 + it % 2)
        el1, el2, prev = s1_state[("s2", it)]
        src, dst, rss, ss_ = ((g2b, g2q, stt[:, 5:6], stt[:, 3:4]) if k < 2
                              else (upb, upq, brx[:, 4:5], brx[:, 2:3]))
        half = k % 2
        e1 = p.emit("vector", lambda eng, src=src, rss=rss, half=half:
                    eng.tensor_scalar(
                        sc1[:, :], src[:, half * 1024:(half + 1) * 1024],
                        rss, MAGIC, AL.mult, AL.add),
                    waits=[el1, el2, e_rsu, prev], inc="V")
        e2 = p.emit("vector", lambda eng, dst=dst, ss_=ss_, half=half:
                    eng.tensor_scalar(
                        dst[:, half * 1024:(half + 1) * 1024], sc1[:, :],
                        MAGIC, ss_, AL.subtract, AL.mult),
                    waits=[e1], inc="V")
        s1_state[("s2", it)] = (el1, el2, e2)

    def stage2_mul(it):
        g2q = fw(4 + it % 2)
        upq = fw(6 + it % 2)
        ob = fw(8 + it % 2)
        _, _, prev = s1_state[("s2", it)]
        wo = [e_o_w[it - 2]] if it >= 2 else []
        e5 = p.emit("vector", lambda eng, ob=ob, g2q=g2q, upq=upq:
                    eng.tensor_tensor(ob, g2q, upq, AL.mult),
                    waits=[prev] + wo, inc="V")
        s1_state[("s2m", it)] = e5

    def stage2_max(it):
        ob = fw(8 + it % 2)
        e5 = s1_state[("s2m", it)]
        if it == 0:
            e_m = p.emit("vector", lambda eng, ob=ob: eng.tensor_reduce(
                mux[:, 6:7], ob, AX.X, AL.max, apply_absolute_value=True),
                waits=[e5, st_["b3r"]], inc="V")
        else:
            e_m = p.emit("vector", lambda eng, ob=ob: eng.tensor_reduce(
                mux[:, 7:8], ob, AX.X, AL.max, apply_absolute_value=True),
                waits=[e5, st_["m2"]], inc="V")
            e_m = p.emit("vector", lambda eng: eng.tensor_tensor(
                mux[:, 6:7], mux[:, 6:7], mux[:, 7:8], AL.max),
                waits=[e_m], inc="V")
        st_["m2"] = e_m
        e_o_w[it] = p.dma("gpsimd", up_sp.ap()[it * 128:(it + 1) * 128, :],
                          ob, waits=[e5, e_m], stream=f"OW2{it % 2}")

    def barrier4():
        e_b4 = barrier_max(st_["m2"], mux[:, 6:7], 1, extra=[st_["s2ev"]])
        e_s4 = p.emit("vector", lambda eng: eng.tensor_scalar(
            stt[:, 6:7], brx[:, 6:7], 1.0 / QA_, 1e-12, AL.mult, AL.max),
            waits=[e_b4], inc="V")
        st_["s4ev"] = e_s4
        st_["rs4ev"] = p.emit("vector", lambda eng: eng.reciprocal(
            stt[:, 7:8], stt[:, 6:7]), waits=[e_s4], inc="V")

    e_oq_w = [None] * ITI

    def stage3_load(it):
        ob = fw(it % 2)
        w = [e_oq_w[it - 2]] if it >= 2 else []
        el = p.dma("gpsimd", ob, up_sp.ap()[it * 128:(it + 1) * 128, :],
                   waits=[e_o_w[ITI - 1], e_o_w[ITI - 2], st_["rs4ev"]] + w,
                   stream=f"S3L{it % 2}")
        s1_state[("s3", it)] = (el, None, [])

    def stage3_half(it, half):
        ob = fw(it % 2)
        el, prev, evs = s1_state[("s3", it)]
        e1 = p.emit("vector", lambda eng, ob=ob, half=half:
                    eng.tensor_scalar(
                        sc1[:, :], ob[:, half * 1024:(half + 1) * 1024],
                        stt[:, 7:8], MAGIC, AL.mult, AL.add),
                    waits=[el, prev], inc="V")
        prev = p.emit("vector", lambda eng, it=it, half=half:
                      eng.tensor_scalar(
                          oqbf[:, (it % 2) * 2048 + half * 1024:
                               (it % 2) * 2048 + (half + 1) * 1024],
                          sc1[:, :], MAGIC, stt[:, 6:7], AL.subtract,
                          AL.mult), waits=[e1], inc="V")
        evs = evs + [prev]
        s1_state[("s3", it)] = (el, prev, evs)
        if half == 1:
            e_oq_w[it] = p.dma("gpsimd",
                               oq_d.ap()[it * 128:(it + 1) * 128, :],
                               oqbf[:, (it % 2) * 2048:(it % 2 + 1) * 2048],
                               waits=evs, stream=f"OQW{it % 2}")

    STB = [ps[:, i * 512:(i + 1) * 512] for i in range(6)]
    PVB = [ps[0:65, 3072:3584], ps[0:65, 3584:4096]]
    kp_ev = [NOEV, NOEV]
    e_kp, e_vp, e_qp = [NOEV, NOEV], [NOEV, NOEV], [NOEV, NOEV]
    e_p3T = p.now("T")

    def prefetch_pair(hp):
        b = hp % 2
        e_kp[b] = p.dma("sync", ar_(KP_O + b * 4096, 4096),
                        kag_o.ap()[:, hp * 128:(hp + 1) * 128],
                        waits=[e_ag_k, kp_ev[b], e_p3T], transpose=True,
                        stream=f"KP{b}")
        vdst = AP(arena, VP_O + b * 4160, [[ARW, 128], [130, KCN], [1, 130]])
        vsrc = AP(vag_o, hp * 130,
                  [[H * 65, 128], [128 * H * 65, KCN], [1, 130]])
        e_vp[b] = p.dma("sync", vdst, vsrc,
                        waits=[e_ag_v, kp_ev[b], e_p3T], stream=f"VP{b}")
        e_qp[b] = p.dma("sync", ar_(QP_O + b * 2048, 2048),
                        q_tm_d.ap()[:, hp * 128:(hp + 1) * 128],
                        waits=[e_q_w[NT - 1], e_q_w[NT - 2], kp_ev[b], e_p3T],
                        transpose=True, stream=f"QP{b}")

    st_["b3r"] = NOEV
    ffn_sched = {}

    def at_(r, g, fn):
        ffn_sched.setdefault((r, g), []).append(fn)

    NG = len(KC_GROUPS)
    for it_ in range(ITI):
        r, o = it_ // 2, it_ % 2
        at_(r, 0 + o, lambda it=it_: stage1_load(it))
        at_(r, 3 + o, lambda it=it_: (stage1_sig(it), stage1_gq(it)))
        at_(r, 7 + o, lambda it=it_: stage1_tail(it))

    def _b3():
        if "g2" in debug:
            t_ = nc.dram_tensor("dbg_g2", [I, T], F32, kind="ExternalOutput")
            st_["dbg_g2"] = p.dma("sync", t_.ap(), gate_sp.ap(),
                                  waits=[e_g2_w[ITI - 1], e_g2_w[ITI - 2]],
                                  stream="DBG")
        barrier3()
        st_["b3r"] = st_["rs2ev"]
    at_(12, 2, _b3)
    for it_ in range(ITI):
        r, o = 13 + it_ // 2, it_ % 2
        at_(r, 0 + o, lambda it=it_: (stage2_load(it), stage2_qdq(it, 0),
                                      stage2_qdq(it, 1)))
        at_(r, 5 + o, lambda it=it_: (stage2_qdq(it, 2), stage2_qdq(it, 3),
                                      stage2_mul(it), stage2_max(it)))
    at_(25, 2, barrier4)
    for it_ in range(ITI):
        r, o = 26 + it_ // 5, it_ % 5
        at_(r, 2 * o, lambda it=it_: (stage3_load(it), stage3_half(it, 0),
                                      stage3_half(it, 1)))
    sched_pos = [0]

    def pump_slot(r, g):
        tgt = r * NG + g
        while sched_pos[0] <= tgt:
            rr, gg = sched_pos[0] // NG, sched_pos[0] % NG
            for fn in ffn_sched.get((rr, gg), []):
                fn()
            sched_pos[0] += 1

    def pump_all():
        mx = max(r * NG + g for (r, g) in ffn_sched) if ffn_sched else 0
        while sched_pos[0] <= mx:
            rr, gg = sched_pos[0] // NG, sched_pos[0] % NG
            for fn in ffn_sched.get((rr, gg), []):
                fn()
            sched_pos[0] += 1

    prefetch_pair(0)
    e_p3V = p.now("V")
    pv_free = [e_p3V, e_p3V]
    e_attn_w = []
    st_free = [e_p3V] * 6
    pt_last = [NOEV] * 4
    e_pv_g = [NOEV, NOEV]

    for hp in range(NPAIR):
        b = hp % 2
        if hp + 1 < NPAIR:
            prefetch_pair(hp + 1)
        kpe, vpe, qpe = e_kp[b], e_vp[b], e_qp[b]
        for qb in range(QBN):
            pend_pv = []

            def emit_pv(hh, gi, kc0, glen, e_e):
                e_pv = None
                for j in range(glen):
                    kc = kc0 + j
                    vlhs = AP(arena, VP_O + b * 4160 + kc * 130 + hh * 65,
                              [[ARW, 128], [1, 65]])
                    pt_off = (hh * 2 + gi % 2) * 1536
                    prhs = ptb[:, pt_off + j * 512:pt_off + (j + 1) * 512]
                    w = [e_e, vpe]
                    if gi == 0 and j == 0:
                        w.append(pv_free[hh])
                    e_pv = p.emit("tensor", lambda eng, hh=hh, vlhs=vlhs,
                                  prhs=prhs, st=(gi == 0 and j == 0),
                                  sp_=(kc == KCN - 1): eng.matmul(
                                      PVB[hh], vlhs, prhs, start=st,
                                      stop=sp_, skip_group_check=True),
                                  waits=w, inc="T")
                e_pv_g[hh] = e_pv
                pt_last[hh * 2 + gi % 2] = ("T", e_pv[1])

            for gi, (kc0, glen) in enumerate(KC_GROUPS):
                e_st = [None, None]
                for hh in range(2):
                    for j in range(glen):
                        kc = kc0 + j
                        bank = hh * 3 + j
                        klhs = AP(arena, hh * 64 * ARW + KP_O + b * 4096
                                  + kc * 128, [[ARW, 64], [1, 128]])
                        qrhs = AP(arena, hh * 64 * ARW + QP_O + b * 2048
                                  + qb * 512, [[ARW, 64], [1, 512]])
                        e_st[hh] = p.emit(
                            "tensor", lambda eng, bank=bank, klhs=klhs,
                            qrhs=qrhs, hh=hh: eng.matmul(
                                STB[bank], klhs, qrhs, start=True, stop=True,
                                skip_group_check=True,
                                tile_position=(0, 0) if hh == 0 else (64, 0)),
                            waits=[kpe, qpe, st_free[bank]], inc="T")
                exps_now = []
                for hh in range(2):
                    pslot = hh * 2 + gi % 2
                    pt_off = pslot * 1536
                    pdst = ptb[:, pt_off:pt_off + glen * 512]
                    psrc = ps[:, hh * 3 * 512: hh * 3 * 512 + glen * 512]
                    e_e = p.emit("scalar", lambda eng, pdst=pdst, psrc=psrc:
                                 eng.activation(pdst, psrc, AF.Exp,
                                                scale=ISCALE),
                                 waits=[e_st[hh], pt_last[pslot]], inc="SC")
                    for j in range(glen):
                        st_free[hh * 3 + j] = e_e
                    exps_now.append((hh, gi, kc0, glen, e_e))
                for args in pend_pv:
                    emit_pv(*args)
                pend_pv = exps_now
                pump_slot(hp * QBN + qb, gi)
            for args in pend_pv:
                emit_pv(*args)
            for hh in range(2):
                auo = (2 * (qb % 2) + hh) * 512
                au = aub[:, auo:auo + 512]
                e_ev = p.emit("vector", lambda eng, au=au, hh=hh:
                              eng.tensor_copy(au[0:65, :], PVB[hh]),
                              waits=[e_pv_g[hh]] +
                              ([e_attn_w[-4]] if len(e_attn_w) >= 4 else []),
                              inc="V")
                pv_free[hh] = e_ev
                e_l1 = p.emit("scalar", lambda eng, auo=auo: eng.activation(
                    lnr[64:65, 0:512], aub[64:65, auo:auo + 512], AF.Ln),
                    waits=[e_ev, st_.get("lnfree", NOEV)], inc="SC")
                e_l2 = p.emit("scalar", lambda eng: eng.activation(
                    lnr[64:65, 0:512], lnr[64:65, 0:512], AF.Exp, scale=-1.0),
                    waits=[e_l1], inc="SC")
                e_bc = p.emit("tensor", lambda eng: eng.matmul(
                    STB[5][0:64, :], ones1f[64:65, 0:64],
                    lnr[64:65, 0:512], start=True, stop=True,
                    skip_group_check=True),
                    waits=[e_l2, st_free[5], e_onesf], inc="T")
                an = anb[:, hh * 512:(hh + 1) * 512]
                e_an = p.emit("vector", lambda eng, an=an, au=au:
                              eng.tensor_tensor(an[0:64, :], au[0:64, :],
                                                STB[5][0:64, :], AL.mult),
                              waits=[e_bc] + ([e_attn_w[-2]]
                                              if len(e_attn_w) >= 2 else []),
                              inc="V")
                st_free[5] = e_an
                st_["lnfree"] = ("T", e_bc[1])
                e_attn_w.append(
                    p.dma("sync",
                          attn_d.ap()[(hp * 2 + hh) * 64:
                                      (hp * 2 + hh + 1) * 64,
                                      qb * 512:(qb + 1) * 512],
                          an[0:64, :], waits=[e_an], stream=f"ATW{hh}"))
        eT = p.now("T")
        kp_ev[b] = eT


    pump_all()

    e_p4T = p.now("T")
    e_wo = None
    for c in range(DC):
        e_wo = p.dma("gpsimd", ar_(WO_O + c * 1024, 1024),
                     woT_d.ap()[c * 128:(c + 1) * 128, :],
                     waits=[e_p4T, e_oq_w[ITI - 1], e_oq_w[ITI - 2], e_wo],
                     stream="WOC")
    wd_last = [None, None]
    for c in range(ITI):
        wd_last[c % 2] = p.dma(
            "sync", AP(arena, WD_O + c * 1024,
                       [[ARW, 128], [128, 8], [1, 128]]),
            AP(wd_bf, c * 128 * 128,
               [[128, 128], [I * 128, 8], [1, 128]]),
            waits=[e_ag_wd, e_p4T, e_oq_w[ITI - 1], e_oq_w[ITI - 2],
                   wd_last[c % 2]],
            stream=f"WDL{c % 2}")
    e_wd = wd_last

    PF = [ps[:, i * 512:(i + 1) * 512] for i in range(8)]
    ch_ev = [NOEV, NOEV, NOEV, NOEV]
    e_out_w = []
    prev_adds = NOEV
    prev_outs = NOEV
    for tg in range(4):
        e_mm_last = [None] * 8
        for ci in range(DC + ITI):
            cb = ci % 4
            if ci < DC:
                src = attn_d.ap()[ci * 128:(ci + 1) * 128,
                                  tg * 512:(tg + 1) * 512]
                wsrc = [e_attn_w[-1], e_attn_w[-2], ch_ev[cb]]
                roff = WO_O + ci * 1024
            else:
                src = oq_d.ap()[(ci - DC) * 128:(ci - DC + 1) * 128,
                                tg * 512:(tg + 1) * 512]
                wsrc = [e_oq_w[ITI - 1], e_oq_w[ITI - 2], ch_ev[cb]]
                roff = WD_O + (ci - DC) * 1024
            el = p.dma("sync", chld[:, cb * 512:(cb + 1) * 512], src,
                       waits=wsrc, stream=f"CHL{cb}")
            for tti in range(4):
                for jb in range(2):
                    s8 = tti * 2 + jb
                    w = ([el, e_wo, prev_adds] + e_wd) if ci == 0 else [el]
                    e_mm_last[s8] = p.emit(
                        "tensor", lambda eng, s8=s8, cb=cb, tti=tti, jb=jb,
                        roff=roff, ci=ci: eng.matmul(
                            PF[s8],
                            chld[:, cb * 512 + tti * 128:
                                 cb * 512 + (tti + 1) * 128],
                            ar_(roff + jb * 512, 512),
                            start=(ci == 0), stop=(ci == DC + ITI - 1),
                            skip_group_check=True),
                        waits=w, inc="T")
            ch_ev[cb] = e_mm_last[7]
        adds = []
        for tti in range(4):
            tt = tg * 4 + tti
            xb2 = fin_x[:, tti * 1024:(tti + 1) * 1024]
            el = p.dma("sync", xb2, x_d.ap()[tt * 128:(tt + 1) * 128, :],
                       waits=[prev_adds, e_p4T, e_oq_w[ITI - 1],
                              e_oq_w[ITI - 2]], stream=f"FX{tti}")
            ot = fin_o[:, tti * 1024:(tti + 1) * 1024]
            tev = []
            for jb in range(2):
                s8 = tti * 2 + jb
                prev_out_slot = (e_out_w[-4] if len(e_out_w) >= 4 else None)
                e_ad = p.emit("vector", lambda eng, ot=ot, xb2=xb2, s8=s8,
                              jb=jb: eng.tensor_tensor(
                                  ot[:, jb * 512:(jb + 1) * 512],
                                  xb2[:, jb * 512:(jb + 1) * 512],
                                  PF[s8], AL.add),
                              waits=[el, e_mm_last[s8], prev_out_slot],
                              inc="V")
                tev.append(e_ad)
                adds.append(e_ad)
            e_out_w.append(p.dma("sync",
                                 out_d.ap()[tt * 128:(tt + 1) * 128, :], ot,
                                 waits=tev, stream=f"OUT{tti}"))
        prev_adds = adds[-1]
        prev_outs = None

    coll_dbg = {"kag_o": (kag_o, e_ag_k), "vag_o": (vag_o, e_ag_v),
                "wu_bf": (wu_bf, e_ag_wu), "wg_bf": (wg_bf, e_ag_wg),
                "wd_bf": (wd_bf, e_ag_wd),
                "scales": None}
    for nm in debug:
        if nm in coll_dbg and nm == "scales":
            dbg_t = nc.dram_tensor("dbg_scales", [128, 16], F32,
                                   kind="ExternalOutput")
            p.dma("sync", dbg_t.ap()[:, 0:8], brx[:, :],
                  waits=[e_out_w[-1]], stream="DBG")
            p.dma("sync", dbg_t.ap()[:, 8:16], stt[:, :],
                  waits=[e_out_w[-1]], stream="DBG")
        elif nm in coll_dbg:
            tsr, ev = coll_dbg[nm]
            dbg_t = nc.dram_tensor("dbg_" + nm, list(tsr.shape), tsr.dtype,
                                   kind="ExternalOutput")
            p.dma("sync", dbg_t.ap(), tsr.ap(), waits=[ev], stream="DBG")

    p.emit("gpsimd", lambda eng: eng.engine_nop(),
           waits=[p.now(k) for k in sorted(p.cnt)], inc="V")

    p.check()

    with nc.Block() as block:
        @block.tensor
        def _(eng):
            for f in p.ops["tensor"]:
                f(eng)

        @block.vector
        def _(eng):
            for f in p.ops["vector"]:
                f(eng)

        @block.scalar
        def _(eng):
            for f in p.ops["scalar"]:
                f(eng)

        @block.sync
        def _(eng):
            for f in p.ops["sync"]:
                f(eng)

        @block.gpsimd
        def _(eng):
            for f in p.ops["gpsimd"]:
                f(eng)

    stack.close()
    return nc


def _perm():
    pm = []
    for h in range(H):
        pm += [h * HD + 2 * j for j in range(HD // 2)]
        pm += [h * HD + 2 * j + 1 for j in range(HD // 2)]
    return np.array(pm)


def _tables(half):
    i = np.arange(32, dtype=np.float64)
    freqs = 10000.0 ** (-i / 32.0)
    p_ = np.arange(128, dtype=np.float64)
    tt = np.arange(NT, dtype=np.float64)
    pos = half * T + tt[None, :, None] * 128 + p_[:, None, None]
    ang = pos * freqs[None, None, :]
    return (np.cos(ang).reshape(128, NT * 32).astype(np.float32),
            np.sin(ang).reshape(128, NT * 32).astype(np.float32))


def kernel(L0_attn_x, L0_attn_Wq, L0_attn_bq, L0_attn_Wk, L0_attn_bk,
           L0_attn_Wv, L0_attn_bv, L0_attn_Wo, L0_ffn_w_gate, L0_ffn_w_up,
           L0_ffn_w_down):
    x = np.ascontiguousarray(np.asarray(L0_attn_x, dtype=np.float32))
    pm = _perm()
    wqT = np.ascontiguousarray(np.asarray(L0_attn_Wq, dtype=np.float32).T[:, pm])
    wkT = np.ascontiguousarray(np.asarray(L0_attn_Wk, dtype=np.float32).T[:, pm])
    wvT = np.ascontiguousarray(np.asarray(L0_attn_Wv, dtype=np.float32).T)
    woT = np.ascontiguousarray(np.asarray(L0_attn_Wo, dtype=np.float32).T)
    bq = np.asarray(L0_attn_bq, dtype=np.float32)[pm][None, :]
    bk = np.asarray(L0_attn_bk, dtype=np.float32)[pm][None, :]
    bv = np.asarray(L0_attn_bv, dtype=np.float32)[None, :]
    wu = np.ascontiguousarray(np.asarray(L0_ffn_w_up, dtype=np.float32))
    wg = np.ascontiguousarray(np.asarray(L0_ffn_w_gate, dtype=np.float32))
    wd = np.ascontiguousarray(np.asarray(L0_ffn_w_down, dtype=np.float32))

    nc = build_program(tuple(DEBUG))

    tables = [_tables(0), _tables(1)]
    in_maps = []
    for c in range(8):
        b, half = c // 2, c % 2
        cos_t, sin_t = tables[half]
        in_maps.append({
            "x": np.ascontiguousarray(x[b, half * T:(half + 1) * T, :]),
            "wqT": wqT, "wkT": wkT, "wvT": wvT, "woT": woT,
            "bq": bq, "bk": bk, "bv": bv,
            "wu_sh": np.ascontiguousarray(wu[c * (I // 8):(c + 1) * (I // 8)]),
            "wg_sh": np.ascontiguousarray(wg[c * (I // 8):(c + 1) * (I // 8)]),
            "wd_sh": np.ascontiguousarray(wd[c * (D // 8):(c + 1) * (D // 8)]),
            "cos_t": cos_t, "sin_t": sin_t,
            "ident": np.eye(128, dtype=np.float32),
        })
    res = run_bass_kernel_spmd(nc, in_maps, core_ids=list(range(8)))
    out = np.empty((B, S, D), dtype=np.float32)
    for c in range(8):
        b, half = c // 2, c % 2
        out[b, half * T:(half + 1) * T, :] = res.results[c]["out"]
    if DEBUG:
        kernel.debug_results = res.results
    return out


# version 23
# speedup vs baseline: 1.1214x; 1.0187x over previous
import numpy as np
import ml_dtypes
import concourse.bass as bass
import concourse.mybir as mybir
from concourse.bass_utils import run_bass_kernel_spmd

F32 = mybir.dt.float32
BF16 = mybir.dt.bfloat16
AL = mybir.AluOpType
AF = mybir.ActivationFunctionType
AX = mybir.AxisListType
AP = bass.AP

B, S, D, H, HD, I = 4, 4096, 1024, 16, 64, 3072
T = 2048
NT = T // 128
DC = D // 128
ITI = I // 128
QBN = T // 512
KCN = S // 128
NPAIR = H // 2
ISCALE = 1.0 / 32.0
MAGIC = 12582912.0
EPS = 1e-5
QA_ = 32767.0
KC_GROUPS = [(g * 3, min(3, KCN - g * 3)) for g in range((KCN + 2) // 3)]

DEBUG = []
NOEV = ("V", 0)
ENGS = ("tensor", "vector", "scalar", "sync", "gpsimd")


class P:

    def __init__(self, nc, sems, sem_factory):
        self.nc = nc
        self.sems = sems
        self.sem_factory = sem_factory
        self.ops = {e: [] for e in ENGS}
        self.meta = {e: [] for e in ENGS}
        self.cnt = {k: 0 for k in sems}
        self.hwm = {e: {k: 0 for k in sems} for e in ENGS}

    def stream(self, name):
        if name not in self.sems:
            self.sems[name] = self.sem_factory(name)
            self.cnt[name] = 0
            for e in ENGS:
                self.hwm[e][name] = 0
        return name

    def emit(self, e, fn, waits=(), inc=None, k=1):
        waits = [w for w in waits if w is not None and w[1] > 0]
        for (s, v) in waits:
            if self.hwm[e][s] < v:
                self.hwm[e][s] = v
                h = self.sems[s]
                self.ops[e].append(lambda eng, h=h, v=v: eng.wait_ge(h, v))
                self.meta[e].append((((s, v),), None, 0))
        ev = None
        if inc is not None:
            self.cnt[inc] += k
            ev = (inc, self.cnt[inc])
            h = self.sems[inc]
            self.ops[e].append(lambda eng, fn=fn, h=h, k=k: fn(eng).then_inc(h, k))
            self.meta[e].append(((), inc, k))
        else:
            self.ops[e].append(lambda eng, fn=fn: fn(eng))
            self.meta[e].append(((), None, 0))
        return ev

    def dma(self, q, out, in_, waits=(), transpose=False, slow=False,
            stream=None):
        assert stream is not None, "dma needs a stream name"
        sem = self.stream(stream)
        eng = "gpsimd" if q == "barrier" else q
        if transpose:
            fn = lambda eng_, out=out, in_=in_: eng_.dma_start_transpose(out, in_)
        elif slow:
            fn = lambda eng_, out=out, in_=in_: eng_.dma_start(
                out=out, in_=in_, allow_slow_non_contiguous=True)
        else:
            fn = lambda eng_, out=out, in_=in_: eng_.dma_start(out=out, in_=in_)
        return self.emit(eng, fn, waits, inc=sem, k=16)

    def coll(self, kind, op, groups, in_ap, out_ap, waits=(), stream="C"):
        fn = lambda eng: eng.collective_compute(
            kind, op, replica_groups=groups, ins=[in_ap], outs=[out_ap])
        return self.emit("gpsimd", fn, waits, inc=self.stream(stream))

    def now(self, sem):
        return (sem, self.cnt[sem])

    def check(self):
        pos = {e: 0 for e in ENGS}
        cnt = {k: 0 for k in self.cnt}
        progress = True
        while progress:
            progress = False
            for e in ENGS:
                while pos[e] < len(self.meta[e]):
                    waits, inc, k = self.meta[e][pos[e]]
                    if any(cnt[s] < v for (s, v) in waits):
                        break
                    if inc is not None:
                        cnt[inc] += k
                    pos[e] += 1
                    progress = True
        stuck = {e: (len(self.meta[e]) - pos[e]) for e in ENGS
                 if pos[e] < len(self.meta[e])}
        if stuck:
            det = {e: self.meta[e][pos[e]][0] for e in stuck}
            raise RuntimeError(
                f"deadlock in plan: remaining={stuck} blocked_on={det} cnt={cnt}")


def build_program(debug=()):
    nc = bass.Bass("TRN2", target_bir_lowering=False)

    COLL_OUTS = {"kag_o", "vag_o", "wu_bf", "wg_bf", "wd_bf"}

    def dram(name, shape, dt, kind="Internal"):
        k = ("ExternalOutput" if (name in debug and kind == "Internal"
                                  and name not in COLL_OUTS) else kind)
        return nc.dram_tensor(name, shape, dt, kind=k)

    x_d = nc.dram_tensor("x", [T, D], F32, kind="ExternalInput")
    wqT_d = nc.dram_tensor("wqT", [D, D], F32, kind="ExternalInput")
    wkT_d = nc.dram_tensor("wkT", [D, D], F32, kind="ExternalInput")
    wvT_d = nc.dram_tensor("wvT", [D, D], F32, kind="ExternalInput")
    woT_d = nc.dram_tensor("woT", [D, D], F32, kind="ExternalInput")
    bq_d = nc.dram_tensor("bq", [1, D], F32, kind="ExternalInput")
    bk_d = nc.dram_tensor("bk", [1, D], F32, kind="ExternalInput")
    bv_d = nc.dram_tensor("bv", [1, D], F32, kind="ExternalInput")
    wu_sh_d = nc.dram_tensor("wu_sh", [I // 8, D], F32, kind="ExternalInput")
    wg_sh_d = nc.dram_tensor("wg_sh", [I // 8, D], F32, kind="ExternalInput")
    wd_sh_d = nc.dram_tensor("wd_sh", [D // 8, I], F32, kind="ExternalInput")
    ident_d = nc.dram_tensor("ident", [128, 128], F32, kind="ExternalInput")
    cos_d = nc.dram_tensor("cos_t", [128, NT * 32], F32, kind="ExternalInput")
    sin_d = nc.dram_tensor("sin_t", [128, NT * 32], F32, kind="ExternalInput")
    out_d = nc.dram_tensor("out", [T, D], F32, kind="ExternalOutput")

    xn32_d = dram("xn32", [T, D], F32)
    xnbf_d = dram("xnbf", [T, D], BF16)
    xqbf_d = dram("xqbf", [T, D], BF16)
    q_tm_d = dram("q_tm", [T, D], BF16)
    kag_i = dram("kag_i", [T, D], BF16)
    kag_o = dram("kag_o", [S, D], BF16)
    vag_i = dram("vag_i", [T, H * 65], BF16)
    vag_o = dram("vag_o", [S, H * 65], BF16)
    wu_ag_i = dram("wu_ag_i", [D, I // 8], BF16)
    wu_bf = dram("wu_bf", [8 * D, I // 8], BF16)
    wg_ag_i = dram("wg_ag_i", [D, I // 8], BF16)
    wg_bf = dram("wg_bf", [8 * D, I // 8], BF16)
    wd_ag_i = dram("wd_ag_i", [I, D // 8], BF16)
    wd_bf = dram("wd_bf", [8 * I, D // 8], BF16)
    up_sp = dram("up_sp", [I, T], F32)
    gate_sp = dram("gate_sp", [I, T], F32)
    oq_d = dram("oq", [I, T], BF16)
    attn_d = dram("attn", [D, T], BF16)
    crs = [dram(f"crd{i}", [128, nv], F32)
           for i, nv in enumerate([1, 2, 1, 1])]
    aris = [dram(f"ar{i}i", [1, 2], F32) for i in range(4)]
    aros = [dram(f"ar{i}o", [1, 2], F32) for i in range(4)]

    from contextlib import ExitStack
    stack = ExitStack()

    def sbt(name, shape, dt):
        return stack.enter_context(nc.sbuf_tensor(name, shape, dt))

    XNFM_O = 0
    WQ_O = XNFM_O + 8 * 2048
    WK_O = WQ_O + 8 * 1024
    WV_O = WK_O + 8 * 1024
    WU_O = 0
    WG_O = WU_O + 8 * 3072
    XQ_O = WG_O + 8 * 3072
    P3_END = XQ_O + 8 * 2048
    KP_O = 0
    VP_O = KP_O + 2 * 4096
    QP_O = VP_O + 2 * 4160
    P4_END_E = QP_O + 2 * 2048
    WO_O = P4_END_E
    WD_O = WO_O + 8 * 1024
    P6_END = WD_O + 24 * 1024

    xst = sbt("xst", [128, 2 * 1024], F32)
    sc1 = sbt("sc1", [128, 1024], F32)
    xn32s = sbt("xn32s", [128, 2 * 1024], F32)
    xnbs = sbt("xnbs", [128, 2 * 1024], BF16)
    stt = sbt("stt", [128, 8], F32)
    lnst = sbt("lnst", [128, 8], F32)
    lnsum = sbt("lnsum", [128, 16], F32)
    lnsq = sbt("lnsq", [128, 16], F32)
    lnmean = sbt("lnmean", [128, 16], F32)
    lnvar = sbt("lnvar", [128, 16], F32)
    lnsqr = sbt("lnsqr", [128, 16], F32)
    lnrst = sbt("lnrst", [128, 16], F32)
    lntmp = sbt("lntmp", [128, 16], F32)
    cosb = sbt("cosb", [128, NT * 32], F32)
    sinb = sbt("sinb", [128, NT * 32], F32)
    bqs = sbt("bqs", [1, D], BF16)
    bks = sbt("bks", [1, D], BF16)
    bvs = sbt("bvs", [1, D], BF16)
    ones1 = sbt("ones1", [128, 128], BF16)
    ones1f = sbt("ones1f", [128, 64], F32)
    identb = sbt("identb", [128, 128], F32)
    rt1 = sbt("rt1", [128, 512], F32)
    rt2 = sbt("rt2", [128, 512], F32)
    dqs = sbt("dqs", [128, 3 * 96], F32)
    spst = sbt("spst", [128, 4 * 512], F32)
    mux = sbt("mux", [128, 8], F32)
    brx = sbt("brx", [128, 8], F32)
    rowb = sbt("rowb", [1, 256], F32)
    m0b = sbt("m0b", [1, 2], F32)
    ptb = sbt("ptb", [128, 4 * 1536], BF16)
    aub = sbt("aub", [128, 4 * 512], F32)
    lnr = sbt("lnr", [128, 512], F32)
    anb = sbt("anb", [128, 2 * 512], BF16)
    chld = sbt("chld", [128, 4 * 512], BF16)

    ARW = 64 * 1024
    arena = sbt("arena", [128, ARW], BF16)
    abase = nc.lookup_mloc(arena).addr
    ffw = nc.alloc_sbuf_tensor_at("ffw", [128, 10 * 2048], F32,
                                  offset=abase + 2 * P4_END_E)
    oqbf = nc.alloc_sbuf_tensor_at("oqbf", [128, 2 * 2048], BF16,
                                   offset=abase + 2 * P4_END_E + 8 * 2048 * 4)
    dqw = nc.alloc_sbuf_tensor_at("dqw", [128, 2 * 3072], F32,
                                  offset=abase + 81920)
    dqo = nc.alloc_sbuf_tensor_at("dqo", [128, 3072], BF16,
                                  offset=abase + 81920 + 24576)
    dqt = nc.alloc_sbuf_tensor_at("dqt", [128, 24 * 128], BF16, offset=abase)
    dqo = nc.alloc_sbuf_tensor_at("dqo", [128, 3072], F32,
                                  offset=abase + 6144)
    fin_x = nc.alloc_sbuf_tensor_at("fin_x", [128, 4 * 1024], F32,
                                    offset=abase)
    fin_o = nc.alloc_sbuf_tensor_at("fin_o", [128, 4 * 1024], F32,
                                    offset=abase + 16384)
    qko = nc.alloc_sbuf_tensor_at("qko", [128, 2 * 1024], BF16,
                                  offset=abase + 112640)
    qko2 = nc.alloc_sbuf_tensor_at("qko2", [128, 2 * 1024], BF16,
                                   offset=abase + 116736)
    v65 = nc.alloc_sbuf_tensor_at("v65", [128, 2 * 1040], BF16,
                                  offset=abase + 120832)

    def ar_(off, nelem, p0=0, np_=128):
        return AP(arena, p0 * ARW + off, [[ARW, np_], [1, nelem]])

    assert max(P3_END, P6_END) <= ARW

    def fw(slot):
        return ffw[:, slot * 2048:(slot + 1) * 2048]

    ps = stack.enter_context(nc.psum_tensor("ps", [128, 4096], F32))

    semnames = ["T", "V", "SC", "C"]
    sems = {n: stack.enter_context(nc.semaphore(n)) for n in semnames}
    p = P(nc, sems, lambda nm: stack.enter_context(nc.semaphore(nm)))

    bar_idx = [0]

    def barrier_max(src_ev, m_ap, nvals, extra=()):
        i = bar_idx[0]
        bar_idx[0] += 1
        cr, ari, aro = crs[i], aris[i], aros[i]
        e1 = p.dma("barrier", cr.ap()[:, 0:nvals], m_ap, waits=[src_ev],
                   stream="BAR")
        e2 = p.dma("barrier", rowb[0:1, 0:nvals * 128],
                   AP(cr, 0, [[1, 1], [1, nvals], [nvals, 128]]), waits=[e1],
                   slow=True, stream="BAR")
        e3 = p.emit("vector", lambda eng, nvals=nvals: eng.tensor_reduce(
            m0b[0:1, 0:nvals, None],
            AP(rowb, 0, [[256, 1], [128, nvals], [1, 128]]),
            AX.X, AL.max), waits=[e2], inc="V")
        e4 = p.dma("barrier", ari.ap()[:, 0:nvals], m0b[0:1, 0:nvals],
                   waits=[e3], stream="BAR")
        e5 = p.coll("AllReduce", AL.max, [list(range(8))],
                    ari.ap()[:, 0:nvals], aro.ap()[:, 0:nvals], waits=[e4],
                    stream=f"CB{i}")
        e6 = p.dma("barrier", brx[:, 6:6 + nvals],
                   aro.ap()[0, 0:nvals].partition_broadcast(128),
                   waits=[e5] + list(extra), stream="BAR")
        return e6

    e_id = p.dma("sync", identb[:, :], ident_d.ap(), stream="IDL")
    e_cos = p.dma("sync", cosb[:, :], cos_d.ap(), stream="COS")
    e_sin = p.dma("sync", sinb[:, :], sin_d.ap(), stream="SIN")
    e_ones = p.emit("vector", lambda eng: eng.memset(ones1[:, :], 1.0), inc="V")
    e_onesf = p.emit("vector", lambda eng: eng.memset(ones1f[:, :], 1.0), inc="V")
    wq_off = {0: WQ_O, 1: WK_O, 2: WV_O}
    cast_items = [(bqs[:, :], bq_d.ap()), (bks[:, :], bk_d.ap()),
                  (bvs[:, :], bv_d.ap())]
    for c in range(DC):
        for wi, wd_ in enumerate([wqT_d, wkT_d, wvT_d]):
            cast_items.append((ar_(wq_off[wi] + c * 1024, 1024),
                               wd_.ap()[c * 128:(c + 1) * 128, :]))
    CAST_STREAMS = ["OUT0", "OUT1", "OUT2", "OUT3", "FX0", "FX1", "FX2",
                    "FX3", "CHL0", "CHL1", "CHL2", "CHL3"]
    wc_last = {}
    for i, (dst, src) in enumerate(cast_items):
        st_ = CAST_STREAMS[i % 12]
        wc_last[st_] = p.dma("gpsimd", dst, src,
                             waits=[wc_last.get(st_, None)], stream=st_)
    e_wcast = tuple(wc_last.values())

    st_dq = {}

    def dequant(src_d, dst_ag_i, ntile, width, stream, start_wait=None):
        nblk = width // 32
        nch = width // 128
        last = start_wait
        ev6 = None
        ekv = []
        for t in range(ntile):
            b = t % 2
            wt = dqw[:, b * 3072:b * 3072 + width]
            el = p.dma("gpsimd", wt, src_d.ap()[t * 128:(t + 1) * 128, :],
                       waits=[last] if last else (), stream=stream)
            wv3 = AP(dqw, b * 3072, [[2 * 3072, 128], [32, nblk], [1, 32]])
            bm = dqs[:, 0:nblk]
            sv = dqs[:, 96:96 + nblk]
            rv = dqs[:, 192:192 + nblk]
            e1 = p.emit("vector", lambda eng, wv3=wv3, bm=bm: eng.tensor_reduce(
                bm[:, :, None], wv3, AX.X, AL.max, apply_absolute_value=True),
                waits=[el, ev6], inc="V")
            e2 = p.emit("vector", lambda eng, bm=bm, sv=sv: eng.tensor_scalar(
                sv, bm, 1.0 / 7.0, 1e-12, AL.mult, AL.max), waits=[e1], inc="V")
            e3 = p.emit("vector", lambda eng, sv=sv, rv=rv: eng.reciprocal(
                rv, sv), waits=[e2], inc="V")
            rbc = rv[:, :, None].broadcast_to([128, nblk, 32])
            sbc = sv[:, :, None].broadcast_to([128, nblk, 32])
            if width <= 2048:
                t1o = b * 3072 + width
            else:
                t1o = (1 - b) * 3072
            t1v = AP(dqw, t1o, [[2 * 3072, 128], [32, nblk], [1, 32]])
            e4 = p.emit("vector", lambda eng, t1v=t1v, wv3=wv3, rbc=rbc:
                        eng.tensor_tensor(t1v, wv3, rbc, AL.mult),
                        waits=[e3], inc="V")
            e5 = p.emit("vector", lambda eng, t1v=t1v: eng.tensor_scalar(
                t1v, t1v, MAGIC, MAGIC, AL.add, AL.subtract),
                waits=[e4], inc="V")
            ov3 = AP(dqo, 0, [[3072, 128], [32, nblk], [1, 32]])
            e6 = p.emit("vector", lambda eng, ov3=ov3, t1v=t1v, sbc=sbc:
                        eng.tensor_tensor(ov3, t1v, sbc, AL.mult),
                        waits=[e5], inc="V")
            ev6 = e6
            for c in range(nch):
                pslot = ps[:, 3072 + (c % 2) * 512:3072 + (c % 2) * 512 + 128]
                etr = p.emit("tensor", lambda eng, c=c, pslot=pslot:
                             eng.transpose(pslot, dqo[:, c * 128:(c + 1) * 128],
                                           identb[:, :]),
                             waits=[e6, e_id,
                                    st_dq.get(c % 2, None)], inc="T")
                eev = p.emit("vector", lambda eng, c=c, t=t, pslot=pslot,
                             ntile=ntile: eng.tensor_copy(
                                 dqt[:, (c * ntile + t) * 128:
                                     (c * ntile + t) * 128 + 128], pslot),
                             waits=[etr], inc="V")
                st_dq[c % 2] = eev
                ekv.append(eev)
            last = el
        ew = p.dma("gpsimd", AP(dst_ag_i, 0,
                              [[ntile * 128, 128], [128 * ntile * 128, nch],
                               [1, ntile * 128]]),
                   dqt[:, 0:nch * ntile * 128].rearrange(
                       "p (c x) -> p c x", c=nch),
                   waits=ekv, stream=stream)
        return ew, ev6

    e_dq_u, ev_u = dequant(wu_sh_d, wu_ag_i, 3, 1024, "DQ")
    e_dq_g, ev_g = dequant(wg_sh_d, wg_ag_i, 3, 1024, "DQ", start_wait=ev_u)
    e_dq_d, ev_d = dequant(wd_sh_d, wd_ag_i, 1, 3072, "DQ", start_wait=ev_g)
    grp8 = [list(range(8))]
    e_ag_wu = p.coll("AllGather", AL.bypass, grp8, wu_ag_i.ap(), wu_bf.ap(),
                     waits=[e_dq_u], stream="CWU")
    e_ag_wg = p.coll("AllGather", AL.bypass, grp8, wg_ag_i.ap(), wg_bf.ap(),
                     waits=[e_dq_g], stream="CWG")
    e_ag_wd = p.coll("AllGather", AL.bypass, grp8, wd_ag_i.ap(), wd_bf.ap(),
                     waits=[e_dq_d], stream="CWD")
    e_dqT_V = p.now("V")

    e_xn_w = [None] * NT
    lnA = {}
    for tt in range(NT):
        b = tt % 2
        xt = xst[:, b * 1024:(b + 1) * 1024]
        w = [lnA.get(tt - 2, None)]
        el = p.dma("sync", xt, x_d.ap()[tt * 128:(tt + 1) * 128, :], waits=w,
                   stream=f"X1{b}")
        e_s = p.emit("vector", lambda eng, xt=xt, tt=tt: eng.reduce_sum(
            lnsum[:, tt:tt + 1], xt, AX.X), waits=[el], inc="V")
        e_q = p.emit("scalar", lambda eng, xt=xt, tt=tt: eng.activation(
            sc1[:, :], xt, AF.Square, accum_out=lnsq[:, tt:tt + 1]),
            waits=[el], inc="SC")
        lnA[tt] = ("V", max(e_s[1], 0))
        lnA[(tt, "sc")] = e_q
    e_rng = [None] * 4
    for g4 in range(4):
        c0, c1 = g4 * 4, g4 * 4 + 4
        wA = [lnA[g4 * 4 + 3], lnA[(g4 * 4 + 3, "sc")],
              lnA[(g4 * 4 + 2, "sc")], lnA[(g4 * 4 + 1, "sc")],
              lnA[(g4 * 4, "sc")]]
        e1 = p.emit("vector", lambda eng, c0=c0, c1=c1: eng.tensor_scalar(
            lnmean[:, c0:c1], lnsum[:, c0:c1], 1.0 / D, None, AL.mult),
            waits=wA, inc="V")
        e2 = p.emit("vector", lambda eng, c0=c0, c1=c1: eng.tensor_scalar(
            lnvar[:, c0:c1], lnsq[:, c0:c1], 1.0 / D, None, AL.mult),
            inc="V")
        e3 = p.emit("vector", lambda eng, c0=c0, c1=c1: eng.tensor_tensor(
            lntmp[:, c0:c1], lnmean[:, c0:c1], lnmean[:, c0:c1], AL.mult),
            waits=[e1], inc="V")
        e4 = p.emit("vector", lambda eng, c0=c0, c1=c1: eng.tensor_tensor(
            lnvar[:, c0:c1], lnvar[:, c0:c1], lntmp[:, c0:c1], AL.subtract),
            waits=[e2, e3], inc="V")
        e5 = p.emit("vector", lambda eng, c0=c0, c1=c1: eng.tensor_scalar(
            lnvar[:, c0:c1], lnvar[:, c0:c1], EPS, None, AL.add),
            waits=[e4], inc="V")
        e_sq = p.emit("scalar", lambda eng, c0=c0, c1=c1: eng.activation(
            lnsqr[:, c0:c1], lnvar[:, c0:c1], AF.Sqrt), waits=[e5], inc="SC")
        e_rn = p.emit("vector", lambda eng, c0=c0, c1=c1: eng.reciprocal(
            lnrst[:, c0:c1], lnsqr[:, c0:c1]), waits=[e_sq], inc="V")
        for _ in range(2):
            en1 = p.emit("vector", lambda eng, c0=c0, c1=c1: eng.tensor_tensor(
                lntmp[:, c0:c1], lnrst[:, c0:c1], lnrst[:, c0:c1], AL.mult),
                waits=[e_rn], inc="V")
            en2 = p.emit("vector", lambda eng, c0=c0, c1=c1: eng.tensor_tensor(
                lntmp[:, c0:c1], lntmp[:, c0:c1], lnvar[:, c0:c1], AL.mult),
                waits=[en1], inc="V")
            en3 = p.emit("vector", lambda eng, c0=c0, c1=c1: eng.tensor_scalar(
                lntmp[:, c0:c1], lntmp[:, c0:c1], -0.5, 1.5, AL.mult, AL.add),
                waits=[en2], inc="V")
            e_rn = p.emit("vector", lambda eng, c0=c0, c1=c1: eng.tensor_tensor(
                lnrst[:, c0:c1], lnrst[:, c0:c1], lntmp[:, c0:c1], AL.mult),
                waits=[en3], inc="V")
        e_rng[g4] = e_rn

    e_mx = None
    for tt in range(NT):
        b = tt % 2
        xt = xst[:, b * 1024:(b + 1) * 1024]
        w = [e_xn_w[tt - 2][0], e_xn_w[tt - 2][1]] if tt >= 2 else \
            [lnA[NT - 1], lnA[NT - 2], lnA[(NT - 1, "sc")],
             lnA[(NT - 2, "sc")]]
        el = p.dma("sync", xt, x_d.ap()[tt * 128:(tt + 1) * 128, :], waits=w,
                   stream=f"X1{b}")
        xnt = xn32s[:, b * 1024:(b + 1) * 1024]
        xbt = xnbs[:, b * 1024:(b + 1) * 1024]
        wprev = [e_xn_w[tt - 2][1], e_xn_w[tt - 2][0]] if tt >= 2 else []
        e_xn = p.emit("vector", lambda eng, xt=xt, xnt=xnt, tt=tt:
                      eng.tensor_scalar(xnt, xt, lnmean[:, tt:tt + 1],
                                        lnrst[:, tt:tt + 1], AL.subtract,
                                        AL.mult),
                      waits=[e_rng[tt // 4], el] + wprev, inc="V")
        e_xb = p.emit("vector", lambda eng, xnt=xnt, xbt=xbt: eng.tensor_copy(
            xbt, xnt), waits=[e_xn], inc="V")
        mc = 0 if tt == 0 else 1
        e_m2 = p.emit("vector", lambda eng, xnt=xnt, mc=mc: eng.tensor_reduce(
            mux[:, mc:mc + 1], xnt, AX.X, AL.max, apply_absolute_value=True),
            waits=[e_xn, e_mx], inc="V")
        if tt > 0:
            e_m2 = p.emit("vector", lambda eng: eng.tensor_tensor(
                mux[:, 0:1], mux[:, 0:1], mux[:, 1:2], AL.max),
                waits=[e_m2], inc="V")
        e_mx = e_m2
        ew1 = p.dma("sync", xn32_d.ap()[tt * 128:(tt + 1) * 128, :], xnt,
                    waits=[e_xn], stream=f"XNW{b}")
        ew2 = p.dma("sync", xnbf_d.ap()[tt * 128:(tt + 1) * 128, :], xbt,
                    waits=[e_xb], stream=f"XBW{b}")
        e_xn_w[tt] = (ew1, ew2)
    e_ln_SC = p.now("SC")

    e_b1 = barrier_max(e_mx, mux[:, 0:1], 1)
    e_s1 = p.emit("vector", lambda eng: eng.tensor_scalar(
        brx[:, 0:1], brx[:, 6:7], 1.0 / QA_, 1e-12, AL.mult, AL.max),
        waits=[e_b1], inc="V")
    e_rs1 = p.emit("vector", lambda eng: eng.reciprocal(
        brx[:, 1:2], brx[:, 0:1]), waits=[e_s1], inc="V")



    e_xq_w = []

    def qdq_tile(tt):
        b = tt % 2
        xt = xst[:, b * 1024:(b + 1) * 1024]
        w = [e_xn_w[tt][0]]
        w += ([e_xq_w[tt - 2]] if tt >= 2 else
              [e_xn_w[NT - 1][0], e_xn_w[NT - 2][0],
               e_xn_w[NT - 1][1], e_xn_w[NT - 2][1]])
        el = p.dma("sync", xt, xn32_d.ap()[tt * 128:(tt + 1) * 128, :],
                   waits=w, stream=f"XQL{tt % 2}")
        e1 = p.emit("vector", lambda eng, xt=xt: eng.tensor_scalar(
            sc1[:, :], xt, brx[:, 1:2], MAGIC, AL.mult, AL.add),
            waits=[el, e_rs1, e_ln_SC], inc="V")
        xbt = xnbs[:, b * 1024:(b + 1) * 1024]
        e2 = p.emit("vector", lambda eng, xbt=xbt: eng.tensor_scalar(
            xbt, sc1[:, :], MAGIC, brx[:, 0:1], AL.subtract, AL.mult),
            waits=[e1], inc="V")
        e_xq_w.append(p.dma("sync", xqbf_d.ap()[tt * 128:(tt + 1) * 128, :],
                            xbt, waits=[e2], stream=f"XQW{tt % 2}"))



    e_xnfm_g = [None] * 4
    for g in range(4):
        for c in range(DC):
            e_xnfm_g[g] = p.dma(
                "sync", ar_(XNFM_O + c * 2048 + g * 512, 512),
                xnbf_d.ap()[g * 512:(g + 1) * 512, c * 128:(c + 1) * 128],
                waits=[e_xn_w[4 * g + 3][1], e_xn_w[4 * g + 2][1],
                       e_xnfm_g[g], e_dq_u, e_dq_g, e_dq_d, ev_d],
                transpose=True, stream=f"XNF{g}")
    e_xnfm = e_xnfm_g[3]
    e_v65i = p.emit("vector", lambda eng: eng.memset(
        AP(v65, 64, [[2 * 1040, 128], [1040, 2], [65, 16]]), 1.0), inc="V")

    PSLOT4 = [ps[:, s * 1024:(s + 1) * 1024] for s in range(4)]
    slot_ev = [NOEV, NOEV, NOEV, e_dqT_V]
    e_q_w, e_k_w, e_v_w = [], [], []
    slot_i = [0]

    def proj(woff, bsb, tt, extra):
        s = slot_i[0] % 4
        slot_i[0] += 1
        pslot = PSLOT4[s]
        for c in range(DC):
            for ob in range(2):
                w = ([slot_ev[s]] + extra) if (c == 0 and ob == 0) else ()
                p.emit("tensor", lambda eng, pslot=pslot, c=c, ob=ob, tt=tt,
                       woff=woff: eng.matmul(
                           pslot[:, ob * 512:(ob + 1) * 512],
                           AP(arena, XNFM_O + c * 2048 + tt * 128,
                              [[ARW, 128], [1, 128]]),
                           AP(arena, woff + c * 1024 + ob * 512,
                              [[ARW, 128], [1, 512]]),
                           start=(c == 0), stop=False, skip_group_check=True),
                       waits=w, inc="T")
        ev = None
        for ob in range(2):
            ev = p.emit("tensor", lambda eng, pslot=pslot, ob=ob, bsb=bsb:
                        eng.matmul(pslot[:, ob * 512:(ob + 1) * 512],
                                   ones1[0:1, 0:128],
                                   bsb[0:1, ob * 512:(ob + 1) * 512],
                                   start=False, stop=True,
                                   skip_group_check=True), inc="T")
        return s, ev

    def rope_evac(s, tt, osb, obase, waits):
        Ev = AP(ps, s * 1024, [[4096, 128], [64, 16], [1, 32]])
        Ov = AP(ps, s * 1024 + 32, [[4096, 128], [64, 16], [1, 32]])
        OE = AP(osb, obase, [[2 * 1024, 128], [64, 16], [1, 32]])
        OO = AP(osb, obase + 32, [[2 * 1024, 128], [64, 16], [1, 32]])
        cv = AP(cosb, tt * 32, [[NT * 32, 128], [0, 16], [1, 32]])
        sv = AP(sinb, tt * 32, [[NT * 32, 128], [0, 16], [1, 32]])
        r1 = AP(rt1, 0, [[512, 128], [32, 16], [1, 32]])
        r2 = AP(rt2, 0, [[512, 128], [32, 16], [1, 32]])
        p.emit("vector", lambda eng: eng.tensor_tensor(r1, Ev, cv, AL.mult),
               waits=list(waits) + [e_cos, e_sin], inc="V")
        p.emit("vector", lambda eng: eng.tensor_tensor(r2, Ov, sv, AL.mult),
               inc="V")
        p.emit("vector", lambda eng: eng.tensor_tensor(OE, r1, r2, AL.subtract),
               inc="V")
        p.emit("vector", lambda eng: eng.tensor_tensor(r1, Ev, sv, AL.mult),
               inc="V")
        p.emit("vector", lambda eng: eng.tensor_tensor(r2, Ov, cv, AL.mult),
               inc="V")
        return p.emit("vector", lambda eng: eng.tensor_tensor(
            OO, r1, r2, AL.add), inc="V")

    for tt in range(NT):
        b = tt % 2
        s, e_mm = proj(WQ_O, bqs, tt,
                       [e_xnfm_g[tt // 4], e_ones] + list(e_wcast))
        e_ro = rope_evac(s, tt, qko, b * 1024,
                         [e_mm] + ([e_q_w[tt - 2]] if tt >= 2 else []))
        slot_ev[s] = e_ro
        e_q_w.append(p.dma("sync", q_tm_d.ap()[tt * 128:(tt + 1) * 128, :],
                           qko[:, b * 1024:(b + 1) * 1024], waits=[e_ro],
                           stream=f"QW{tt % 2}"))
        s, e_mm = proj(WK_O, bks, tt, [])
        e_ro = rope_evac(s, tt, qko2, b * 1024,
                         [e_mm] + ([e_k_w[tt - 2]] if tt >= 2 else []))
        slot_ev[s] = e_ro
        e_k_w.append(p.dma("sync", kag_i.ap()[tt * 128:(tt + 1) * 128, :],
                           qko2[:, b * 1024:(b + 1) * 1024], waits=[e_ro],
                           stream=f"KW{tt % 2}"))
        s, e_mm = proj(WV_O, bvs, tt, [])
        pv_in = AP(ps, s * 1024, [[4096, 128], [64, 16], [1, 64]])
        vv_out = AP(v65, b * 1040, [[2 * 1040, 128], [65, 16], [1, 64]])
        e_cv = p.emit("vector", lambda eng, pv_in=pv_in, vv_out=vv_out:
                      eng.tensor_copy(vv_out, pv_in),
                      waits=[e_mm, e_v65i] +
                      ([e_v_w[tt - 2]] if tt >= 2 else []), inc="V")
        slot_ev[s] = e_cv
        e_v_w.append(p.dma("sync", vag_i.ap()[tt * 128:(tt + 1) * 128, :],
                           v65[:, b * 1040:(b + 1) * 1040], waits=[e_cv],
                           stream=f"VW{tt % 2}"))
        qdq_tile(tt)

    grpp = [[0, 1], [2, 3], [4, 5], [6, 7]]
    e_ag_k = p.coll("AllGather", AL.bypass, grpp, kag_i.ap(), kag_o.ap(),
                    waits=[e_k_w[NT - 1], e_k_w[NT - 2]], stream="CK")
    e_ag_v = p.coll("AllGather", AL.bypass, grpp, vag_i.ap(), vag_o.ap(),
                    waits=[e_v_w[NT - 1], e_v_w[NT - 2]], stream="CV")

    e_p2T = p.now("T")
    e_p2V = [e for e in slot_ev]
    wul_gate = [e_ag_wu, e_ag_wg, e_p2T] + e_xnfm_g + e_p2V
    xqf_gate = [e_dq_d, ev_d,
                e_v_w[NT - 1], e_v_w[NT - 2], e_q_w[NT - 1], e_q_w[NT - 2],
                e_k_w[NT - 1], e_k_w[NT - 2]]
    WUL_STREAMS = [("WUL0", None), ("WUL1", None),
                   ("QW0", e_q_w[NT - 2]), ("QW1", e_q_w[NT - 1]),
                   ("KW0", e_k_w[NT - 2]), ("KW1", e_k_w[NT - 1]),
                   ("VW0", e_v_w[NT - 2]), ("VW1", e_v_w[NT - 1]),
                   ("WUL2", None), ("WUL3", None),
                   ("XQW0", e_xq_w[NT - 2]), ("XQW1", e_xq_w[NT - 1]),
                   ("XNW0", e_xn_w[NT - 2][0]), ("XNW1", e_xn_w[NT - 1][0]),
                   ("XNF0", e_xnfm_g[0]), ("XNF1", e_xnfm_g[1])]
    e_wu_only, e_wg_only = [], []
    e_wT_l, e_xqfm_l = [], []
    for c in range(DC):
        st0, ep0 = WUL_STREAMS[c]
        st1, ep1 = WUL_STREAMS[8 + c]
        e1_ = p.dma("sync", AP(arena, WU_O + c * 3072,
                               [[ARW, 128], [384, 8], [1, 384]]),
                    AP(wu_bf, c * 128 * 384,
                       [[384, 128], [D * 384, 8], [1, 384]]),
                    waits=wul_gate + [ep0], stream=st0)
        e_wu_only.append(e1_)
        e_wT_l.append(e1_)
        e2_ = p.dma("sync", AP(arena, WG_O + c * 3072,
                               [[ARW, 128], [384, 8], [1, 384]]),
                    AP(wg_bf, c * 128 * 384,
                       [[384, 128], [D * 384, 8], [1, 384]]),
                    waits=wul_gate + [ep1], stream=st1)
        e_wg_only.append(e2_)
        e_wT_l.append(e2_)
    wul_last = e_wu_only[0:2] + e_wg_only[0:2] + [None, None]
    xqf_ev = {}
    e_xqf_g = [[] for _ in range(4)]
    for g in range(4):
        for c in range(DC):
            st_ = f"XQF{c % 4}"
            e_ = p.dma(
                "sync", ar_(XQ_O + c * 2048 + g * 512, 512),
                xqbf_d.ap()[g * 512:(g + 1) * 512, c * 128:(c + 1) * 128],
                waits=xqf_gate + [e_xq_w[4 * g + 3], e_xq_w[4 * g + 2],
                                 xqf_ev.get(st_, None)],
                transpose=True, stream=st_)
            xqf_ev[st_] = e_
            e_xqf_g[g].append(e_)
    e_wul = [e for e in wul_last[0:4]]
    e_wT = None
    e_xqfm = None

    if "xqfm" in debug:
        dbg_xqfm = nc.dram_tensor("dbg_xqfm", [128, 8 * 2048], BF16,
                                  kind="ExternalOutput")
        dbg_wuT = nc.dram_tensor("dbg_wuT", [128, 8 * 3072], BF16,
                                 kind="ExternalOutput")
        p.dma("sync", dbg_xqfm.ap(), ar_(XQ_O, 8 * 2048),
              waits=e_wul + [e for gl in e_xqf_g for e in gl], stream="DBG")
        p.dma("sync", dbg_wuT.ap(), ar_(WU_O, 8 * 3072), waits=e_wul,
              stream="DBG")

    PS8 = [ps[:, s * 512:(s + 1) * 512] for s in range(8)]
    ps8_ev = [NOEV] * 8
    sp_ev = [NOEV] * 4
    si = [0, 0]
    mux_last = {}
    spill_last = {}
    for wi, (woff, spd, muxc) in enumerate([(WU_O, up_sp, 1),
                                            (WG_O, gate_sp, 2)]):
        e_m = None
        e_sp = None
        for it in range(ITI):
            for tb in range(QBN):
                s8 = si[0] % 8
                si[0] += 1
                e_mm = None
                for c in range(DC):
                    wgt = e_wu_only if woff == WU_O else e_wg_only
                    w = ([ps8_ev[s8]] + e_xqf_g[tb] + wgt) if c == 0 else ()
                    e_mm = p.emit("tensor", lambda eng, s8=s8, c=c, it=it,
                                  tb=tb, woff=woff: eng.matmul(
                                      PS8[s8],
                                      AP(arena, woff + c * 3072 + it * 128,
                                         [[ARW, 128], [1, 128]]),
                                      AP(arena, XQ_O + c * 2048 + tb * 512,
                                         [[ARW, 128], [1, 512]]),
                                      start=(c == 0), stop=(c == DC - 1),
                                      skip_group_check=True),
                                  waits=w, inc="T")
                ss = si[1] % 4
                si[1] += 1
                stg = spst[:, ss * 512:(ss + 1) * 512]
                e_ev = p.emit("vector", lambda eng, stg=stg, s8=s8:
                              eng.tensor_copy(stg, PS8[s8]),
                              waits=[e_mm, sp_ev[ss]], inc="V")
                if it == 0 and tb == 0:
                    e_m = p.emit("vector", lambda eng, s8=s8, muxc=muxc:
                                 eng.tensor_reduce(
                                     mux[:, muxc:muxc + 1], PS8[s8], AX.X,
                                     AL.max, apply_absolute_value=True),
                                 waits=[e_mm], inc="V")
                else:
                    e_m = p.emit("vector", lambda eng, s8=s8: eng.tensor_reduce(
                        mux[:, 3:4], PS8[s8], AX.X, AL.max,
                        apply_absolute_value=True), waits=[e_mm, e_m], inc="V")
                    e_m = p.emit("vector", lambda eng, muxc=muxc:
                                 eng.tensor_tensor(mux[:, muxc:muxc + 1],
                                                   mux[:, muxc:muxc + 1],
                                                   mux[:, 3:4], AL.max),
                                 waits=[e_m], inc="V")
                ps8_ev[s8] = ("V", max(e_ev[1], e_m[1]))
                e_sp = p.dma("scalar",
                             spd.ap()[it * 128:(it + 1) * 128,
                                      tb * 512:(tb + 1) * 512],
                             stg, waits=[e_ev], stream=f"SP{ss}")
                sp_ev[ss] = e_sp
        mux_last[wi] = e_m
        spill_last[wi] = [e for e in sp_ev]

    e_cp = p.emit("vector", lambda eng: eng.tensor_copy(
        mux[:, 4:5], mux[:, 1:2]), waits=[mux_last[0], mux_last[1]], inc="V")
    e_cp = p.emit("vector", lambda eng: eng.tensor_copy(
        mux[:, 5:6], mux[:, 2:3]), waits=[e_cp], inc="V")
    e_b2 = barrier_max(e_cp, mux[:, 4:6], 2, extra=[e_s1])
    e_su = p.emit("vector", lambda eng: eng.tensor_scalar(
        brx[:, 2:4], brx[:, 6:8], 1.0 / QA_, 1e-12, AL.mult, AL.max),
        waits=[e_b2], inc="V")
    e_rsu = p.emit("vector", lambda eng: eng.reciprocal(
        brx[:, 4:6], brx[:, 2:4]), waits=[e_su], inc="V")
    e_mbg = p.emit("vector", lambda eng: eng.tensor_scalar(
        lnst[:, 6:7], brx[:, 3:4], -MAGIC, None, AL.mult),
        waits=[e_su], inc="V")

    e_dbg_up = e_dbg_gate = None
    if "up_raw" in debug:
        t_ = nc.dram_tensor("dbg_up_raw", [I, T], F32, kind="ExternalOutput")
        e_dbg_up = p.dma("sync", t_.ap(), up_sp.ap(),
                         waits=spill_last[0], stream="DBG")
    if "gate_raw" in debug:
        t_ = nc.dram_tensor("dbg_gate_raw", [I, T], F32, kind="ExternalOutput")
        e_dbg_gate = p.dma("sync", t_.ap(), gate_sp.ap(),
                           waits=spill_last[1], stream="DBG")
    e_g2_w = [None] * ITI
    e_sig = [None] * ITI
    e_gq = [None] * ITI
    gq_read = [None] * ITI
    sig_read = [None] * ITI
    st_ = {}

    s1_state = {}

    def stage1_load(it):
        inb = fw(it % 2)
        w = [s1_state.get(("t1e", it - 2), None)]
        el = p.dma("gpsimd", inb, gate_sp.ap()[it * 128:(it + 1) * 128, :],
                   waits=spill_last[1] + [e_dbg_gate] + w,
                   stream=f"S1L{it % 2}")
        t1 = fw(2 + it % 2)
        e1 = p.emit("vector", lambda eng, inb=inb, t1=t1: eng.tensor_scalar(
            t1, inb, brx[:, 5:6], MAGIC, AL.mult, AL.add),
            waits=[el, e_rsu, e_mbg, s1_state.get(("gq", it - 2), None),
                   s1_state.get(("sg", it - 2), None)], inc="V")
        s1_state[("t1e", it)] = e1

    def stage1_sig(it):
        t1 = fw(2 + it % 2)
        sg = fw(6 + it % 2)
        e = p.emit("scalar", lambda eng, t1=t1, sg=sg: eng.activation(
            sg, t1, AF.Sigmoid, bias=lnst[:, 6:7], scale=brx[:, 3:4]),
            waits=[s1_state[("t1e", it)],
                   s1_state.get(("sgq", it - 2), None)], inc="SC")
        e_sig[it] = e

    def stage1_gq(it):
        t1 = fw(2 + it % 2)
        gq = fw(4 + it % 2)
        e = p.emit("vector", lambda eng, t1=t1, gq=gq: eng.tensor_scalar(
            gq, t1, MAGIC, brx[:, 3:4], AL.subtract, AL.mult),
            waits=[s1_state[("t1e", it)],
                   s1_state.get(("g2", it - 2), None)], inc="V")
        s1_state[("gq", it)] = e
        e_gq[it] = e

    def stage1_tail(it):
        gq = fw(4 + it % 2)
        sg = fw(6 + it % 2)
        ob = fw(8 + it % 2)
        e3 = p.emit("vector", lambda eng, sg=sg: eng.tensor_scalar(
            sg, sg, 65536.0, MAGIC, AL.mult, AL.add),
            waits=[e_sig[it]], inc="V")
        e4 = p.emit("vector", lambda eng, sg=sg: eng.tensor_scalar(
            sg, sg, MAGIC, 65535.0, AL.subtract, AL.min), waits=[e3], inc="V")
        wo = [e_g2_w[it - 2]] if it >= 2 else []
        e5 = p.emit("vector", lambda eng, ob=ob, gq=gq, sg=sg:
                    eng.scalar_tensor_tensor(ob, gq, 1.0 / 65536.0, sg,
                                             AL.mult, AL.mult),
                    waits=[e4] + wo, inc="V")
        s1_state[("g2", it)] = e5
        s1_state[("sg", it)] = e5
        s1_state[("sgq", it)] = e5
        if it == 0:
            e_m = p.emit("vector", lambda eng, ob=ob: eng.tensor_reduce(
                mux[:, 6:7], ob, AX.X, AL.max, apply_absolute_value=True),
                waits=[e5], inc="V")
        else:
            e_m = p.emit("vector", lambda eng, ob=ob: eng.tensor_reduce(
                mux[:, 7:8], ob, AX.X, AL.max, apply_absolute_value=True),
                waits=[e5, st_["m1"]], inc="V")
            e_m = p.emit("vector", lambda eng: eng.tensor_tensor(
                mux[:, 6:7], mux[:, 6:7], mux[:, 7:8], AL.max),
                waits=[e_m], inc="V")
        st_["m1"] = e_m
        e_g2_w[it] = p.dma("gpsimd", gate_sp.ap()[it * 128:(it + 1) * 128, :],
                           ob, waits=[e5, e_m], stream=f"G2W{it % 2}")

    def barrier3():
        e_b3 = barrier_max(st_["m1"], mux[:, 6:7], 1, extra=[e_su])
        e_s3 = p.emit("vector", lambda eng: eng.tensor_scalar(
            stt[:, 3:4], brx[:, 6:7], 1.0 / QA_, 1e-12, AL.mult, AL.max),
            waits=[e_b3], inc="V")
        st_["s2ev"] = e_s3
        st_["rs2ev"] = p.emit("vector", lambda eng: eng.reciprocal(
            stt[:, 5:6], stt[:, 3:4]), waits=[e_s3], inc="V")

    e_o_w = [None] * ITI
    e_dbg_g2 = None

    def stage2_load(it):
        g2b = fw(it % 2)
        upb = fw(2 + it % 2)
        w1 = [e_o_w[it - 2]] if it >= 2 else []
        el1 = p.dma("gpsimd", g2b, gate_sp.ap()[it * 128:(it + 1) * 128, :],
                    waits=[e_g2_w[ITI - 1], e_g2_w[ITI - 2], st_["rs2ev"],
                           st_.get("dbg_g2", None)] + w1,
                    stream=f"S2A{it % 2}")
        el2 = p.dma("gpsimd", upb, up_sp.ap()[it * 128:(it + 1) * 128, :],
                    waits=spill_last[0] + [e_g2_w[ITI - 1], e_g2_w[ITI - 2],
                                           e_dbg_up] + w1,
                    stream=f"S2B{it % 2}")
        s1_state[("s2", it)] = (el1, el2, None)

    def stage2_qdq(it, k):
        g2b = fw(it % 2)
        upb = fw(2 + it % 2)
        g2q = fw(4 + it % 2)
        upq = fw(6 + it % 2)
        el1, el2, prev = s1_state[("s2", it)]
        src, dst, rss, ss_ = ((g2b, g2q, stt[:, 5:6], stt[:, 3:4]) if k < 2
                              else (upb, upq, brx[:, 4:5], brx[:, 2:3]))
        half = k % 2
        e1 = p.emit("vector", lambda eng, src=src, rss=rss, half=half:
                    eng.tensor_scalar(
                        sc1[:, :], src[:, half * 1024:(half + 1) * 1024],
                        rss, MAGIC, AL.mult, AL.add),
                    waits=[el1, el2, e_rsu, prev], inc="V")
        e2 = p.emit("vector", lambda eng, dst=dst, ss_=ss_, half=half:
                    eng.tensor_scalar(
                        dst[:, half * 1024:(half + 1) * 1024], sc1[:, :],
                        MAGIC, ss_, AL.subtract, AL.mult),
                    waits=[e1], inc="V")
        s1_state[("s2", it)] = (el1, el2, e2)

    def stage2_mul(it):
        g2q = fw(4 + it % 2)
        upq = fw(6 + it % 2)
        ob = fw(8 + it % 2)
        _, _, prev = s1_state[("s2", it)]
        wo = [e_o_w[it - 2]] if it >= 2 else []
        e5 = p.emit("vector", lambda eng, ob=ob, g2q=g2q, upq=upq:
                    eng.tensor_tensor(ob, g2q, upq, AL.mult),
                    waits=[prev] + wo, inc="V")
        s1_state[("s2m", it)] = e5

    def stage2_max(it):
        ob = fw(8 + it % 2)
        e5 = s1_state[("s2m", it)]
        if it == 0:
            e_m = p.emit("vector", lambda eng, ob=ob: eng.tensor_reduce(
                mux[:, 6:7], ob, AX.X, AL.max, apply_absolute_value=True),
                waits=[e5, st_["b3r"]], inc="V")
        else:
            e_m = p.emit("vector", lambda eng, ob=ob: eng.tensor_reduce(
                mux[:, 7:8], ob, AX.X, AL.max, apply_absolute_value=True),
                waits=[e5, st_["m2"]], inc="V")
            e_m = p.emit("vector", lambda eng: eng.tensor_tensor(
                mux[:, 6:7], mux[:, 6:7], mux[:, 7:8], AL.max),
                waits=[e_m], inc="V")
        st_["m2"] = e_m
        e_o_w[it] = p.dma("gpsimd", up_sp.ap()[it * 128:(it + 1) * 128, :],
                          ob, waits=[e5, e_m], stream=f"OW2{it % 2}")

    def barrier4():
        e_b4 = barrier_max(st_["m2"], mux[:, 6:7], 1, extra=[st_["s2ev"]])
        e_s4 = p.emit("vector", lambda eng: eng.tensor_scalar(
            stt[:, 6:7], brx[:, 6:7], 1.0 / QA_, 1e-12, AL.mult, AL.max),
            waits=[e_b4], inc="V")
        st_["s4ev"] = e_s4
        st_["rs4ev"] = p.emit("vector", lambda eng: eng.reciprocal(
            stt[:, 7:8], stt[:, 6:7]), waits=[e_s4], inc="V")

    e_oq_w = [None] * ITI

    def stage3_load(it):
        ob = fw(it % 2)
        w = [e_oq_w[it - 2]] if it >= 2 else []
        el = p.dma("gpsimd", ob, up_sp.ap()[it * 128:(it + 1) * 128, :],
                   waits=[e_o_w[ITI - 1], e_o_w[ITI - 2], st_["rs4ev"]] + w,
                   stream=f"S3L{it % 2}")
        s1_state[("s3", it)] = (el, None, [])

    def stage3_half(it, half):
        ob = fw(it % 2)
        el, prev, evs = s1_state[("s3", it)]
        e1 = p.emit("vector", lambda eng, ob=ob, half=half:
                    eng.tensor_scalar(
                        sc1[:, :], ob[:, half * 1024:(half + 1) * 1024],
                        stt[:, 7:8], MAGIC, AL.mult, AL.add),
                    waits=[el, prev], inc="V")
        prev = p.emit("vector", lambda eng, it=it, half=half:
                      eng.tensor_scalar(
                          oqbf[:, (it % 2) * 2048 + half * 1024:
                               (it % 2) * 2048 + (half + 1) * 1024],
                          sc1[:, :], MAGIC, stt[:, 6:7], AL.subtract,
                          AL.mult), waits=[e1], inc="V")
        evs = evs + [prev]
        s1_state[("s3", it)] = (el, prev, evs)
        if half == 1:
            e_oq_w[it] = p.dma("gpsimd",
                               oq_d.ap()[it * 128:(it + 1) * 128, :],
                               oqbf[:, (it % 2) * 2048:(it % 2 + 1) * 2048],
                               waits=evs, stream=f"OQW{it % 2}")

    STB = [ps[:, i * 512:(i + 1) * 512] for i in range(6)]
    PVB = [ps[0:65, 3072:3584], ps[0:65, 3584:4096]]
    kp_ev = [NOEV, NOEV]
    e_kp, e_vp, e_qp = [NOEV, NOEV], [NOEV, NOEV], [NOEV, NOEV]
    e_p3T = p.now("T")

    def prefetch_pair(hp):
        b = hp % 2
        e_kp[b] = p.dma("sync", ar_(KP_O + b * 4096, 4096),
                        kag_o.ap()[:, hp * 128:(hp + 1) * 128],
                        waits=[e_ag_k, kp_ev[b], e_p3T], transpose=True,
                        stream=f"KP{b}")
        vdst = AP(arena, VP_O + b * 4160, [[ARW, 128], [130, KCN], [1, 130]])
        vsrc = AP(vag_o, hp * 130,
                  [[H * 65, 128], [128 * H * 65, KCN], [1, 130]])
        e_vp[b] = p.dma("sync", vdst, vsrc,
                        waits=[e_ag_v, kp_ev[b], e_p3T], stream=f"VP{b}")
        e_qp[b] = p.dma("sync", ar_(QP_O + b * 2048, 2048),
                        q_tm_d.ap()[:, hp * 128:(hp + 1) * 128],
                        waits=[e_q_w[NT - 1], e_q_w[NT - 2], kp_ev[b], e_p3T],
                        transpose=True, stream=f"QP{b}")

    st_["b3r"] = NOEV
    ffn_sched = {}

    def at_(r, g, fn):
        ffn_sched.setdefault((r, g), []).append(fn)

    NG = len(KC_GROUPS)
    for it_ in range(ITI):
        r, o = it_ // 2, it_ % 2
        at_(r, 0 + o, lambda it=it_: stage1_load(it))
        at_(r, 3 + o, lambda it=it_: (stage1_sig(it), stage1_gq(it)))
        at_(r, 7 + o, lambda it=it_: stage1_tail(it))

    def _b3():
        if "g2" in debug:
            t_ = nc.dram_tensor("dbg_g2", [I, T], F32, kind="ExternalOutput")
            st_["dbg_g2"] = p.dma("sync", t_.ap(), gate_sp.ap(),
                                  waits=[e_g2_w[ITI - 1], e_g2_w[ITI - 2]],
                                  stream="DBG")
        barrier3()
        st_["b3r"] = st_["rs2ev"]
    at_(12, 2, _b3)
    for it_ in range(ITI):
        r, o = 13 + it_ // 2, it_ % 2
        at_(r, 0 + o, lambda it=it_: (stage2_load(it), stage2_qdq(it, 0),
                                      stage2_qdq(it, 1)))
        at_(r, 5 + o, lambda it=it_: (stage2_qdq(it, 2), stage2_qdq(it, 3),
                                      stage2_mul(it), stage2_max(it)))
    at_(25, 2, barrier4)
    for it_ in range(ITI):
        r, o = 26 + it_ // 5, it_ % 5
        at_(r, 2 * o, lambda it=it_: (stage3_load(it), stage3_half(it, 0),
                                      stage3_half(it, 1)))
    sched_pos = [0]

    def pump_slot(r, g):
        tgt = r * NG + g
        while sched_pos[0] <= tgt:
            rr, gg = sched_pos[0] // NG, sched_pos[0] % NG
            for fn in ffn_sched.get((rr, gg), []):
                fn()
            sched_pos[0] += 1

    def pump_all():
        mx = max(r * NG + g for (r, g) in ffn_sched) if ffn_sched else 0
        while sched_pos[0] <= mx:
            rr, gg = sched_pos[0] // NG, sched_pos[0] % NG
            for fn in ffn_sched.get((rr, gg), []):
                fn()
            sched_pos[0] += 1

    prefetch_pair(0)
    e_p3V = p.now("V")
    pv_free = [e_p3V, e_p3V]
    e_attn_w = []
    st_free = [e_p3V] * 6
    pt_last = [NOEV] * 4
    e_pv_g = [NOEV, NOEV]

    for hp in range(NPAIR):
        b = hp % 2
        if hp + 1 < NPAIR:
            prefetch_pair(hp + 1)
        kpe, vpe, qpe = e_kp[b], e_vp[b], e_qp[b]
        for qb in range(QBN):
            pend_pv = []

            def emit_pv(hh, gi, kc0, glen, e_e):
                e_pv = None
                for j in range(glen):
                    kc = kc0 + j
                    vlhs = AP(arena, VP_O + b * 4160 + kc * 130 + hh * 65,
                              [[ARW, 128], [1, 65]])
                    pt_off = (hh * 2 + gi % 2) * 1536
                    prhs = ptb[:, pt_off + j * 512:pt_off + (j + 1) * 512]
                    w = [e_e, vpe]
                    if gi == 0 and j == 0:
                        w.append(pv_free[hh])
                    e_pv = p.emit("tensor", lambda eng, hh=hh, vlhs=vlhs,
                                  prhs=prhs, st=(gi == 0 and j == 0),
                                  sp_=(kc == KCN - 1): eng.matmul(
                                      PVB[hh], vlhs, prhs, start=st,
                                      stop=sp_, skip_group_check=True),
                                  waits=w, inc="T")
                e_pv_g[hh] = e_pv
                pt_last[hh * 2 + gi % 2] = ("T", e_pv[1])

            for gi, (kc0, glen) in enumerate(KC_GROUPS):
                e_st = [None, None]
                for hh in range(2):
                    for j in range(glen):
                        kc = kc0 + j
                        bank = hh * 3 + j
                        klhs = AP(arena, hh * 64 * ARW + KP_O + b * 4096
                                  + kc * 128, [[ARW, 64], [1, 128]])
                        qrhs = AP(arena, hh * 64 * ARW + QP_O + b * 2048
                                  + qb * 512, [[ARW, 64], [1, 512]])
                        e_st[hh] = p.emit(
                            "tensor", lambda eng, bank=bank, klhs=klhs,
                            qrhs=qrhs, hh=hh: eng.matmul(
                                STB[bank], klhs, qrhs, start=True, stop=True,
                                skip_group_check=True,
                                tile_position=(0, 0) if hh == 0 else (64, 0)),
                            waits=[kpe, qpe, st_free[bank]], inc="T")
                exps_now = []
                for hh in range(2):
                    pslot = hh * 2 + gi % 2
                    pt_off = pslot * 1536
                    pdst = ptb[:, pt_off:pt_off + glen * 512]
                    psrc = ps[:, hh * 3 * 512: hh * 3 * 512 + glen * 512]
                    e_e = p.emit("scalar", lambda eng, pdst=pdst, psrc=psrc:
                                 eng.activation(pdst, psrc, AF.Exp,
                                                scale=ISCALE),
                                 waits=[e_st[hh], pt_last[pslot]], inc="SC")
                    for j in range(glen):
                        st_free[hh * 3 + j] = e_e
                    exps_now.append((hh, gi, kc0, glen, e_e))
                for args in pend_pv:
                    emit_pv(*args)
                pend_pv = exps_now
                pump_slot(hp * QBN + qb, gi)
            for args in pend_pv:
                emit_pv(*args)
            for hh in range(2):
                auo = (2 * (qb % 2) + hh) * 512
                au = aub[:, auo:auo + 512]
                e_ev = p.emit("vector", lambda eng, au=au, hh=hh:
                              eng.tensor_copy(au[0:65, :], PVB[hh]),
                              waits=[e_pv_g[hh]] +
                              ([e_attn_w[-4]] if len(e_attn_w) >= 4 else []),
                              inc="V")
                pv_free[hh] = e_ev
                e_l1 = p.emit("scalar", lambda eng, auo=auo: eng.activation(
                    lnr[64:65, 0:512], aub[64:65, auo:auo + 512], AF.Ln),
                    waits=[e_ev, st_.get("lnfree", NOEV)], inc="SC")
                e_l2 = p.emit("scalar", lambda eng: eng.activation(
                    lnr[64:65, 0:512], lnr[64:65, 0:512], AF.Exp, scale=-1.0),
                    waits=[e_l1], inc="SC")
                e_bc = p.emit("tensor", lambda eng: eng.matmul(
                    STB[5][0:64, :], ones1f[64:65, 0:64],
                    lnr[64:65, 0:512], start=True, stop=True,
                    skip_group_check=True),
                    waits=[e_l2, st_free[5], e_onesf], inc="T")
                an = anb[:, hh * 512:(hh + 1) * 512]
                e_an = p.emit("vector", lambda eng, an=an, au=au:
                              eng.tensor_tensor(an[0:64, :], au[0:64, :],
                                                STB[5][0:64, :], AL.mult),
                              waits=[e_bc] + ([e_attn_w[-2]]
                                              if len(e_attn_w) >= 2 else []),
                              inc="V")
                st_free[5] = e_an
                st_["lnfree"] = ("T", e_bc[1])
                e_attn_w.append(
                    p.dma("sync",
                          attn_d.ap()[(hp * 2 + hh) * 64:
                                      (hp * 2 + hh + 1) * 64,
                                      qb * 512:(qb + 1) * 512],
                          an[0:64, :], waits=[e_an], stream=f"ATW{hh}"))
        eT = p.now("T")
        kp_ev[b] = eT


    pump_all()

    e_p4T = p.now("T")
    e_wo = None
    for c in range(DC):
        e_wo = p.dma("gpsimd", ar_(WO_O + c * 1024, 1024),
                     woT_d.ap()[c * 128:(c + 1) * 128, :],
                     waits=[e_p4T, e_oq_w[ITI - 1], e_oq_w[ITI - 2], e_wo],
                     stream="WOC")
    wd_last = [None, None]
    for c in range(ITI):
        wd_last[c % 2] = p.dma(
            "sync", AP(arena, WD_O + c * 1024,
                       [[ARW, 128], [128, 8], [1, 128]]),
            AP(wd_bf, c * 128 * 128,
               [[128, 128], [I * 128, 8], [1, 128]]),
            waits=[e_ag_wd, e_p4T, e_oq_w[ITI - 1], e_oq_w[ITI - 2],
                   wd_last[c % 2]],
            stream=f"WDL{c % 2}")
    e_wd = wd_last

    PF = [ps[:, i * 512:(i + 1) * 512] for i in range(8)]
    ch_ev = [NOEV, NOEV, NOEV, NOEV]
    e_out_w = []
    prev_adds = NOEV
    prev_outs = NOEV
    for tg in range(4):
        e_mm_last = [None] * 8
        for ci in range(DC + ITI):
            cb = ci % 4
            if ci < DC:
                src = attn_d.ap()[ci * 128:(ci + 1) * 128,
                                  tg * 512:(tg + 1) * 512]
                wsrc = [e_attn_w[-1], e_attn_w[-2], ch_ev[cb]]
                roff = WO_O + ci * 1024
            else:
                src = oq_d.ap()[(ci - DC) * 128:(ci - DC + 1) * 128,
                                tg * 512:(tg + 1) * 512]
                wsrc = [e_oq_w[ITI - 1], e_oq_w[ITI - 2], ch_ev[cb]]
                roff = WD_O + (ci - DC) * 1024
            el = p.dma("sync", chld[:, cb * 512:(cb + 1) * 512], src,
                       waits=wsrc, stream=f"CHL{cb}")
            for tti in range(4):
                for jb in range(2):
                    s8 = tti * 2 + jb
                    w = ([el, e_wo, prev_adds] + e_wd) if ci == 0 else [el]
                    e_mm_last[s8] = p.emit(
                        "tensor", lambda eng, s8=s8, cb=cb, tti=tti, jb=jb,
                        roff=roff, ci=ci: eng.matmul(
                            PF[s8],
                            chld[:, cb * 512 + tti * 128:
                                 cb * 512 + (tti + 1) * 128],
                            ar_(roff + jb * 512, 512),
                            start=(ci == 0), stop=(ci == DC + ITI - 1),
                            skip_group_check=True),
                        waits=w, inc="T")
            ch_ev[cb] = e_mm_last[7]
        adds = []
        for tti in range(4):
            tt = tg * 4 + tti
            xb2 = fin_x[:, tti * 1024:(tti + 1) * 1024]
            el = p.dma("sync", xb2, x_d.ap()[tt * 128:(tt + 1) * 128, :],
                       waits=[prev_adds, e_p4T, e_oq_w[ITI - 1],
                              e_oq_w[ITI - 2]], stream=f"FX{tti}")
            ot = fin_o[:, tti * 1024:(tti + 1) * 1024]
            tev = []
            for jb in range(2):
                s8 = tti * 2 + jb
                prev_out_slot = (e_out_w[-4] if len(e_out_w) >= 4 else None)
                e_ad = p.emit("vector", lambda eng, ot=ot, xb2=xb2, s8=s8,
                              jb=jb: eng.tensor_tensor(
                                  ot[:, jb * 512:(jb + 1) * 512],
                                  xb2[:, jb * 512:(jb + 1) * 512],
                                  PF[s8], AL.add),
                              waits=[el, e_mm_last[s8], prev_out_slot],
                              inc="V")
                tev.append(e_ad)
                adds.append(e_ad)
            e_out_w.append(p.dma("sync",
                                 out_d.ap()[tt * 128:(tt + 1) * 128, :], ot,
                                 waits=tev, stream=f"OUT{tti}"))
        prev_adds = adds[-1]
        prev_outs = None

    coll_dbg = {"kag_o": (kag_o, e_ag_k), "vag_o": (vag_o, e_ag_v),
                "wu_bf": (wu_bf, e_ag_wu), "wg_bf": (wg_bf, e_ag_wg),
                "wd_bf": (wd_bf, e_ag_wd),
                "scales": None}
    for nm in debug:
        if nm in coll_dbg and nm == "scales":
            dbg_t = nc.dram_tensor("dbg_scales", [128, 16], F32,
                                   kind="ExternalOutput")
            p.dma("sync", dbg_t.ap()[:, 0:8], brx[:, :],
                  waits=[e_out_w[-1]], stream="DBG")
            p.dma("sync", dbg_t.ap()[:, 8:16], stt[:, :],
                  waits=[e_out_w[-1]], stream="DBG")
        elif nm in coll_dbg:
            tsr, ev = coll_dbg[nm]
            dbg_t = nc.dram_tensor("dbg_" + nm, list(tsr.shape), tsr.dtype,
                                   kind="ExternalOutput")
            p.dma("sync", dbg_t.ap(), tsr.ap(), waits=[ev], stream="DBG")

    p.emit("gpsimd", lambda eng: eng.engine_nop(),
           waits=[p.now(k) for k in sorted(p.cnt)], inc="V")

    p.check()

    with nc.Block() as block:
        @block.tensor
        def _(eng):
            for f in p.ops["tensor"]:
                f(eng)

        @block.vector
        def _(eng):
            for f in p.ops["vector"]:
                f(eng)

        @block.scalar
        def _(eng):
            for f in p.ops["scalar"]:
                f(eng)

        @block.sync
        def _(eng):
            for f in p.ops["sync"]:
                f(eng)

        @block.gpsimd
        def _(eng):
            for f in p.ops["gpsimd"]:
                f(eng)

    stack.close()
    return nc


def _perm():
    pm = []
    for h in range(H):
        pm += [h * HD + 2 * j for j in range(HD // 2)]
        pm += [h * HD + 2 * j + 1 for j in range(HD // 2)]
    return np.array(pm)


def _tables(half):
    i = np.arange(32, dtype=np.float64)
    freqs = 10000.0 ** (-i / 32.0)
    p_ = np.arange(128, dtype=np.float64)
    tt = np.arange(NT, dtype=np.float64)
    pos = half * T + tt[None, :, None] * 128 + p_[:, None, None]
    ang = pos * freqs[None, None, :]
    return (np.cos(ang).reshape(128, NT * 32).astype(np.float32),
            np.sin(ang).reshape(128, NT * 32).astype(np.float32))


def kernel(L0_attn_x, L0_attn_Wq, L0_attn_bq, L0_attn_Wk, L0_attn_bk,
           L0_attn_Wv, L0_attn_bv, L0_attn_Wo, L0_ffn_w_gate, L0_ffn_w_up,
           L0_ffn_w_down):
    x = np.ascontiguousarray(np.asarray(L0_attn_x, dtype=np.float32))
    pm = _perm()
    wqT = np.ascontiguousarray(np.asarray(L0_attn_Wq, dtype=np.float32).T[:, pm])
    wkT = np.ascontiguousarray(np.asarray(L0_attn_Wk, dtype=np.float32).T[:, pm])
    wvT = np.ascontiguousarray(np.asarray(L0_attn_Wv, dtype=np.float32).T)
    woT = np.ascontiguousarray(np.asarray(L0_attn_Wo, dtype=np.float32).T)
    bq = np.asarray(L0_attn_bq, dtype=np.float32)[pm][None, :]
    bk = np.asarray(L0_attn_bk, dtype=np.float32)[pm][None, :]
    bv = np.asarray(L0_attn_bv, dtype=np.float32)[None, :]
    wu = np.ascontiguousarray(np.asarray(L0_ffn_w_up, dtype=np.float32))
    wg = np.ascontiguousarray(np.asarray(L0_ffn_w_gate, dtype=np.float32))
    wd = np.ascontiguousarray(np.asarray(L0_ffn_w_down, dtype=np.float32))

    nc = build_program(tuple(DEBUG))

    tables = [_tables(0), _tables(1)]
    in_maps = []
    for c in range(8):
        b, half = c // 2, c % 2
        cos_t, sin_t = tables[half]
        in_maps.append({
            "x": np.ascontiguousarray(x[b, half * T:(half + 1) * T, :]),
            "wqT": wqT, "wkT": wkT, "wvT": wvT, "woT": woT,
            "bq": bq, "bk": bk, "bv": bv,
            "wu_sh": np.ascontiguousarray(wu[c * (I // 8):(c + 1) * (I // 8)]),
            "wg_sh": np.ascontiguousarray(wg[c * (I // 8):(c + 1) * (I // 8)]),
            "wd_sh": np.ascontiguousarray(wd[c * (D // 8):(c + 1) * (D // 8)]),
            "cos_t": cos_t, "sin_t": sin_t,
            "ident": np.eye(128, dtype=np.float32),
        })
    res = run_bass_kernel_spmd(nc, in_maps, core_ids=list(range(8)))
    out = np.empty((B, S, D), dtype=np.float32)
    for c in range(8):
        b, half = c // 2, c % 2
        out[b, half * T:(half + 1) * T, :] = res.results[c]["out"]
    if DEBUG:
        kernel.debug_results = res.results
    return out
